# version 01
# baseline (speedup 1.0000x reference)
import numpy as np
import ml_dtypes

import concourse.bass as bass
import concourse.mybir as mybir
from concourse.bass_utils import run_bass_kernel_spmd

F32 = mybir.dt.float32
BF16 = mybir.dt.bfloat16
AF = mybir.ActivationFunctionType
ALU = mybir.AluOpType
AX = mybir.AxisListType.X
BF = ml_dtypes.bfloat16

D = 2048
NH = 16
HD = 128
SEQ = 8192
WINDOW = 4096
CORE_TOK = 2048
NG = 4
GT = 512
KSPAN = 36

RUNS = [[0, 7, 8, 9], [1, 6, 10, 11], [2, 5, 12, 13], [3, 4, 14, 15]]
SCALE = float(1.0 / np.sqrt(HD))


def core_blocks(r):
    return [4 * k + j for k in RUNS[r] for j in range(4)]


def core_positions(r):
    return np.concatenate([np.arange(b * 128, b * 128 + 128) for b in core_blocks(r)])


_INVF = 10000.0 ** (-(np.arange(64, dtype=np.float64) * 2.0) / 128.0)


def rope_tables(positions):
    t = np.asarray(positions, dtype=np.float64)
    ang = _INVF[:, None] * t[None, :]
    c, s = np.cos(ang), np.sin(ang)
    cos0 = np.concatenate([c, c], 0).astype(np.float32)
    sin0 = np.concatenate([s, s], 0).astype(np.float32)
    cos1 = np.repeat(c, 2, axis=0).astype(np.float32)
    sin1 = np.repeat(s, 2, axis=0).astype(np.float32)
    return cos0, sin0, cos1, sin1


def rot_mat_l0():
    P = np.zeros((128, 128), dtype=BF)
    for i in range(64):
        P[i + 64, i] = -1.0
        P[i, i + 64] = 1.0
    return P


def rot_mat_l1():
    P = np.zeros((128, 128), dtype=BF)
    for j in range(64):
        P[2 * j + 1, 2 * j] = -1.0
        P[2 * j, 2 * j + 1] = 1.0
    return P


def to_fm(x_tok):
    T = x_tok.shape[0]
    return np.ascontiguousarray(x_tok.reshape(T, D // 128, 128).transpose(1, 2, 0))


def from_fm(x_fm):
    return np.ascontiguousarray(x_fm.transpose(2, 0, 1).reshape(x_fm.shape[2], D))


FVALS = [1e-5, 1e-6, 1.0 / 2048.0, 1.0 / 128.0, -1.0, 1.0, 0.0625,
         -1e30, 1e30, 0.5, 2.0, -0.5, -0.01, 1.0 / 8192.0]


def _consts(nc):
    c_P0 = nc.inline_tensor(rot_mat_l0().view(np.uint16), name="c_P0")
    c_P1 = nc.inline_tensor(rot_mat_l1().view(np.uint16), name="c_P1")
    c_id = nc.inline_tensor(np.eye(128, dtype=BF).view(np.uint16), name="c_id")
    c_o128 = nc.inline_tensor(np.ones((128, 1), dtype=BF).view(np.uint16), name="c_o128")
    c_ocol = nc.inline_tensor(np.ones((1, 128), dtype=np.float32), name="c_ocol")
    c_idf = nc.inline_tensor(np.eye(128, dtype=np.float32), name="c_idf")
    c_fp = nc.inline_tensor(np.array([FVALS], dtype=np.float32), name="c_fp")
    sb = nc.alloc_sbuf_tensor
    t = dict(
        P0=sb("P0", [128, 128], BF16),
        P1=sb("P1", [128, 128], BF16),
        id128=sb("id128", [128, 128], BF16),
        o128=sb("o128", [128, 1], BF16),
        ocol=sb("ocol", [1, 128], F32),
        idf32=sb("idf32", [128, 128], F32),
        fpool=sb("fpool", [1, len(FVALS)], F32),
    )
    for i, v in enumerate(FVALS):
        nc.const_aps.register(F32, float(v), t["fpool"][0:1, i:i + 1])

    t["pairs"] = [
        (t["P0"][:], c_P0.ap().bitcast(BF16)),
        (t["P1"][:], c_P1.ap().bitcast(BF16)),
        (t["id128"][:], c_id.ap().bitcast(BF16)),
        (t["o128"][:], c_o128.ap().bitcast(BF16)),
        (t["ocol"][:], c_ocol.ap()),
        (t["idf32"][:], c_idf.ap()),
        (t["fpool"][:], c_fp.ap()),
    ]

    def load(g, sem):
        for dst, srcap in t["pairs"]:
            g.dma_start(out=dst, in_=srcap).then_inc(sem, 16)

    return t, load, 6


def _dma_rec(nc, sem, marks):
    def dma(eng, out, in_):
        n0 = nc.n_instructions()
        eng.dma_start(out=out, in_=in_).then_inc(sem, 16)
        marks.append(marks[-1] + 16 * (nc.n_instructions() - n0))
        return marks[-1]
    return dma


def build_l0a():
    nc = bass.Bass("TRN2", target_bir_lowering=False, debug=False)
    x = nc.dram_tensor("x", [128, 16 * CORE_TOK], F32, kind="ExternalInput").ap()
    wqkv = nc.dram_tensor("wqkv", [48, 128, 16 * 128], F32, kind="ExternalInput").ap()
    qg = nc.dram_tensor("qg", [1, 128], F32, kind="ExternalInput").ap()
    kg = nc.dram_tensor("kg", [1, 128], F32, kind="ExternalInput").ap()
    cos0 = nc.dram_tensor("cos0", [128, CORE_TOK], F32, kind="ExternalInput").ap()
    sin0 = nc.dram_tensor("sin0", [128, CORE_TOK], F32, kind="ExternalInput").ap()
    q0 = nc.dram_tensor("q0", [NH, 128, CORE_TOK], BF16, kind="ExternalOutput").ap()
    k0 = nc.dram_tensor("k0", [NH, 128, CORE_TOK], BF16, kind="ExternalOutput").ap()
    v0 = nc.dram_tensor("v0", [NH, CORE_TOK, 128], BF16, kind="ExternalOutput").ap()

    C, cload, NC_ = _consts(nc)
    sb = nc.alloc_sbuf_tensor
    xb = sb("xb", [128, 16, CORE_TOK], BF16)
    sqb = sb("sqb", [128, 16, GT], BF16)
    xhat = sb("xhat", [128, 16, CORE_TOK], BF16)
    gq = sb("gq", [1, 128], F32)
    gk = sb("gk", [1, 128], F32)
    coss = sb("coss", [128, CORE_TOK], F32)
    sins = sb("sins", [128, CORE_TOK], F32)
    wt = sb("wt", [128, 2, 16, 128], BF16)
    mean_r = sb("mean_r", [1, GT], F32)
    msq_r = sb("msq_r", [1, GT], F32)
    es_r = sb("es_r", [1, GT], F32)
    var_r = sb("var_r", [1, GT], F32)
    sd_r = sb("sd_r", [1, GT], F32)
    rstd_r = sb("rstd_r", [1, GT], F32)
    mr_r = sb("mr_r", [1, GT], F32)
    abc = sb("abc", [128, GT], F32)
    bbc = sb("bbc", [128, GT], F32)
    sqs = sb("sqs", [128, GT], BF16)
    sdq_r = sb("sdq_r", [1, GT], F32)
    rsq_r = sb("rsq_r", [1, GT], F32)
    grs = sb("grs", [128, GT], F32)
    qn = sb("qn", [128, GT], BF16)
    t1 = sb("t1", [128, GT], F32)
    t2 = sb("t2", [128, GT], F32)
    qout = sb("qout", [128, 2, GT], BF16)
    vout = sb("vout", [128, 2, GT], BF16)

    ps = nc.alloc_psum_tensor
    p_mm = [ps("p_mm0", [128, GT], F32), ps("p_mm1", [128, GT], F32)]
    p_s1 = ps("p_s1", [1, GT], F32)
    p_s2 = ps("p_s2", [1, GT], F32)
    p_var = ps("p_var", [1, GT], F32)
    p_gb = ps("p_gb", [128, GT], F32)
    p_rot = ps("p_rot", [128, GT], F32)
    p_vt = ps("p_vt", [128, 4, 128], BF16)

    sem = nc.alloc_semaphore
    s_ldc = sem("s_ldc")
    s_ldx = sem("s_ldx")
    s_ldw = [sem("s_ldw0"), sem("s_ldw1")]
    s_st = [sem("s_st0"), sem("s_st1")]
    s_tl = sem("s_tl")
    s_sl = sem("s_sl")
    s_vl = sem("s_vl")
    s_tq = sem("s_tq")
    s_vch = sem("s_vch")
    s_sch = sem("s_sch")
    s_sq = sem("s_sq")
    s_vq = sem("s_vq")

    NLDC = NC_ + 4
    ctr = {"vch": 0, "sch": 0}

    def vedge(v, inst):
        ctr["vch"] += 1
        inst.then_inc(s_vch, 1)
        v.wait_ge(s_vch, ctr["vch"])

    def sedge(s, inst):
        ctr["sch"] += 1
        inst.then_inc(s_sch, 1)
        s.wait_ge(s_sch, ctr["sch"])

    def te_mm(i):
        return 4 * i + 1 if i < 128 else 512 + 2 * (i - 128) + 1

    def te_tp(i):
        return 512 + 2 * (i - 128) + 2

    def vq_qn(i):
        return 3 * i + 2 if i < 128 else 384 + 2 * (i - 128) + 1

    def vq_out(i):
        return 3 * i + 3 if i < 128 else 384 + 2 * (i - 128) + 2

    with nc.Block() as block:
        m_ldc, m_ldx = [0], [0]
        m_ldw = [[0], [0]]
        d_ldc = _dma_rec(nc, s_ldc, m_ldc)
        d_ldx = _dma_rec(nc, s_ldx, m_ldx)
        d_ldw = [_dma_rec(nc, s_ldw[0], m_ldw[0]), _dma_rec(nc, s_ldw[1], m_ldw[1])]

        def g_all(g):
            cload(g, s_ldc)
            m_ldc.append(m_ldc[-1] + NC_ * 16)
            d_ldc(g, gq[:], qg)
            d_ldc(g, gk[:], kg)
            d_ldc(g, coss[:], cos0)
            d_ldc(g, sins[:], sin0)
            d_ldx(g, xb[:, :, :].rearrange("p a n -> p (a n)"), x[:, :])
            for m in range(48):
                if m >= 2:
                    g.wait_ge(s_tq, te_mm((m - 2) * 4 + 3))
                d_ldw[m % 2](g, wt[:, m % 2, :, :].rearrange("p a n -> p (a n)"),
                             wqkv[m, :, :])
        block.gpsimd(g_all)

        def t_ln(t):
            t.wait_ge(s_ldc, m_ldc[-1])
            t.wait_ge(s_ldx, m_ldx[-1])
            for g in range(NG):
                if g >= 1:
                    t.wait_ge(s_sl, 4 * (g - 1) + 2)
                for di in range(16):
                    inst = t.matmul(p_s1[:], C["o128"][:],
                                    xb[:, di, g * GT:(g + 1) * GT],
                                    start=(di == 0), stop=(di == 15))
                inst.then_inc(s_tl, 1)
                t.wait_ge(s_sl, 4 * g + 1)
                for di in range(16):
                    inst = t.matmul(p_s2[:], C["o128"][:], sqb[:, di, :],
                                    start=(di == 0), stop=(di == 15))
                inst.then_inc(s_tl, 1)
                t.wait_ge(s_vl, 3 * g + 2)
                t.matmul(p_gb[:], C["ocol"][:], rstd_r[:], start=True, stop=True)
                t.matmul(p_rot[:], C["ocol"][:], mr_r[:], start=True,
                         stop=True).then_inc(s_tl, 1)
        block.tensor(t_ln)

        def s_ln(s):
            s.wait_ge(s_ldx, m_ldx[-1])
            for g in range(NG):
                if g >= 1:
                    s.wait_ge(s_tl, 3 * (g - 1) + 2)
                for di in range(16):
                    inst = s.activation(sqb[:, di, :], xb[:, di, g * GT:(g + 1) * GT],
                                        AF.Square)
                inst.then_inc(s_sl, 1)
                s.wait_ge(s_tl, 3 * g + 1)
                sedge(s, s.mul(mean_r[:], p_s1[0:1, :], 1.0 / D))
                s.activation(msq_r[:], mean_r[:], AF.Square).then_inc(s_sl, 1)
                s.wait_ge(s_tl, 3 * g + 2)
                s.mul(es_r[:], p_s2[0:1, :], 1.0 / D).then_inc(s_sl, 1)
                s.wait_ge(s_vl, 3 * g + 1)
                s.activation(sd_r[:], var_r[:], AF.Sqrt, bias=1e-5).then_inc(s_sl, 1)
        block.scalar(s_ln)

        def v_ln(v):
            for g in range(NG):
                v.wait_ge(s_sl, 4 * g + 3)
                v.tensor_sub(var_r[:], es_r[:], msq_r[:]).then_inc(s_vl, 1)
                v.wait_ge(s_sl, 4 * g + 4)
                vedge(v, v.reciprocal(rstd_r[:], sd_r[:]))
                v.tensor_mul(mr_r[:], mean_r[:], rstd_r[:]).then_inc(s_vl, 1)
                v.wait_ge(s_tl, 3 * g + 3)
                v.tensor_copy(abc[:], p_gb[:])
                vedge(v, v.tensor_copy(bbc[:], p_rot[:]))
                for di in range(16):
                    inst = v.tensor_mul(xhat[:, di, g * GT:(g + 1) * GT],
                                        xb[:, di, g * GT:(g + 1) * GT], abc[:])
                vedge(v, inst)
                for di in range(16):
                    inst = v.tensor_sub(xhat[:, di, g * GT:(g + 1) * GT],
                                        xhat[:, di, g * GT:(g + 1) * GT], bbc[:])
                inst.then_inc(s_vl, 1)
        block.vector(v_ln)

        def t_qk(t):
            t.wait_ge(s_vl, 3 * NG)
            for m in range(48):
                t.wait_ge(s_ldw[m % 2], m_ldw[m % 2][m // 2 + 1])
                isv = m >= 32
                for tc in range(NG):
                    i = m * 4 + tc
                    par = i % 2
                    if i >= 2:
                        t.wait_ge(s_vq, vq_qn(i - 2))
                    for di in range(16):
                        inst = t.matmul(p_mm[par][:], wt[:, m % 2, di, :],
                                        xhat[:, di, tc * GT:(tc + 1) * GT],
                                        start=(di == 0), stop=(di == 15))
                    inst.then_inc(s_tq, 1)
                    if not isv:
                        t.wait_ge(s_sq, 2 * i + 1)
                        t.matmul(p_var[:], C["o128"][:], sqs[:], start=True,
                                 stop=True).then_inc(s_tq, 1)
                        t.wait_ge(s_vq, 3 * i + 1)
                        gam = gq if m < 16 else gk
                        t.matmul(p_gb[:], gam[:], rsq_r[:], start=True,
                                 stop=True).then_inc(s_tq, 1)
                        t.wait_ge(s_vq, 3 * i + 2)
                        t.matmul(p_rot[:], C["P0"][:], qn[:], start=True,
                                 stop=True).then_inc(s_tq, 1)
                    else:
                        t.wait_ge(s_vq, vq_qn(i))
                        for blk in range(4):
                            inst = t.transpose(p_vt[:, blk, :],
                                               qn[:, blk * 128:(blk + 1) * 128],
                                               C["id128"][:])
                        inst.then_inc(s_tq, 1)
        block.tensor(t_qk)

        def s_qk(s):
            for m in range(32):
                for tc in range(NG):
                    i = m * 4 + tc
                    par = i % 2
                    s.wait_ge(s_tq, 4 * i + 1)
                    if i >= 1:
                        s.wait_ge(s_tq, 4 * (i - 1) + 2)
                    s.activation(sqs[:], p_mm[par][:], AF.Square).then_inc(s_sq, 1)
                    s.wait_ge(s_tq, 4 * i + 2)
                    if i >= 1:
                        s.wait_ge(s_vq, 3 * (i - 1) + 1)
                    s.activation(sdq_r[:], p_var[0:1, :], AF.Sqrt, scale=1.0 / HD,
                                 bias=1e-6).then_inc(s_sq, 1)
        block.scalar(s_qk)

        m_st = [[0], [0]]
        d_st = [_dma_rec(nc, s_st[0], m_st[0]), _dma_rec(nc, s_st[1], m_st[1])]

        def sy_qk(sy):
            for m in range(48):
                for tc in range(NG):
                    i = m * 4 + tc
                    par = i % 2
                    sy.wait_ge(s_vq, vq_out(i))
                    if m < 16:
                        d_st[par](sy, q0[m, :, tc * GT:(tc + 1) * GT], qout[:, par, :])
                    elif m < 32:
                        d_st[par](sy, k0[m - 16, :, tc * GT:(tc + 1) * GT], qout[:, par, :])
                    else:
                        d_st[par](sy, v0[m - 32, tc * GT:(tc + 1) * GT, :]
                                  .rearrange("(b p) d -> p b d", p=128),
                                  vout[:, par, :].rearrange("p (b d) -> p b d", b=4))
            sy.wait_ge(s_st[0], m_st[0][-1])
            sy.wait_ge(s_st[1], m_st[1][-1])
        block.sync(sy_qk)

        def v_qk(v):
            for m in range(48):
                isv = m >= 32
                for tc in range(NG):
                    i = m * 4 + tc
                    par = i % 2
                    if not isv:
                        v.wait_ge(s_sq, 2 * i + 2)
                        v.reciprocal(rsq_r[:], sdq_r[:]).then_inc(s_vq, 1)
                        v.wait_ge(s_tq, 4 * i + 3)
                        vedge(v, v.tensor_copy(grs[:], p_gb[:]))
                        v.tensor_mul(qn[:], p_mm[par][:], grs[:]).then_inc(s_vq, 1)
                        v.wait_ge(s_tq, 4 * i + 4)
                        v.tensor_mul(t1[:], qn[:], coss[:, tc * GT:(tc + 1) * GT])
                        vedge(v, v.tensor_mul(t2[:], p_rot[:], sins[:, tc * GT:(tc + 1) * GT]))
                        if i >= 2:
                            v.wait_ge(s_st[par], m_st[par][i // 2])
                        v.tensor_add(qout[:, par, :], t1[:], t2[:]).then_inc(s_vq, 1)
                    else:
                        v.wait_ge(s_tq, te_mm(i))
                        v.tensor_copy(qn[:], p_mm[par][:]).then_inc(s_vq, 1)
                        v.wait_ge(s_tq, te_tp(i))
                        if i >= 2:
                            v.wait_ge(s_st[par], m_st[par][i // 2])
                        v.tensor_copy(vout[:, par, :].rearrange("p (b n) -> p b n", b=4),
                                      p_vt[:, :, :]).then_inc(s_vq, 1)
        block.vector(v_qk)


    return nc


def hm_layout(x_tok):
    T = x_tok.shape[0]
    return np.ascontiguousarray(
        x_tok.reshape(T, 16, 128).transpose(2, 1, 0).reshape(128, 16 * T))


def retile_w(w, din=D):
    nm, nd = w.shape[1] // 128, din // 128
    return np.ascontiguousarray(
        w.reshape(nd, 128, nm, 128).transpose(2, 1, 0, 3).reshape(nm, 128, nd * 128))


def _prep_l0a_inputs(inputs):
    x_full = np.asarray(inputs["L0_attn_x"], dtype=np.float32)
    wqkv = retile_w(np.asarray(inputs["L0_attn_w_qkv"], dtype=np.float32))
    qg = np.asarray(inputs["L0_attn_q_gamma"], dtype=np.float32).reshape(1, 128)
    kg = np.asarray(inputs["L0_attn_k_gamma"], dtype=np.float32).reshape(1, 128)
    maps = []
    for c in range(8):
        b, r = c // 4, c % 4
        pos = core_positions(r)
        cos0, sin0, _, _ = rope_tables(pos)
        x_slice = x_full[b][pos]
        maps.append(dict(x=hm_layout(x_slice), wqkv=wqkv, qg=qg, kg=kg,
                         cos0=np.ascontiguousarray(cos0),
                         sin0=np.ascontiguousarray(sin0)))
    return maps


def run_l0a(inputs):
    nc = build_l0a()
    maps = _prep_l0a_inputs(inputs)
    res = run_bass_kernel_spmd(nc, maps, core_ids=list(range(8)))
    return res.results


def kernel(**inputs):
    ncA = build_l0a()
    resA = run_bass_kernel_spmd(ncA, _prep_l0a_inputs(inputs),
                                core_ids=list(range(8))).results
    ncB = build_l0b()
    resB = run_bass_kernel_spmd(ncB, _prep_l0b_inputs(inputs, resA),
                                core_ids=list(range(8))).results
    x2_cores = [np.asarray(resB[c]["x2o"]) for c in range(8)]
    ncC = build_l1b()
    resC = run_bass_kernel_spmd(ncC, _prep_l1b_inputs(inputs, x2_cores, resB),
                                core_ids=list(range(8))).results
    out = np.zeros((2, SEQ, D), dtype=np.float32)
    for c in range(8):
        b, r = c // 4, c % 4
        pos = core_positions(r)
        x4 = np.asarray(resC[c]["x4o"], dtype=np.float32)
        out[b][pos] = from_fm(x4.reshape(128, 16, CORE_TOK).transpose(1, 0, 2))
    return out



class _EngProxy:

    def __init__(self, cx, key):
        self._cx = cx
        self._key = key

    def live(self):
        return self._cx.mode == self._key

    def __getattr__(self, name):
        cx, key = self._cx, self._key

        def f(*a, **k):
            if cx.mode == key:
                cx.last[key] = getattr(cx.real, name)(*a, **k)
            return None

        return f

    def wait(self, sem_key, val):
        if val is None or val <= 0:
            return
        if self.live():
            self._cx.real.wait_ge(self._cx.sems[sem_key], val)

    def wait_dma(self, mark):
        if mark is None:
            assert self._cx.mode == "plan" or not self.live(), \
                "wait_dma on a mark from a not-yet-emitted engine"
            return
        if self.live():
            sem, val = mark
            if val > 0:
                self._cx.real.wait_ge(sem, val)

    def inc(self):
        cx, key = self._cx, self._key
        cx.ev[key] += 1
        if cx.mode == key:
            cx.last[key].then_inc(cx.sems[key], 1)
        return cx.ev[key]

    def edge(self):
        cx, key = self._cx, self._key
        ck = key + "ch"
        cx.ev[ck] = cx.ev.get(ck, 0) + 1
        if cx.mode == key:
            cx.last[key].then_inc(cx.sems[ck], 1)
            cx.real.wait_ge(cx.sems[ck], cx.ev[ck])


class Cx:

    ENGS = ("g", "sy", "t", "s", "v")

    def __init__(self, nc):
        self.nc = nc
        self.M = {}
        self.dmarks = {}
        self.mode = "plan"
        self.real = None
        self.last = {}
        self.ev = {k: 0 for k in self.ENGS}
        self.sems = {}
        for k in list(self.ENGS) + ["tch", "sch", "vch"]:
            self.sems[k] = nc.alloc_semaphore("se_" + k)
        self._dsems = {}
        for e in self.ENGS:
            setattr(self, e, _EngProxy(self, e))

    def begin(self, mode, real=None):
        self.mode = mode
        self.real = real
        self.ev = {k: 0 for k in self.ENGS}
        for k in ("tch", "sch", "vch"):
            self.ev[k] = 0
        self._dcnt = {k: 0 for k in self.dmarks}
        self._emitted = getattr(self, "_emitted", set())
        if mode != "plan":
            self._emitted.add(mode)

    def mark(self, name, eng):
        if self.mode == "plan":
            self.M[name] = self.ev[eng]
        return self.M.get(name)

    def dma(self, eng_key, slot, out, in_):
        if slot not in self._dsems:
            self._dsems[slot] = self.nc.alloc_semaphore("sd_" + slot)
            self.dmarks[slot] = [0]
            self._dcnt[slot] = 0
        self._dcnt[slot] += 1
        n = self._dcnt[slot]
        if self.mode == eng_key:
            n0 = self.nc.n_instructions()
            self.real.dma_start(out=out, in_=in_).then_inc(self._dsems[slot], 16)
            assert len(self.dmarks[slot]) == n, (slot, n, len(self.dmarks[slot]))
            self.dmarks[slot].append(
                self.dmarks[slot][-1] + 16 * (self.nc.n_instructions() - n0))
        if self.mode == "plan" or len(self.dmarks[slot]) <= n:
            return None
        return (self._dsems[slot], self.dmarks[slot][n])

    def run(self, block, body):
        self.begin("plan")
        body(self)
        order = [("sy", block.sync), ("g", block.gpsimd), ("t", block.tensor),
                 ("s", block.scalar), ("v", block.vector)]
        for key, starter in order:
            def f(real, key=key):
                self.begin(key, real)
                body(self)
            starter(f)



NEGC_SCALE = float(-np.sqrt(128.0) * 1.002)


def build_l0b(upto="l1a"):
    nc = bass.Bass("TRN2", target_bir_lowering=False, debug=False)
    x = nc.dram_tensor("x", [128, 16 * CORE_TOK], F32, kind="ExternalInput").ap()
    q0o = nc.dram_tensor("q0o", [NH, 128, CORE_TOK], BF16, kind="ExternalInput").ap()
    k0g = nc.dram_tensor("k0g", [NG, NH, 128, KSPAN * 128], BF16, kind="ExternalInput").ap()
    v0g = nc.dram_tensor("v0g", [NG, NH, 128, KSPAN * 128], BF16, kind="ExternalInput").ap()
    masks = nc.dram_tensor("masks", [NG, 128, KSPAN * GT], BF16, kind="ExternalInput").ap()
    qg = nc.dram_tensor("qg", [1, 128], F32, kind="ExternalInput").ap()
    kg = nc.dram_tensor("kg", [1, 128], F32, kind="ExternalInput").ap()
    wout = nc.dram_tensor("wout", [16, 128, 16 * 128], F32, kind="ExternalInput").ap()
    wg_t = nc.dram_tensor("wg_t", [64, 128, 16 * 128], F32, kind="ExternalInput").ap()
    wu_t = nc.dram_tensor("wu_t", [64, 128, 16 * 128], F32, kind="ExternalInput").ap()
    wd_t = nc.dram_tensor("wd_t", [16, 128, 64 * 128], F32, kind="ExternalInput").ap()
    wq_t = nc.dram_tensor("wq_t", [16, 128, 16 * 128], F32, kind="ExternalInput").ap()
    wk_t = nc.dram_tensor("wk_t", [16, 128, 16 * 128], F32, kind="ExternalInput").ap()
    wv_t = nc.dram_tensor("wv_t", [16, 128, 16 * 128], F32, kind="ExternalInput").ap()
    cos1 = nc.dram_tensor("cos1", [128, CORE_TOK], F32, kind="ExternalInput").ap()
    sin1 = nc.dram_tensor("sin1", [128, CORE_TOK], F32, kind="ExternalInput").ap()
    x1o = nc.dram_tensor("x1o", [128, 16 * CORE_TOK], F32, kind="ExternalOutput").ap()
    x2o = nc.dram_tensor("x2o", [128, 16 * CORE_TOK], F32, kind="ExternalOutput").ap()
    q1 = nc.dram_tensor("q1", [NH, 128, CORE_TOK], BF16, kind="ExternalOutput").ap()
    k1 = nc.dram_tensor("k1", [NH, 128, CORE_TOK], BF16, kind="ExternalOutput").ap()
    v1 = nc.dram_tensor("v1", [NH, CORE_TOK, 128], BF16, kind="ExternalOutput").ap()
    qkn = nc.dram_tensor("qkn", [1, 2 * NH], F32, kind="ExternalOutput").ap()

    C, cload, NC_ = _consts(nc)
    sb = nc.alloc_sbuf_tensor
    gq_s = sb("gq_s", [1, 128], F32)
    gk_s = sb("gk_s", [1, 128], F32)
    coss = sb("coss", [128, CORE_TOK], F32)
    sins = sb("sins", [128, CORE_TOK], F32)
    wA = sb("wA", [128, 4, 16 * 128], BF16)
    rows = sb("rows", [1, 12, GT], F32)
    (mean_r, msq_r, es_r, var_r, sd_r, rstd_r, mr_r, den_r, sdq_r, rsq_r,
     nrm_r, spare_r) = [rows[0:1, i, :] for i in range(12)]
    tiny = sb("tiny", [1, 16], F32)
    negc = sb("negc", [128, 1], F32)
    abc = sb("abc", [128, GT], F32)
    bbc = sb("bbc", [128, GT], F32)
    qb = sb("qb", [128, 2, GT], BF16)
    probs = sb("probs", [128, 2, GT], BF16)
    t1 = sb("t1", [128, GT], F32)
    t2 = sb("t2", [128, GT], F32)
    qn = sb("qn", [128, GT], BF16)
    sqs = sb("sqs", [128, GT], BF16)
    qout = sb("qout", [128, 2, GT], BF16)
    vout = sb("vout", [128, 2, GT], BF16)
    rbc = sb("rbc", [128, GT], F32)
    nacc = sb("nacc", [1, 2 * NH], F32)
    xout = sb("xout", [128, 2, GT], F32)

    ARENA = 112 * 1024
    arena = sb("arena", [128, ARENA], mybir.dt.uint8)
    abase = nc.lookup_mloc(arena).addr

    def at(name, shape, dtype, off):
        return nc.alloc_sbuf_tensor_at(name, shape, dtype, offset=abase + off)

    kbuf = at("kbuf", [128, 2, KSPAN * 128], BF16, 0)
    vbuf = at("vbuf", [128, 2, KSPAN * 128], BF16, 18 * 1024)
    mbuf = at("mbuf", [128, KSPAN, GT], BF16, 36 * 1024)
    ao = at("ao", [128, 16, GT], BF16, 72 * 1024)
    yb = at("yb", [128, 16, GT], BF16, 88 * 1024)
    ysq = at("ysq", [128, GT], BF16, 104 * 1024)
    xres = at("xres", [128, 16, GT], F32, 36 * 1024)
    xb1 = at("xb1", [128, 16, GT], BF16, 0)
    xhat1 = at("xhat1", [128, 16, GT], BF16, 16 * 1024)
    gated = at("gated", [128, 64, GT], BF16, 32 * 1024)
    wB = at("wB", [128, 2, 32 * 128], BF16, 96 * 1024)
    yb0 = at("yb0", [128, 16, GT], BF16, 0)
    ysq0 = at("ysq0", [128, GT], BF16, 16 * 1024)
    xres0 = at("xres0", [128, 2, GT], F32, 17 * 1024)
    xres0f = at("xres0f", [128, 16, GT], F32, 32 * 1024)
    xb2 = at("xb2", [128, 16, GT], BF16, 0)
    xhat2 = at("xhat2", [128, 16, CORE_TOK], BF16, 16 * 1024)
    sql1 = at("sql1", [128, 16, GT], BF16, 80 * 1024)
    at0 = dict(xres0f=xres0f, sql1=sql1)

    ps = nc.alloc_psum_tensor
    pA = [ps(f"pA{i}", [128, GT], F32) for i in range(4)]
    pB = [ps(f"pB{i}", [1, GT], F32) for i in range(2)]
    pC = ps("pC", [128, 4, 128], BF16)
    pD = ps("pD", [128, GT], F32)

    cx = Cx(nc)
    M = cx.M

    def ln_rows(cx, key, bca=pA[2], bcb=pA[3]):
        s, v, t = cx.s, cx.v, cx.t
        s.wait("t", M.get((key, "s12t")))
        s.mul(mean_r, pB[0][0:1, :], 1.0 / D)
        s.edge()
        s.activation(msq_r, mean_r, AF.Square)
        s.edge()
        s.mul(es_r, pB[1][0:1, :], 1.0 / D)
        s.inc()
        cx.mark((key, "es"), "s")
        v.wait("s", M.get((key, "es")))
        v.tensor_sub(var_r, es_r, msq_r)
        v.inc()
        cx.mark((key, "var"), "v")
        s.wait("v", M.get((key, "var")))
        s.activation(sd_r, var_r, AF.Sqrt, bias=1e-5)
        s.inc()
        cx.mark((key, "sd"), "s")
        v.wait("s", M.get((key, "sd")))
        v.reciprocal(rstd_r, sd_r)
        v.edge()
        v.tensor_mul(mr_r, mean_r, rstd_r)
        v.inc()
        cx.mark((key, "mr"), "v")
        t.wait("v", M.get((key, "mr")))
        t.matmul(bca[:], C["ocol"][:], rstd_r, start=True, stop=True)
        t.matmul(bcb[:], C["ocol"][:], mr_r, start=True, stop=True)
        t.inc()
        cx.mark((key, "bc"), "t")
        v.wait("t", M.get((key, "bc")))
        v.tensor_copy(abc[:], bca[:])
        v.tensor_copy(bbc[:], bcb[:])
        v.inc()
        cx.mark((key, "abv"), "v")
        v.wait("v", M.get((key, "abv")))

    def body(cx):
        g, sy, t, s, v = cx.g, cx.sy, cx.t, cx.s, cx.v
        mk = {}
        for dst, srcap in C["pairs"]:
            mk["c"] = cx.dma("g", "c", dst, srcap)
        cx.dma("g", "c", gq_s[:], qg)
        cx.dma("g", "c", gk_s[:], kg)
        cx.dma("g", "c", coss[:], cos1)
        mk["c"] = cx.dma("g", "c", sins[:], sin1)

        v.wait_dma(mk["c"])
        v.memset(nacc[:], 0.0)
        v.reduce_max(tiny[0:1, 0:1], gq_s[:], axis=AX, apply_absolute_value=True)
        v.edge()
        v.reduce_max(tiny[0:1, 1:2], gk_s[:], axis=AX, apply_absolute_value=True)
        v.edge()
        v.tensor_mul(tiny[0:1, 2:3], tiny[0:1, 0:1], tiny[0:1, 1:2])
        v.inc()
        cx.mark("negv", "v")
        s.wait("v", M.get("negv"))
        s.activation(tiny[0:1, 3:4], tiny[0:1, 2:3], AF.Copy,
                     scale=NEGC_SCALE, bias=-0.01)
        s.inc()
        cx.mark("negs", "s")
        t.wait("s", M.get("negs"))
        t.wait_dma(mk["c"])
        t.matmul(pD[:, 0:1], C["ocol"][:], tiny[0:1, 3:4], start=True, stop=True)
        t.inc()
        cx.mark("negt", "t")
        v.wait("t", M.get("negt"))
        v.tensor_copy(negc[:], pD[:, 0:1])
        v.inc()
        cx.mark("negbc", "v")

        for gi in range(NG):
            if gi >= 1:
                g.wait("v", M.get(("resv", gi - 1, 15)))
            mk["m"] = cx.dma("g", "m", mbuf[:, :, :].rearrange("p a n -> p (a n)"),
                             masks[gi, :, :])
            for h in range(NH):
                a = gi * NH + h
                hp = a % 2
                if a >= 2:
                    g.wait("t", M.get(("att_t_end", a - 2)))
                cx.dma("g", "k%d" % hp, kbuf[:, hp, :], k0g[gi, h, :, :])
                cx.dma("g", "k%d" % hp, vbuf[:, hp, :], v0g[gi, h, :, :])
                mk[("kvq", a)] = cx.dma("g", "k%d" % hp, qb[:, hp, :],
                                        q0o[h, :, gi * GT:(gi + 1) * GT])
                t.wait_dma(mk[("kvq", a)])
                for i in range(KSPAN):
                    ip = i % 2
                    if i >= 2:
                        t.wait("s", M.get(("exp", a, i - 2)))
                    elif a >= 1:
                        t.wait("s", M.get(("exp", a - 1, KSPAN - 2 + i)))
                    if i <= 1 and h == 0 and gi >= 1:
                        t.wait("s", M.get(("ye", gi - 1, 15)))
                    t.matmul(pA[ip][:], kbuf[:, hp, i * 128:(i + 1) * 128],
                             qb[:, hp, :], start=True, stop=True)
                    t.inc()
                    cx.mark(("sc", a, i), "t")
                    t.wait("v", M.get(("pm", a, i)))
                    if i == 0:
                        if a >= 1:
                            t.wait("v", M.get(("aov", a - 1)))
                        if h == 0 and gi >= 1:
                            t.wait("s", M.get(("opmean", gi - 1)))
                    t.matmul(pB[0][:], C["o128"][:], probs[:, ip, :],
                             start=(i == 0), stop=(i == KSPAN - 1),
                             skip_group_check=True)
                    t.matmul(pA[2][:], vbuf[:, hp, i * 128:(i + 1) * 128],
                             probs[:, ip, :], start=(i == 0), stop=(i == KSPAN - 1),
                             skip_group_check=True)
                    t.inc()
                    cx.mark(("pv", a, i), "t")
                t.wait("v", M.get(("rec", a)))
                t.matmul(pD[:], C["ocol"][:], den_r, start=True, stop=True)
                t.inc()
                cx.mark(("rbc", a), "t")
                cx.mark(("att_t_end", a), "t")
                if a == 0:
                    s.wait("v", M.get("negbc"))
                for i in range(KSPAN):
                    ip = i % 2
                    s.wait("t", M.get(("sc", a, i)))
                    if i >= 2:
                        s.wait("t", M.get(("pv", a, i - 2)))
                    elif a >= 1:
                        s.wait("t", M.get(("pv", a - 1, KSPAN - 2 + i)))
                    s.activation(probs[:, ip, :], pA[ip][:], AF.Exp,
                                 scale=SCALE, bias=negc[:, 0:1])
                    s.inc()
                    cx.mark(("exp", a, i), "s")
                if h == 0:
                    v.wait_dma(mk["m"])
                for i in range(KSPAN):
                    ip = i % 2
                    v.wait("s", M.get(("exp", a, i)))
                    v.tensor_mul(probs[:, ip, :], probs[:, ip, :], mbuf[:, i, :])
                    v.inc()
                    cx.mark(("pm", a, i), "v")
                v.wait("t", M.get(("pv", a, KSPAN - 1)))
                v.reciprocal(den_r, pB[0][0:1, :])
                v.inc()
                cx.mark(("rec", a), "v")
                v.wait("t", M.get(("rbc", a)))
                if a >= 1:
                    v.wait("v", M.get(("aov", a - 1)))
                v.tensor_copy(rbc[:], pD[:])
                v.edge()
                v.tensor_mul(ao[:, h, :], pA[2][:], rbc[:])
                v.inc()
                cx.mark(("aov", a), "v")
            cx.mark(("attv_end", gi), "v")

            for m in range(16):
                mp = m % 2
                if m >= 2:
                    g.wait("t", M.get(("yt", gi, m - 2)))
                else:
                    g.wait("t", M.get(("att_t_end", gi * NH + NH - 1)))
                mk[("w", gi, m)] = cx.dma("g", "w%d" % mp, wA[:, mp, :], wout[m, :, :])
                mk[("xr", gi)] = cx.dma(
                    "g", "xr", xres[:, m, :],
                    x[:, m * CORE_TOK + gi * GT:m * CORE_TOK + (gi + 1) * GT])
                t.wait_dma(mk[("w", gi, m)])
                if m >= 2:
                    t.wait("s", M.get(("ye", gi, m - 2)))
                if m == 0:
                    t.wait("v", M.get(("aov", gi * NH + NH - 1)))
                for di in range(16):
                    t.matmul(pA[mp][:], wA[:, mp, di * 128:(di + 1) * 128],
                             ao[:, di, :], start=(di == 0), stop=(di == 15))
                t.inc()
                cx.mark(("yt", gi, m), "t")
                s.wait("t", M.get(("yt", gi, m)))
                s.copy(yb[:, m, :], pA[mp][:])
                s.edge()
                s.activation(ysq[:], yb[:, m, :], AF.Square)
                s.inc()
                cx.mark(("ye", gi, m), "s")
                t.wait("s", M.get(("ye", gi, m)))
                if m == 0 and gi >= 1:
                    t.wait("s", M.get(("opes", gi - 1)))
                t.matmul(pB[0][:], C["o128"][:], yb[:, m, :],
                         start=(m == 0), stop=(m == 15), skip_group_check=True)
                t.matmul(pB[1][:], C["o128"][:], ysq[:],
                         start=(m == 0), stop=(m == 15), skip_group_check=True)
                t.inc()
                if m == 15:
                    cx.mark((("op", gi), "s12t"), "t")
            ln_rows(cx, ("op", gi))
            if cx.mode == "plan":
                M[("opmean", gi)] = M[(("op", gi), "es")]
                M[("opes", gi)] = M[(("op", gi), "es")]
            for m in range(16):
                mp = m % 2
                if m == 0:
                    v.wait_dma(mk[("xr", gi)])
                    if gi >= 1:
                        v.wait("v", M.get(("resv", gi - 1, 15)))
                else:
                    v.wait("v", M.get(("resv", gi, m - 1)))
                if m >= 2:
                    v.wait_dma(mk.get(("st", gi, m - 2)))
                elif gi >= 1:
                    v.wait_dma(mk.get(("st", gi - 1, m + 14)))
                v.tensor_sub(t1[:], xres[:, m, :], bbc[:])
                v.tensor_mul(t2[:], yb[:, m, :], abc[:])
                v.edge()
                v.tensor_add(xout[:, mp, :], t1[:], t2[:])
                v.inc()
                cx.mark(("resv", gi, m), "v")
                sy.wait("v", M.get(("resv", gi, m)))
                mk[("st", gi, m)] = cx.dma(
                    "sy", "st%d" % mp,
                    x1o[:, m * CORE_TOK + gi * GT:m * CORE_TOK + (gi + 1) * GT],
                    xout[:, mp, :])
        def drain(slots):
            for sl in slots:
                if cx.mode != "plan" and sl in cx.dmarks:
                    sy.wait_dma((cx._dsems[sl], cx.dmarks[sl][-1]))

        if upto == "att":
            drain(("st0", "st1"))
            return

        xres0f = at0["xres0f"]
        for c in range(NG):
            g.wait("t", M.get(("att_t_end", 63)) if c == 0 else None)
            if c >= 1:
                g.wait("v", M.get(("f0resv", c - 1, 15)))
            g.wait_dma(mk.get(("st", c, 14)))
            g.wait_dma(mk.get(("st", c, 15)))
            mk[("xb1", c)] = cx.dma(
                "g", "xc", xb1[:, :, :],
                x1o[:, :].rearrange("p (a n) -> p a n", a=16)[:, :, c * GT:(c + 1) * GT])
            t.wait_dma(mk[("xb1", c)])
            if c == 0:
                t.wait("s", M.get((("op", 3), "es")))
            else:
                t.wait("s", M.get((("f0o", c - 1), "es")))
            for di in range(16):
                t.matmul(pB[0][:], C["o128"][:], xb1[:, di, :],
                         start=(di == 0), stop=(di == 15), skip_group_check=True)
            t.inc()
            cx.mark(("f0s1", c), "t")
            s.wait_dma(mk[("xb1", c)])
            if c >= 1:
                s.wait("t", M.get(("f0s2", c - 1)))
            for di in range(16):
                s.activation(xhat1[:, di, :], xb1[:, di, :], AF.Square)
            s.inc()
            cx.mark(("f0sq", c), "s")
            t.wait("s", M.get(("f0sq", c)))
            for di in range(16):
                t.matmul(pB[1][:], C["o128"][:], xhat1[:, di, :],
                         start=(di == 0), stop=(di == 15), skip_group_check=True)
            t.inc()
            cx.mark(("f0s2", c), "t")
            if cx.mode == "plan":
                M[(("f0i", c), "s12t")] = M[("f0s2", c)]
            ln_rows(cx, ("f0i", c))
            for di in range(16):
                if di == 0:
                    v.wait("t", M.get(("f0s2", c)))
                v.tensor_mul(xhat1[:, di, :], xb1[:, di, :], abc[:])
            v.edge()
            for di in range(16):
                v.tensor_sub(xhat1[:, di, :], xhat1[:, di, :], bbc[:])
            v.inc()
            cx.mark(("f0xh", c), "v")
            for ht in range(64):
                hp = ht % 2
                if ht >= 2:
                    g.wait("t", M.get(("f0g", c, ht - 2)))
                elif c == 0:
                    g.wait("t", M.get(("yt", 3, 14 + ht)))
                else:
                    g.wait("t", M.get(("f0g", c - 1, 62 + ht)))
                cx.dma("g", "wg%d" % hp, wA[:, hp, :], wg_t[ht, :, :])
                mk[("wgu", c, ht)] = cx.dma("g", "wg%d" % hp, wA[:, 2 + hp, :],
                                            wu_t[ht, :, :])
                t.wait_dma(mk[("wgu", c, ht)])
                if ht == 0:
                    t.wait("v", M.get(("f0xh", c)))
                if ht >= 1:
                    t.wait("s", M.get(("f0sl", c, ht - 1)))
                for di in range(16):
                    t.matmul(pA[0][:], wA[:, hp, di * 128:(di + 1) * 128],
                             xhat1[:, di, :], start=(di == 0), stop=(di == 15))
                t.inc()
                cx.mark(("f0gm", c, ht), "t")
                if ht >= 1:
                    t.wait("v", M.get(("f0gt", c, ht - 1)))
                for di in range(16):
                    t.matmul(pA[1][:], wA[:, 2 + hp, di * 128:(di + 1) * 128],
                             xhat1[:, di, :], start=(di == 0), stop=(di == 15))
                t.inc()
                cx.mark(("f0g", c, ht), "t")
                s.wait("t", M.get(("f0gm", c, ht)))
                if ht >= 1:
                    s.wait("v", M.get(("f0gt", c, ht - 1)))
                s.activation(sqs[:], pA[0][:], AF.Silu)
                s.inc()
                cx.mark(("f0sl", c, ht), "s")
                v.wait("s", M.get(("f0sl", c, ht)))
                v.wait("t", M.get(("f0g", c, ht)))
                if ht <= 1 and c >= 1:
                    v.wait("v", M.get(("f0resv", c - 1, 15)))
                v.tensor_mul(gated[:, ht, :], sqs[:], pA[1][:])
                v.inc()
                cx.mark(("f0gt", c, ht), "v")
            for j in range(16):
                jp = j % 2
                for half in range(2):
                    wp = (2 * j + half) % 2
                    if 2 * j + half >= 2:
                        g.wait("t", M.get(("f0dm", c, (2 * j + half - 2) // 2,
                                           (2 * j + half - 2) % 2)))
                    mk[("wd", c, j, half)] = cx.dma(
                        "g", "wd%d" % wp, wB[:, wp, :],
                        wd_t[j, :, half * 4096:(half + 1) * 4096])
                    t.wait_dma(mk[("wd", c, j, half)])
                    if j == 0 and half == 0:
                        t.wait("v", M.get(("f0gt", c, 63)))
                    if half == 0 and j >= 2:
                        t.wait("s", M.get(("f0ye", c, j - 2)))
                    for q in range(32):
                        t.matmul(pA[jp][:], wB[:, wp, q * 128:(q + 1) * 128],
                                 gated[:, half * 32 + q, :],
                                 start=(half == 0 and q == 0),
                                 stop=(half == 1 and q == 31))
                    t.inc()
                    cx.mark(("f0dm", c, j, half), "t")
                s.wait("t", M.get(("f0dm", c, j, 1)))
                if c >= 1 and j <= 1:
                    s.wait("v", M.get(("f0resv", c - 1, 15)))
                s.copy(yb0[:, j, :], pA[jp][:])
                s.edge()
                s.activation(ysq0[:], yb0[:, j, :], AF.Square)
                s.inc()
                cx.mark(("f0ye", c, j), "s")
                t.wait("s", M.get(("f0ye", c, j)))
                if j == 0:
                    t.wait("s", M.get((("f0i", c), "es")))
                t.matmul(pB[0][:], C["o128"][:], yb0[:, j, :],
                         start=(j == 0), stop=(j == 15), skip_group_check=True)
                t.matmul(pB[1][:], C["o128"][:], ysq0[:],
                         start=(j == 0), stop=(j == 15), skip_group_check=True)
                t.inc()
                if j == 15:
                    cx.mark((("f0o", c), "s12t"), "t")
            ln_rows(cx, ("f0o", c))
            g.wait("t", M.get(("f0dm", c, 15, 1)))
            for m in range(16):
                mk[("xr0", c)] = cx.dma(
                    "g", "xr0", xres0f[:, m, :],
                    x1o[:, m * CORE_TOK + c * GT:m * CORE_TOK + (c + 1) * GT])
            for m in range(16):
                mp = m % 2
                if m == 0:
                    v.wait_dma(mk[("xr0", c)])
                    if c >= 1:
                        v.wait("v", M.get(("f0resv", c - 1, 15)))
                    else:
                        v.wait("v", M.get(("resv", 3, 15)))
                else:
                    v.wait("v", M.get(("f0resv", c, m - 1)))
                if m >= 2:
                    v.wait_dma(mk.get(("stx2", c, m - 2)))
                elif c >= 1:
                    v.wait_dma(mk.get(("stx2", c - 1, m + 14)))
                else:
                    v.wait_dma(mk.get(("st", 3, m + 14)))
                v.tensor_sub(t1[:], xres0f[:, m, :], bbc[:])
                v.tensor_mul(t2[:], yb0[:, m, :], abc[:])
                v.edge()
                v.tensor_add(xout[:, mp, :], t1[:], t2[:])
                v.inc()
                cx.mark(("f0resv", c, m), "v")
                sy.wait("v", M.get(("f0resv", c, m)))
                mk[("stx2", c, m)] = cx.dma(
                    "sy", "st%d" % mp,
                    x2o[:, m * CORE_TOK + c * GT:m * CORE_TOK + (c + 1) * GT],
                    xout[:, mp, :])
        if upto == "f0":
            drain(("st0", "st1"))
            return

        sql1 = at0["sql1"]
        for c in range(NG):
            g.wait("v", M.get(("l1xh", c - 1)) if c >= 1 else
               M.get(("f0resv", 3, 15)))
            g.wait_dma(mk.get(("stx2", c, 14)))
            g.wait_dma(mk.get(("stx2", c, 15)))
            mk[("xb2", c)] = cx.dma(
                "g", "xc", xb2[:, :, :],
                x2o[:, :].rearrange("p (a n) -> p a n", a=16)[:, :, c * GT:(c + 1) * GT])
            t.wait_dma(mk[("xb2", c)])
            if c == 0:
                t.wait("s", M.get((("f0o", 3), "es")))
            else:
                t.wait("s", M.get((("l1i", c - 1), "es")))
            for di in range(16):
                t.matmul(pB[0][:], C["o128"][:], xb2[:, di, :],
                         start=(di == 0), stop=(di == 15), skip_group_check=True)
            t.inc()
            cx.mark(("l1s1", c), "t")
            s.wait_dma(mk[("xb2", c)])
            if c >= 1:
                s.wait("t", M.get(("l1s2", c - 1)))
            if c == 0:
                s.wait("v", M.get(("f0resv", 3, 15)))
            for di in range(16):
                s.activation(sql1[:, di, :], xb2[:, di, :], AF.Square)
            s.inc()
            cx.mark(("l1sq", c), "s")
            t.wait("s", M.get(("l1sq", c)))
            for di in range(16):
                t.matmul(pB[1][:], C["o128"][:], sql1[:, di, :],
                         start=(di == 0), stop=(di == 15), skip_group_check=True)
            t.inc()
            cx.mark(("l1s2", c), "t")
            if cx.mode == "plan":
                M[(("l1i", c), "s12t")] = M[("l1s2", c)]
            ln_rows(cx, ("l1i", c))
            for di in range(16):
                if di == 0:
                    v.wait("t", M.get(("l1s2", c)))
                    if c == 0:
                        v.wait("v", M.get(("f0resv", 3, 15)))
                v.tensor_mul(xhat2[:, di, c * GT:(c + 1) * GT], xb2[:, di, :], abc[:])
            v.edge()
            for di in range(16):
                v.tensor_sub(xhat2[:, di, c * GT:(c + 1) * GT],
                             xhat2[:, di, c * GT:(c + 1) * GT], bbc[:])
            v.inc()
            cx.mark(("l1xh", c), "v")

        wmats = [wq_t, wk_t, wv_t]
        for m in range(48):
            mp = m % 2
            if m >= 2:
                g.wait("t", M.get(("l1mm", (m - 2) * 4 + 3)))
            elif cx.mode == "plan" or True:
                g.wait("t", M.get(("f0g", 3, 62 + m)))
            mk[("l1w", m)] = cx.dma("g", "w%d" % mp, wA[:, mp, :],
                                    wmats[m // 16][m % 16, :, :])
            isv = m >= 32
            for tc in range(NG):
                i = m * 4 + tc
                par = i % 2
                t.wait_dma(mk[("l1w", m)])
                if i == 0:
                    t.wait("v", M.get(("l1xh", 3)))
                    t.wait("s", M.get(("f0ye", 3, 15)))
                if i >= 2:
                    t.wait("v", M.get(("l1qn", i - 2)))
                for di in range(16):
                    t.matmul(pA[par][:], wA[:, mp, di * 128:(di + 1) * 128],
                             xhat2[:, di, tc * GT:(tc + 1) * GT],
                             start=(di == 0), stop=(di == 15))
                t.inc()
                cx.mark(("l1mm", i), "t")
                if not isv:
                    s.wait("t", M.get(("l1mm", i)))
                    if i >= 1:
                        s.wait("t", M.get(("l1nt", i - 1)))
                    s.activation(sqs[:], pA[par][:], AF.Square)
                    s.inc()
                    cx.mark(("l1sqp", i), "s")
                    t.wait("s", M.get(("l1sqp", i)))
                    if i >= 1:
                        t.wait("v", M.get(("l1nv", i - 1)))
                    elif cx.mode == "plan" or True:
                        t.wait("s", M.get((("l1i", 3), "es")))
                    t.matmul(pB[0][:], C["o128"][:], sqs[:], start=True, stop=True)
                    t.inc()
                    cx.mark(("l1nt", i), "t")
                    v.wait("t", M.get(("l1nt", i)))
                    v.reduce_max(tiny[0:1, 4:5], pB[0][0:1, :], axis=AX)
                    v.edge()
                    slot = (m // 16) * 16 + (m % 16)
                    v.tensor_max(nacc[0:1, slot:slot + 1], nacc[0:1, slot:slot + 1],
                                 tiny[0:1, 4:5])
                    v.inc()
                    cx.mark(("l1nv", i), "v")
                    v.wait("t", M.get(("l1mm", i)))
                    v.tensor_copy(qn[:], pA[par][:])
                    v.inc()
                    cx.mark(("l1qn", i), "v")
                    t.wait("v", M.get(("l1qn", i)))
                    if i >= 1:
                        t.wait("v", M.get(("l1out", i - 1)))
                    t.matmul(pD[:], C["P1"][:], qn[:], start=True, stop=True)
                    t.inc()
                    cx.mark(("l1rot", i), "t")
                    v.wait("t", M.get(("l1rot", i)))
                    v.tensor_mul(t1[:], qn[:], coss[:, tc * GT:(tc + 1) * GT])
                    v.tensor_mul(t2[:], pD[:], sins[:, tc * GT:(tc + 1) * GT])
                    v.edge()
                    if i >= 2:
                        v.wait_dma(mk.get(("l1st", i - 2)))
                    v.tensor_add(qout[:, par, :], t1[:], t2[:])
                    v.inc()
                    cx.mark(("l1out", i), "v")
                    sy.wait("v", M.get(("l1out", i)))
                    dst = (q1 if m < 16 else k1)[m % 16, :, tc * GT:(tc + 1) * GT]
                    mk[("l1st", i)] = cx.dma("sy", "st%d" % par, dst, qout[:, par, :])
                else:
                    v.wait("t", M.get(("l1mm", i)))
                    v.tensor_copy(qn[:], pA[par][:])
                    v.inc()
                    cx.mark(("l1qn", i), "v")
                    t.wait("v", M.get(("l1qn", i)))
                    if i >= 1:
                        t.wait("v", M.get(("l1out", i - 1)))
                    for blk in range(4):
                        t.transpose(pC[:, blk, :], qn[:, blk * 128:(blk + 1) * 128],
                                    C["id128"][:])
                    t.inc()
                    cx.mark(("l1rot", i), "t")
                    v.wait("t", M.get(("l1rot", i)))
                    if i >= 2:
                        v.wait_dma(mk.get(("l1st", i - 2)))
                    v.tensor_copy(vout[:, par, :].rearrange("p (b n) -> p b n", b=4),
                                  pC[:, :, :])
                    v.inc()
                    cx.mark(("l1out", i), "v")
                    sy.wait("v", M.get(("l1out", i)))
                    mk[("l1st", i)] = cx.dma(
                        "sy", "st%d" % par,
                        v1[m - 32, tc * GT:(tc + 1) * GT, :]
                        .rearrange("(b p) d -> p b d", p=128),
                        vout[:, par, :].rearrange("p (b d) -> p b d", b=4))
        sy.wait("v", M.get(("l1nv", 127)))
        cx.dma("sy", "st0", qkn[:, :], nacc[:, :])
        drain(("st0", "st1"))

    with nc.Block() as block:
        cx.run(block, body)
    return nc


def build_masks(r):
    out = np.zeros((NG, 128, KSPAN, NG, 128), dtype=BF)
    for g in range(NG):
        kg_run = RUNS[r][g]
        for i in range(KSPAN):
            kb = 4 * kg_run - 32 + i
            if kb < 0:
                continue
            kpos = kb * 128 + np.arange(128)[:, None, None]
            qpos = ((4 * kg_run + np.arange(NG)) * 128)[None, :, None] \
                + np.arange(128)[None, None, :]
            valid = (kpos <= qpos) & (qpos - kpos <= WINDOW)
            out[g, :, i, :, :] = valid.astype(BF)
    return np.ascontiguousarray(out.reshape(NG, 128, KSPAN * NG * 128))


def _prep_l0b_inputs(inputs, resA):
    x_full = np.asarray(inputs["L0_attn_x"], dtype=np.float32)
    qg = np.asarray(inputs["L0_attn_q_gamma"], dtype=np.float32).reshape(1, 128)
    kg = np.asarray(inputs["L0_attn_k_gamma"], dtype=np.float32).reshape(1, 128)
    wout = retile_w(np.asarray(inputs["L0_attn_w_out"], dtype=np.float32))
    wg_t = retile_w(np.ascontiguousarray(
        np.asarray(inputs["L0_ffn_w_gate"], dtype=np.float32).T))
    wu_t = retile_w(np.ascontiguousarray(
        np.asarray(inputs["L0_ffn_w_up"], dtype=np.float32).T))
    wd_t = retile_w(np.ascontiguousarray(
        np.asarray(inputs["L0_ffn_w_down"], dtype=np.float32).T), din=8192)
    wq_t = retile_w(np.ascontiguousarray(
        np.asarray(inputs["L1_attn_wq"], dtype=np.float32).T))
    wk_t = retile_w(np.ascontiguousarray(
        np.asarray(inputs["L1_attn_wk"], dtype=np.float32).T))
    wv_t = retile_w(np.ascontiguousarray(
        np.asarray(inputs["L1_attn_wv"], dtype=np.float32).T))
    owner = {}
    for rr in range(4):
        for li, bb in enumerate(core_blocks(rr)):
            owner[bb] = (rr, li)
    maps = []
    for c in range(8):
        b, r = c // 4, c % 4
        pos = core_positions(r)
        _, _, cos1, sin1 = rope_tables(pos)
        k0g = np.zeros((NG, NH, 128, KSPAN * 128), dtype=BF)
        v0g = np.zeros((NG, NH, 128, KSPAN * 128), dtype=BF)
        for g in range(NG):
            kg_run = RUNS[r][g]
            for i in range(KSPAN):
                bb = 4 * kg_run - 32 + i
                if bb < 0:
                    continue
                rb, li = owner[bb]
                src = resA[b * 4 + rb]
                k0g[g, :, :, i * 128:(i + 1) * 128] = src["k0"][:, :, li * 128:(li + 1) * 128]
                v0g[g, :, :, i * 128:(i + 1) * 128] = src["v0"][:, li * 128:(li + 1) * 128, :]
        maps.append(dict(
            x=hm_layout(x_full[b][pos]), q0o=resA[c]["q0"],
            k0g=k0g, v0g=v0g, masks=build_masks(r), qg=qg, kg=kg,
            wout=wout, wg_t=wg_t, wu_t=wu_t, wd_t=wd_t,
            wq_t=wq_t, wk_t=wk_t, wv_t=wv_t,
            cos1=np.ascontiguousarray(cos1), sin1=np.ascontiguousarray(sin1)))
    return maps


def _f1_consts(nc):
    ssel = np.zeros((8, 128, 64), dtype=np.float32)
    esel = np.zeros((8, 128, 128), dtype=BF)
    for g in range(8):
        for p in range(128):
            a = 8 * g + p // 16
            ssel[g, p, a - 8 * g * 0] = 0.0
    for g in range(8):
        for p in range(128):
            a_local = p // 16
            a_glob = 8 * g + a_local
            ssel[g, p, a_glob] = 1.0
            esel[g, a_glob, p] = 1.0
    G = np.zeros((128, 16), dtype=BF)
    for p in range(128):
        G[p, p % 16] = 1.0
    vsel = np.zeros((16, 128), dtype=BF)
    for p in range(128):
        vsel[p % 16, p] = 1.0 / 16.0
    c_ssel = nc.inline_tensor(ssel.reshape(8 * 128, 64), name="c_ssel")
    c_esel = nc.inline_tensor(esel.reshape(8 * 128, 128).view(np.uint16), name="c_esel")
    c_G = nc.inline_tensor(G.view(np.uint16), name="c_G")
    c_vsel = nc.inline_tensor(vsel.view(np.uint16), name="c_vsel")
    sb = nc.alloc_sbuf_tensor
    t = dict(
        ssel=sb("ssel", [128, 8, 64], F32),
        esel=sb("esel", [128, 8, 128], BF16),
        selG=sb("selG", [128, 16], BF16),
        vsel=sb("vsel", [16, 128], BF16),
    )
    pairs = [
        (t["ssel"][:, :, :],
         c_ssel.ap().rearrange("(a p) n -> p a n", p=128)),
        (t["esel"][:, :, :],
         c_esel.ap().bitcast(BF16).rearrange("(a p) n -> p a n", p=128)),
        (t["selG"][:], c_G.ap().bitcast(BF16)),
        (t["vsel"][:], c_vsel.ap().bitcast(BF16)),
    ]
    return t, pairs


def build_l1b(upto="f1"):
    nc = bass.Bass("TRN2", target_bir_lowering=False, debug=False)
    x = nc.dram_tensor("x", [128, 16 * CORE_TOK], F32, kind="ExternalInput").ap()
    q1o = nc.dram_tensor("q1o", [NH, 128, CORE_TOK], BF16, kind="ExternalInput").ap()
    k1g = nc.dram_tensor("k1g", [NH, 128, SEQ], BF16, kind="ExternalInput").ap()
    v1g = nc.dram_tensor("v1g", [NH, 128, SEQ], BF16, kind="ExternalInput").ap()
    qkng = nc.dram_tensor("qkng", [1, 4 * 2 * NH], F32, kind="ExternalInput").ap()
    wo_t = nc.dram_tensor("wo_t", [16, 128, 16 * 128], F32, kind="ExternalInput").ap()
    W_t = nc.dram_tensor("W_t", [16, 128, 16 * 128], F32, kind="ExternalInput").ap()
    A_t = nc.dram_tensor("A_t", [8, 128, 16 * 128], F32, kind="ExternalInput").ap()
    B_t = nc.dram_tensor("B_t", [16, 128, 8 * 128], F32, kind="ExternalInput").ap()
    bias_t = nc.dram_tensor("bias_t", [128, 16], F32, kind="ExternalInput").ap()
    x3o = nc.dram_tensor("x3o", [128, 16 * CORE_TOK], F32, kind="ExternalOutput").ap()
    x4o = nc.dram_tensor("x4o", [128, 16 * CORE_TOK], F32, kind="ExternalOutput").ap()

    C, cload, NC_ = _consts(nc)
    S1, spairs = _f1_consts(nc)
    sb = nc.alloc_sbuf_tensor
    wA = sb("wA", [128, 4, 16 * 128], BF16)
    rows = sb("rows", [1, 12, GT], F32)
    (mean_r, msq_r, es_r, var_r, sd_r, rstd_r, mr_r, den_r, sdq_r, rsq_r,
     nrm_r, spare_r) = [rows[0:1, i, :] for i in range(12)]
    tiny = sb("tiny", [1, 64], F32)
    negc16 = sb("negc16", [128, 16], F32)
    abc = sb("abc", [128, GT], F32)
    bbc = sb("bbc", [128, GT], F32)
    qb = sb("qb", [128, 2, GT], BF16)
    probs = sb("probs", [128, 2, GT], BF16)
    t1 = sb("t1", [128, GT], F32)
    t2 = sb("t2", [128, GT], F32)
    sqs = sb("sqs", [128, GT], BF16)
    rbc = sb("rbc", [128, GT], F32)
    xout = sb("xout", [128, 2, GT], F32)
    biasb = sb("biasb", [128, 16], F32)
    qkn_s = sb("qkn_s", [1, 128], F32)

    ARENA = 110 * 1024
    arena = sb("arena", [128, ARENA], mybir.dt.uint8)
    abase = nc.lookup_mloc(arena).addr

    def at(name, shape, dtype, off):
        return nc.alloc_sbuf_tensor_at(name, shape, dtype, offset=abase + off)

    kbuf = at("kbuf", [128, 2, 4096], BF16, 0)
    vbuf = at("vbuf", [128, 2, 4096], BF16, 16 * 1024)
    ao = at("ao", [128, 16, GT], BF16, 32 * 1024)
    yb = at("yb", [128, 16, GT], BF16, 48 * 1024)
    ysq = at("ysq", [128, GT], BF16, 64 * 1024)
    xres = at("xres", [128, 16, GT], F32, 66 * 1024)
    xb3 = at("xb3", [128, 16, GT], BF16, 0)
    xhat3 = at("xhat3", [128, 16, GT], BF16, 16 * 1024)
    vecs = at("vecs", [128, 8, GT], BF16, 32 * 1024)
    me = at("me", [128, 8, GT], BF16, 40 * 1024)
    arr_sb = at("arr_sb", [128, GT], F32, 48 * 1024)
    tb = at("tb", [128, 64], F32, 50 * 1024)
    routesT = at("routesT", [128, GT], BF16, 51 * 1024)
    mvt = at("mvt", [128, GT], BF16, 52 * 1024)
    vh_sb = at("vh_sb", [16, GT], BF16, 53 * 1024)
    yb0 = at("yb0", [128, 16, GT], BF16, 54 * 1024)
    ysq0 = at("ysq0", [128, GT], BF16, 70 * 1024)
    xres0f = at("xres0f", [128, 16, GT], F32, 71 * 1024)

    ps = nc.alloc_psum_tensor
    pA = [ps(f"pA{i}", [128, GT], F32) for i in range(4)]
    pB = [ps(f"pB{i}", [1, GT], F32) for i in range(2)]
    pC = ps("pC", [128, 4, 128], BF16)
    pD = ps("pD", [128, GT], F32)

    cx = Cx(nc)
    M = cx.M

    def ln_rows(cx, key):
        s, v, t = cx.s, cx.v, cx.t
        s.wait("t", M.get((key, "s12t")))
        s.mul(mean_r, pB[0][0:1, :], 1.0 / D)
        s.edge()
        s.activation(msq_r, mean_r, AF.Square)
        s.edge()
        s.mul(es_r, pB[1][0:1, :], 1.0 / D)
        s.inc()
        cx.mark((key, "es"), "s")
        v.wait("s", M.get((key, "es")))
        v.tensor_sub(var_r, es_r, msq_r)
        v.inc()
        cx.mark((key, "var"), "v")
        s.wait("v", M.get((key, "var")))
        s.activation(sd_r, var_r, AF.Sqrt, bias=1e-5)
        s.inc()
        cx.mark((key, "sd"), "s")
        v.wait("s", M.get((key, "sd")))
        v.reciprocal(rstd_r, sd_r)
        v.edge()
        v.tensor_mul(mr_r, mean_r, rstd_r)
        v.inc()
        cx.mark((key, "mr"), "v")
        t.wait("v", M.get((key, "mr")))
        t.matmul(pA[2][:], C["ocol"][:], rstd_r, start=True, stop=True)
        t.matmul(pA[3][:], C["ocol"][:], mr_r, start=True, stop=True)
        t.inc()
        cx.mark((key, "bc"), "t")
        v.wait("t", M.get((key, "bc")))
        v.tensor_copy(abc[:], pA[2][:])
        v.tensor_copy(bbc[:], pA[3][:])
        v.inc()
        cx.mark((key, "abv"), "v")
        v.wait("v", M.get((key, "abv")))

    NKT = 64

    def body(cx):
        g, sy, t, s, v = cx.g, cx.sy, cx.t, cx.s, cx.v
        mk = {}
        for dst, srcap in C["pairs"] + spairs:
            mk["c"] = cx.dma("g", "c", dst, srcap)
        cx.dma("g", "c", qkn_s[:], qkng)
        mk["c"] = cx.dma("g", "c", biasb[:], bias_t)

        v.wait_dma(mk["c"])
        v.tensor_max(tiny[0:1, 0:32], qkn_s[0:1, 0:32], qkn_s[0:1, 32:64])
        v.edge()
        v.tensor_max(tiny[0:1, 32:64], qkn_s[0:1, 64:96], qkn_s[0:1, 96:128])
        v.edge()
        v.tensor_max(tiny[0:1, 0:32], tiny[0:1, 0:32], tiny[0:1, 32:64])
        v.edge()
        v.tensor_mul(tiny[0:1, 32:48], tiny[0:1, 0:16], tiny[0:1, 16:32])
        v.inc()
        cx.mark("negv", "v")
        s.wait("v", M.get("negv"))
        s.activation(tiny[0:1, 48:64], tiny[0:1, 32:48], AF.Sqrt)
        s.edge()
        s.activation(tiny[0:1, 48:64], tiny[0:1, 48:64], AF.Copy,
                     scale=NEGC_SCALE / 128.0, bias=-0.01)
        s.inc()
        cx.mark("negs", "s")
        t.wait("s", M.get("negs"))
        t.matmul(pD[:, 0:16], C["ocol"][:], tiny[0:1, 48:64], start=True, stop=True)
        t.inc()
        cx.mark("negt", "t")
        v.wait("t", M.get("negt"))
        v.tensor_copy(negc16[:], pD[:, 0:16])
        v.inc()
        cx.mark("negbc", "v")

        for gi in range(NG):
            for h in range(NH):
                a = gi * NH + h
                ap_ = a % 2
                if a >= 2:
                    g.wait("t", M.get(("att_t_end", a - 2)))
                mk[("q", a)] = cx.dma("g", "k%d" % ap_, qb[:, ap_, :],
                                      q1o[h, :, gi * GT:(gi + 1) * GT])
                for hh in range(2):
                    sl = (2 * a + hh) % 2
                    if 2 * a + hh >= 2:
                        pa, phh = divmod(2 * a + hh - 2, 2)
                        g.wait("t", M.get(("c1h", pa, phh)))
                    cx.dma("g", "kv%d" % sl, kbuf[:, sl, :],
                           k1g[h, :, hh * 4096:(hh + 1) * 4096])
                    mk[("kv", a, hh)] = cx.dma("g", "kv%d" % sl, vbuf[:, sl, :],
                                               v1g[h, :, hh * 4096:(hh + 1) * 4096])
                t.wait_dma(mk[("q", a)])
                for i in range(NKT):
                    ip = i % 2
                    hh = i // 32
                    sl = (2 * a + hh) % 2
                    if i % 32 == 0:
                        t.wait_dma(mk[("kv", a, hh)])
                    if i >= 2:
                        t.wait("s", M.get(("exp", a, i - 2)))
                    elif a >= 1:
                        t.wait("s", M.get(("exp", a - 1, NKT - 2 + i)))
                    if i <= 1 and h == 0 and gi >= 1:
                        t.wait("s", M.get(("ye", gi - 1, 15)))
                    t.matmul(pA[ip][:], kbuf[:, sl, (i % 32) * 128:(i % 32 + 1) * 128],
                             qb[:, ap_, :], start=True, stop=True)
                    t.inc()
                    cx.mark(("sc", a, i), "t")
                    t.wait("s", M.get(("exp", a, i)))
                    if i == 0:
                        if a >= 1:
                            t.wait("v", M.get(("aov", a - 1)))
                        if h == 0 and gi >= 1:
                            t.wait("s", M.get(("opmean", gi - 1)))
                            t.wait("v", M.get((("op", gi - 1), "abv")))
                    t.matmul(pB[0][:], C["o128"][:], probs[:, ip, :],
                             start=(i == 0), stop=(i == NKT - 1),
                             skip_group_check=True)
                    t.matmul(pA[2][:], vbuf[:, sl, (i % 32) * 128:(i % 32 + 1) * 128],
                             probs[:, ip, :], start=(i == 0), stop=(i == NKT - 1),
                             skip_group_check=True)
                    t.inc()
                    cx.mark(("pv", a, i), "t")
                    if i % 32 == 31:
                        cx.mark(("c1h", a, hh), "t")
                t.wait("v", M.get(("rec", a)))
                t.matmul(pD[:], C["ocol"][:], den_r, start=True, stop=True)
                t.inc()
                cx.mark(("rbc", a), "t")
                cx.mark(("att_t_end", a), "t")
                if a == 0:
                    s.wait("v", M.get("negbc"))
                for i in range(NKT):
                    ip = i % 2
                    s.wait("t", M.get(("sc", a, i)))
                    if i >= 2:
                        s.wait("t", M.get(("pv", a, i - 2)))
                    elif a >= 1:
                        s.wait("t", M.get(("pv", a - 1, NKT - 2 + i)))
                    s.activation(probs[:, ip, :], pA[ip][:], AF.Exp,
                                 scale=SCALE, bias=negc16[:, h:h + 1])
                    s.inc()
                    cx.mark(("exp", a, i), "s")
                v.wait("t", M.get(("pv", a, NKT - 1)))
                v.reciprocal(den_r, pB[0][0:1, :])
                v.inc()
                cx.mark(("rec", a), "v")
                v.wait("t", M.get(("rbc", a)))
                if a >= 1:
                    v.wait("v", M.get(("aov", a - 1)))
                v.tensor_copy(rbc[:], pD[:])
                v.edge()
                v.tensor_mul(ao[:, h, :], pA[2][:], rbc[:])
                v.inc()
                cx.mark(("aov", a), "v")
            cx.mark(("attv_end", gi), "v")

            for m in range(16):
                mp = m % 2
                if m >= 2:
                    g.wait("t", M.get(("yt", gi, m - 2)))
                else:
                    g.wait("t", M.get(("att_t_end", gi * NH + NH - 1)))
                mk[("w", gi, m)] = cx.dma("g", "w%d" % mp, wA[:, mp, :], wo_t[m, :, :])
                mk[("xr", gi)] = cx.dma(
                    "g", "xr", xres[:, m, :],
                    x[:, m * CORE_TOK + gi * GT:m * CORE_TOK + (gi + 1) * GT])
                t.wait_dma(mk[("w", gi, m)])
                if m >= 2:
                    t.wait("s", M.get(("ye", gi, m - 2)))
                if m == 0:
                    t.wait("v", M.get(("aov", gi * NH + NH - 1)))
                for di in range(16):
                    t.matmul(pA[mp][:], wA[:, mp, di * 128:(di + 1) * 128],
                             ao[:, di, :], start=(di == 0), stop=(di == 15))
                t.inc()
                cx.mark(("yt", gi, m), "t")
                s.wait("t", M.get(("yt", gi, m)))
                s.copy(yb[:, m, :], pA[mp][:])
                s.edge()
                s.activation(ysq[:], yb[:, m, :], AF.Square)
                s.inc()
                cx.mark(("ye", gi, m), "s")
                t.wait("s", M.get(("ye", gi, m)))
                if m == 0 and gi >= 1:
                    t.wait("s", M.get(("opes", gi - 1)))
                t.matmul(pB[0][:], C["o128"][:], yb[:, m, :],
                         start=(m == 0), stop=(m == 15), skip_group_check=True)
                t.matmul(pB[1][:], C["o128"][:], ysq[:],
                         start=(m == 0), stop=(m == 15), skip_group_check=True)
                t.inc()
                if m == 15:
                    cx.mark((("op", gi), "s12t"), "t")
            ln_rows(cx, ("op", gi))
            if cx.mode == "plan":
                M[("opmean", gi)] = M[(("op", gi), "es")]
                M[("opes", gi)] = M[(("op", gi), "es")]
            for m in range(16):
                mp = m % 2
                if m == 0:
                    v.wait_dma(mk[("xr", gi)])
                    if gi >= 1:
                        v.wait("v", M.get(("resv", gi - 1, 15)))
                else:
                    v.wait("v", M.get(("resv", gi, m - 1)))
                if m >= 2:
                    v.wait_dma(mk.get(("st", gi, m - 2)))
                elif gi >= 1:
                    v.wait_dma(mk.get(("st", gi - 1, m + 14)))
                v.tensor_sub(t1[:], xres[:, m, :], bbc[:])
                v.tensor_mul(t2[:], yb[:, m, :], abc[:])
                v.edge()
                v.tensor_add(xout[:, mp, :], t1[:], t2[:])
                v.inc()
                cx.mark(("resv", gi, m), "v")
                sy.wait("v", M.get(("resv", gi, m)))
                mk[("st", gi, m)] = cx.dma(
                    "sy", "st%d" % mp,
                    x3o[:, m * CORE_TOK + gi * GT:m * CORE_TOK + (gi + 1) * GT],
                    xout[:, mp, :])

        def drain(slots):
            for sl in slots:
                if cx.mode != "plan" and sl in cx.dmarks:
                    sy.wait_dma((cx._dsems[sl], cx.dmarks[sl][-1]))

        if upto == "att":
            drain(("st0", "st1"))
            return

        for c in range(NG):
            g.wait("t", M.get(("att_t_end", 63)) if c == 0 else None)
            if c >= 1:
                g.wait("v", M.get(("f1resv", c - 1, 15)))
            g.wait_dma(mk.get(("st", c, 14)))
            g.wait_dma(mk.get(("st", c, 15)))
            mk[("xb3", c)] = cx.dma(
                "g", "xc", xb3[:, :, :],
                x3o[:, :].rearrange("p (a n) -> p a n", a=16)[:, :, c * GT:(c + 1) * GT])
            t.wait_dma(mk[("xb3", c)])
            if c == 0:
                t.wait("s", M.get((("op", 3), "es")))
            else:
                t.wait("s", M.get((("f1o", c - 1), "es")))
            for di in range(16):
                t.matmul(pB[0][:], C["o128"][:], xb3[:, di, :],
                         start=(di == 0), stop=(di == 15), skip_group_check=True)
            t.inc()
            cx.mark(("f1s1", c), "t")
            s.wait_dma(mk[("xb3", c)])
            if c >= 1:
                s.wait("t", M.get(("f1s2", c - 1)))
            for di in range(16):
                s.activation(xhat3[:, di, :], xb3[:, di, :], AF.Square)
            s.inc()
            cx.mark(("f1sq", c), "s")
            t.wait("s", M.get(("f1sq", c)))
            for di in range(16):
                t.matmul(pB[1][:], C["o128"][:], xhat3[:, di, :],
                         start=(di == 0), stop=(di == 15), skip_group_check=True)
            t.inc()
            cx.mark(("f1s2", c), "t")
            if cx.mode == "plan":
                M[(("f1i", c), "s12t")] = M[("f1s2", c)]
            ln_rows(cx, ("f1i", c))
            for di in range(16):
                if di == 0:
                    v.wait("t", M.get(("f1s2", c)))
                    if c == 0:
                        v.wait("v", M.get(("resv", 3, 15)))
                v.tensor_mul(xhat3[:, di, :], xb3[:, di, :], abc[:])
            v.edge()
            for di in range(16):
                v.tensor_sub(xhat3[:, di, :], xhat3[:, di, :], bbc[:])
            v.inc()
            cx.mark(("f1xh", c), "v")
            for ag in range(8):
                gp = ag % 2
                if ag >= 2:
                    g.wait("t", M.get(("f1vm", c, ag - 2)))
                elif c == 0:
                    g.wait("t", M.get(("yt", 3, 14 + ag)))
                else:
                    g.wait("t", M.get(("f1bm", c - 1, 14 + ag)))
                mk[("wa", c, ag)] = cx.dma("g", "w%d" % gp, wA[:, gp, :], A_t[ag, :, :])
                t.wait_dma(mk[("wa", c, ag)])
                if ag == 0:
                    t.wait("v", M.get(("f1xh", c)))
                if ag >= 2:
                    t.wait("v", M.get(("f1ve", c, ag - 2)))
                for di in range(16):
                    t.matmul(pA[gp][:], wA[:, gp, di * 128:(di + 1) * 128],
                             xhat3[:, di, :], start=(di == 0), stop=(di == 15))
                t.inc()
                cx.mark(("f1vm", c, ag), "t")
                v.wait("t", M.get(("f1vm", c, ag)))
                v.tensor_copy(vecs[:, ag, :], pA[gp][:])
                v.inc()
                cx.mark(("f1ve", c, ag), "v")
            for ag in range(8):
                s.wait("v", M.get(("f1ve", c, ag)))
                if ag >= 1:
                    s.wait("t", M.get(("f1am", c, ag - 1)))
                s.activation(t1[:], vecs[:, ag, :], AF.Square)
                s.inc()
                cx.mark(("f1as", c, ag), "s")
                t.wait("s", M.get(("f1as", c, ag)))
                if ag == 0:
                    t.wait("s", M.get((("f1i", c), "es")))
                    t.wait("v", M.get((("f1i", c), "abv")))
                t.matmul(pA[2][0:64, :], S1["ssel"][:, ag, :], t1[:],
                         start=(ag == 0), stop=(ag == 7), skip_group_check=True)
                t.inc()
                cx.mark(("f1am", c, ag), "t")
            v.wait("t", M.get(("f1am", c, 7)))
            v.tensor_copy(arr_sb[0:64, :], pA[2][0:64, :])
            v.inc()
            cx.mark(("f1ac", c), "v")
            t.wait("v", M.get(("f1ac", c)))
            if c >= 1:
                t.wait("v", M.get(("f1t4", c - 1, 3)))
            for tt in range(4):
                t.transpose(pD[:, tt * 128:(tt + 1) * 128],
                            arr_sb[:, tt * 128:(tt + 1) * 128], C["idf32"][:])
            t.inc()
            cx.mark(("f1tr", c), "t")
            for tt in range(4):
                v.wait("t", M.get(("f1tr", c)))
                v.tensor_copy(tb[:], pD[:, tt * 128:tt * 128 + 64])
                v.edge()
                for rnd in range(3):
                    v.reduce_max(t2[:, 0:1], tb[:], axis=AX)
                    v.edge()
                    v.tensor_single_scalar(mvt[:, 0:64], tb[:], t2[:, 0:1], ALU.is_ge)
                    v.edge()
                    v.scalar_tensor_tensor(tb[:], mvt[:, 0:64], -1e30, tb[:],
                                           ALU.mult, ALU.add)
                    v.edge()
                v.reduce_max(t2[:, 0:1], tb[:], axis=AX)
                v.edge()
                v.tensor_single_scalar(mvt[:, 0:64], pD[:, tt * 128:tt * 128 + 64],
                                       t2[:, 0:1], ALU.is_ge)
                v.inc()
                cx.mark(("f1t4", c, tt), "v")
                t.wait("v", M.get(("f1t4", c, tt)))
                if tt == 0 and c >= 1:
                    t.wait("v", M.get(("f1rtc", c - 1, 3)))
                t.transpose(pC[:, tt, :][0:64, :], mvt[:, 0:64], C["id128"][:])
                t.inc()
                cx.mark(("f1rt", c, tt), "t")
                v.wait("t", M.get(("f1rt", c, tt)))
                if tt == 0:
                    if c >= 1:
                        v.wait("t", M.get(("f1mem", c - 1, 7)))
                    v.memset(routesT[:], 0.0)
                    v.edge()
                v.tensor_copy(routesT[0:64, tt * 128:(tt + 1) * 128], pC[:, tt, :][0:64, :])
                v.inc()
                cx.mark(("f1rtc", c, tt), "v")
            for ag in range(8):
                t.wait("v", M.get(("f1rtc", c, 3)))
                if ag == 0:
                    if c >= 1:
                        t.wait("v", M.get(("f1u", c - 1, 7)))
                else:
                    t.wait("v", M.get(("f1me", c, ag - 1)))
                t.matmul(pA[2][:], S1["esel"][:, ag, :], routesT[:, :],
                         start=True, stop=True)
                t.inc()
                cx.mark(("f1mem", c, ag), "t")
                v.wait("t", M.get(("f1mem", c, ag)))
                v.tensor_copy(me[:, ag, :], pA[2][:])
                v.edge()
                v.tensor_mul(mvt[:], vecs[:, ag, :], me[:, ag, :])
                v.inc()
                cx.mark(("f1me", c, ag), "v")
                t.wait("v", M.get(("f1me", c, ag)))
                if ag == 0 and c >= 1:
                    t.wait("v", M.get(("f1vh", c - 1)))
                t.matmul(pA[3][0:16, :], S1["selG"][:], mvt[:],
                         start=(ag == 0), stop=(ag == 7), skip_group_check=True)
                t.inc()
                cx.mark(("f1gm", c, ag), "t")
            v.wait("t", M.get(("f1gm", c, 7)))
            v.tensor_copy(vh_sb[:], pA[3][0:16, :])
            v.inc()
            cx.mark(("f1vh", c), "v")
            t.wait("v", M.get(("f1vh", c)))
            t.matmul(pA[2][:], S1["vsel"][:], vh_sb[:], start=True, stop=True)
            t.inc()
            cx.mark(("f1vr", c), "t")
            for ag in range(8):
                v.wait("t", M.get(("f1vr", c)))
                if ag >= 1:
                    v.wait("v", M.get(("f1u", c, ag - 1)))
                v.tensor_mul(me[:, ag, :], me[:, ag, :], pA[2][:])
                v.inc()
                cx.mark(("f1u", c, ag), "v")
            for j in range(16):
                jp = j % 2
                if j >= 2:
                    g.wait("t", M.get(("f1bm", c, j - 2)))
                else:
                    g.wait("t", M.get(("f1vm", c, 6 + j)))
                cx.dma("g", "w%d" % jp, wA[:, jp, :], W_t[j, :, :])
                mk[("wb1", c, j)] = cx.dma("g", "w%d" % jp,
                                           wA[:, 2 + jp, 0:1024], B_t[j, :, :])
                t.wait_dma(mk[("wb1", c, j)])
                if j == 0:
                    t.wait("v", M.get(("f1u", c, 7)))
                if j >= 2:
                    t.wait("s", M.get(("f1ye", c, j - 2)))
                for di in range(16):
                    t.matmul(pA[jp][:], wA[:, jp, di * 128:(di + 1) * 128],
                             xhat3[:, di, :], start=(di == 0), stop=False)
                for ag in range(8):
                    t.matmul(pA[jp][:], wA[:, 2 + jp, ag * 128:(ag + 1) * 128],
                             me[:, ag, :], start=False, stop=(ag == 7))
                t.inc()
                cx.mark(("f1bm", c, j), "t")
                s.wait("t", M.get(("f1bm", c, j)))
                s.add(yb0[:, j, :], pA[jp][:], biasb[:, j:j + 1])
                s.edge()
                s.activation(ysq0[:], yb0[:, j, :], AF.Square)
                s.inc()
                cx.mark(("f1ye", c, j), "s")
                t.wait("s", M.get(("f1ye", c, j)))
                if j == 0:
                    t.wait("s", M.get((("f1i", c), "es")))
                t.matmul(pB[0][:], C["o128"][:], yb0[:, j, :],
                         start=(j == 0), stop=(j == 15), skip_group_check=True)
                t.matmul(pB[1][:], C["o128"][:], ysq0[:],
                         start=(j == 0), stop=(j == 15), skip_group_check=True)
                t.inc()
                if j == 15:
                    cx.mark((("f1o", c), "s12t"), "t")
            ln_rows(cx, ("f1o", c))
            g.wait("t", M.get(("f1bm", c, 15)))
            if c == 0:
                g.wait("v", M.get(("resv", 3, 15)))
            else:
                g.wait("v", M.get(("f1resv", c - 1, 15)))
            for m in range(16):
                mk[("xr0", c)] = cx.dma(
                    "g", "xr0", xres0f[:, m, :],
                    x3o[:, m * CORE_TOK + c * GT:m * CORE_TOK + (c + 1) * GT])
            for m in range(16):
                mp = m % 2
                if m == 0:
                    v.wait_dma(mk[("xr0", c)])
                    if c >= 1:
                        v.wait("v", M.get(("f1resv", c - 1, 15)))
                    else:
                        v.wait("v", M.get(("resv", 3, 15)))
                else:
                    v.wait("v", M.get(("f1resv", c, m - 1)))
                if m >= 2:
                    v.wait_dma(mk.get(("stx4", c, m - 2)))
                elif c >= 1:
                    v.wait_dma(mk.get(("stx4", c - 1, m + 14)))
                else:
                    v.wait_dma(mk.get(("st", 3, m + 14)))
                v.tensor_sub(t1[:], xres0f[:, m, :], bbc[:])
                v.tensor_mul(t2[:], yb0[:, m, :], abc[:])
                v.edge()
                v.tensor_add(xout[:, mp, :], t1[:], t2[:])
                v.inc()
                cx.mark(("f1resv", c, m), "v")
                sy.wait("v", M.get(("f1resv", c, m)))
                mk[("stx4", c, m)] = cx.dma(
                    "sy", "st%d" % mp,
                    x4o[:, m * CORE_TOK + c * GT:m * CORE_TOK + (c + 1) * GT],
                    xout[:, mp, :])
        drain(("st0", "st1"))
    with nc.Block() as block:
        cx.run(block, body)
    return nc


def _prep_l1b_inputs(inputs, x2_cores, resB):
    wo_t = retile_w(np.ascontiguousarray(
        np.asarray(inputs["L1_attn_wo"], dtype=np.float32).T))
    W_t = retile_w(np.ascontiguousarray(
        np.asarray(inputs["L1_ffn_W"], dtype=np.float32).T))
    A = np.asarray(inputs["L1_ffn_A"], dtype=np.float32)
    A_t = retile_w(np.ascontiguousarray(A.reshape(1024, 2048).T))
    B = np.asarray(inputs["L1_ffn_B"], dtype=np.float32)
    B_flat = np.ascontiguousarray(B.transpose(0, 2, 1).reshape(1024, 2048))
    B_t = retile_w(B_flat, din=1024)
    bias_t = np.ascontiguousarray(
        np.asarray(inputs["L1_ffn_bias"], dtype=np.float32).reshape(16, 128).T)
    owner = {}
    for rr in range(4):
        for li, bb in enumerate(core_blocks(rr)):
            owner[bb] = (rr, li)
    maps = []
    for c in range(8):
        b, r = c // 4, c % 4
        k1gf = np.zeros((NH, 128, SEQ), dtype=BF)
        v1gf = np.zeros((NH, 128, SEQ), dtype=BF)
        for bb in range(64):
            rb, li = owner[bb]
            src = resB[b * 4 + rb]
            k1gf[:, :, bb * 128:(bb + 1) * 128] = src["k1"][:, :, li * 128:(li + 1) * 128]
            v1gf[:, :, bb * 128:(bb + 1) * 128] = src["v1"][:, li * 128:(li + 1) * 128, :]
        qkng = np.concatenate(
            [np.asarray(resB[b * 4 + rr]["qkn"], dtype=np.float32).reshape(1, 32)
             for rr in range(4)], axis=1)
        maps.append(dict(
            x=x2_cores[c], q1o=resB[c]["q1"], k1g=k1gf, v1g=v1gf,
            qkng=np.ascontiguousarray(qkng),
            wo_t=wo_t, W_t=W_t, A_t=A_t, B_t=B_t, bias_t=bias_t))
    return maps


# version 02
# speedup vs baseline: 1.3535x; 1.3535x over previous
import numpy as np
import ml_dtypes

import concourse.bass as bass
import concourse.mybir as mybir
from concourse.bass_utils import run_bass_kernel_spmd

F32 = mybir.dt.float32
BF16 = mybir.dt.bfloat16
AF = mybir.ActivationFunctionType
ALU = mybir.AluOpType
AX = mybir.AxisListType.X
BF = ml_dtypes.bfloat16

D = 2048
NH = 16
HD = 128
SEQ = 8192
WINDOW = 4096
CORE_TOK = 2048
NG = 4
GT = 512
KSPAN = 36

RUNS = [[0, 7, 8, 9], [1, 6, 10, 11], [2, 5, 12, 13], [3, 4, 14, 15]]
SCALE = float(1.0 / np.sqrt(HD))


def core_blocks(r):
    return [4 * k + j for k in RUNS[r] for j in range(4)]


def core_positions(r):
    return np.concatenate([np.arange(b * 128, b * 128 + 128) for b in core_blocks(r)])


_INVF = 10000.0 ** (-(np.arange(64, dtype=np.float64) * 2.0) / 128.0)


def rope_tables(positions):
    t = np.asarray(positions, dtype=np.float64)
    ang = _INVF[:, None] * t[None, :]
    c, s = np.cos(ang), np.sin(ang)
    cos0 = np.concatenate([c, c], 0).astype(np.float32)
    sin0 = np.concatenate([s, s], 0).astype(np.float32)
    cos1 = np.repeat(c, 2, axis=0).astype(np.float32)
    sin1 = np.repeat(s, 2, axis=0).astype(np.float32)
    return cos0, sin0, cos1, sin1


def rot_mat_l0():
    P = np.zeros((128, 128), dtype=BF)
    for i in range(64):
        P[i + 64, i] = -1.0
        P[i, i + 64] = 1.0
    return P


def rot_mat_l1():
    P = np.zeros((128, 128), dtype=BF)
    for j in range(64):
        P[2 * j + 1, 2 * j] = -1.0
        P[2 * j, 2 * j + 1] = 1.0
    return P


def to_fm(x_tok):
    T = x_tok.shape[0]
    return np.ascontiguousarray(x_tok.reshape(T, D // 128, 128).transpose(1, 2, 0))


def from_fm(x_fm):
    return np.ascontiguousarray(x_fm.transpose(2, 0, 1).reshape(x_fm.shape[2], D))


FVALS = [1e-5, 1e-6, 1.0 / 2048.0, 1.0 / 128.0, -1.0, 1.0, 0.0625,
         -1e30, 1e30, 0.5, 2.0, -0.5, -0.01, 1.0 / 8192.0]


def _consts(nc):
    c_P0 = nc.inline_tensor(rot_mat_l0().view(np.uint16), name="c_P0")
    c_P1 = nc.inline_tensor(rot_mat_l1().view(np.uint16), name="c_P1")
    c_id = nc.inline_tensor(np.eye(128, dtype=BF).view(np.uint16), name="c_id")
    c_o128 = nc.inline_tensor(np.ones((128, 1), dtype=BF).view(np.uint16), name="c_o128")
    c_ocol = nc.inline_tensor(np.ones((1, 128), dtype=np.float32), name="c_ocol")
    c_idf = nc.inline_tensor(np.eye(128, dtype=np.float32), name="c_idf")
    c_fp = nc.inline_tensor(np.array([FVALS], dtype=np.float32), name="c_fp")
    sb = nc.alloc_sbuf_tensor
    t = dict(
        P0=sb("P0", [128, 128], BF16),
        P1=sb("P1", [128, 128], BF16),
        id128=sb("id128", [128, 128], BF16),
        o128=sb("o128", [128, 1], BF16),
        ocol=sb("ocol", [1, 128], F32),
        idf32=sb("idf32", [128, 128], F32),
        fpool=sb("fpool", [1, len(FVALS)], F32),
    )
    for i, v in enumerate(FVALS):
        nc.const_aps.register(F32, float(v), t["fpool"][0:1, i:i + 1])

    t["pairs"] = [
        (t["P0"][:], c_P0.ap().bitcast(BF16)),
        (t["P1"][:], c_P1.ap().bitcast(BF16)),
        (t["id128"][:], c_id.ap().bitcast(BF16)),
        (t["o128"][:], c_o128.ap().bitcast(BF16)),
        (t["ocol"][:], c_ocol.ap()),
        (t["idf32"][:], c_idf.ap()),
        (t["fpool"][:], c_fp.ap()),
    ]

    def load(g, sem):
        for dst, srcap in t["pairs"]:
            g.dma_start(out=dst, in_=srcap).then_inc(sem, 16)

    return t, load, 6


def _dma_rec(nc, sem, marks):
    def dma(eng, out, in_):
        n0 = nc.n_instructions()
        eng.dma_start(out=out, in_=in_).then_inc(sem, 16)
        marks.append(marks[-1] + 16 * (nc.n_instructions() - n0))
        return marks[-1]
    return dma


def build_l0a():
    nc = bass.Bass("TRN2", target_bir_lowering=False, debug=False)
    x = nc.dram_tensor("x", [128, 16 * CORE_TOK], F32, kind="ExternalInput").ap()
    wqkv = nc.dram_tensor("wqkv", [48, 128, 16 * 128], F32, kind="ExternalInput").ap()
    qg = nc.dram_tensor("qg", [1, 128], F32, kind="ExternalInput").ap()
    kg = nc.dram_tensor("kg", [1, 128], F32, kind="ExternalInput").ap()
    cos0 = nc.dram_tensor("cos0", [128, CORE_TOK], F32, kind="ExternalInput").ap()
    sin0 = nc.dram_tensor("sin0", [128, CORE_TOK], F32, kind="ExternalInput").ap()
    q0 = nc.dram_tensor("q0", [NH, 128, CORE_TOK], BF16, kind="ExternalOutput").ap()
    k0 = nc.dram_tensor("k0", [NH, 128, CORE_TOK], BF16, kind="ExternalOutput").ap()
    v0 = nc.dram_tensor("v0", [NH, CORE_TOK, 128], BF16, kind="ExternalOutput").ap()

    C, cload, NC_ = _consts(nc)
    sb = nc.alloc_sbuf_tensor
    xb = sb("xb", [128, 16, CORE_TOK], BF16)
    sqb = sb("sqb", [128, 16, GT], BF16)
    xhat = sb("xhat", [128, 16, CORE_TOK], BF16)
    gq = sb("gq", [1, 128], F32)
    gk = sb("gk", [1, 128], F32)
    coss = sb("coss", [128, CORE_TOK], F32)
    sins = sb("sins", [128, CORE_TOK], F32)
    wt = sb("wt", [128, 2, 16, 128], BF16)
    mean_r = sb("mean_r", [1, GT], F32)
    msq_r = sb("msq_r", [1, GT], F32)
    es_r = sb("es_r", [1, GT], F32)
    var_r = sb("var_r", [1, GT], F32)
    sd_r = sb("sd_r", [1, GT], F32)
    rstd_r = sb("rstd_r", [1, GT], F32)
    mr_r = sb("mr_r", [1, GT], F32)
    abc = sb("abc", [128, GT], F32)
    bbc = sb("bbc", [128, GT], F32)
    sqs = sb("sqs", [128, GT], BF16)
    sdq_r = sb("sdq_r", [1, GT], F32)
    rsq_r = sb("rsq_r", [1, GT], F32)
    grs = sb("grs", [128, GT], F32)
    qn = sb("qn", [128, GT], BF16)
    t1 = sb("t1", [128, GT], F32)
    t2 = sb("t2", [128, GT], F32)
    qout = sb("qout", [128, 2, GT], BF16)
    vout = sb("vout", [128, 2, GT], BF16)

    ps = nc.alloc_psum_tensor
    p_mm = [ps("p_mm0", [128, GT], F32), ps("p_mm1", [128, GT], F32)]
    p_s1 = ps("p_s1", [1, GT], F32)
    p_s2 = ps("p_s2", [1, GT], F32)
    p_var = ps("p_var", [1, GT], F32)
    p_gb = ps("p_gb", [128, GT], F32)
    p_rot = ps("p_rot", [128, GT], F32)
    p_vt = ps("p_vt", [128, 4, 128], BF16)

    sem = nc.alloc_semaphore
    s_ldc = sem("s_ldc")
    s_ldx = sem("s_ldx")
    s_ldw = [sem("s_ldw0"), sem("s_ldw1")]
    s_st = [sem("s_st0"), sem("s_st1")]
    s_tl = sem("s_tl")
    s_sl = sem("s_sl")
    s_vl = sem("s_vl")
    s_tq = sem("s_tq")
    s_vch = sem("s_vch")
    s_sch = sem("s_sch")
    s_sq = sem("s_sq")
    s_vq = sem("s_vq")

    NLDC = NC_ + 4
    ctr = {"vch": 0, "sch": 0}

    def vedge(v, inst):
        ctr["vch"] += 1
        inst.then_inc(s_vch, 1)
        v.wait_ge(s_vch, ctr["vch"])

    def sedge(s, inst):
        ctr["sch"] += 1
        inst.then_inc(s_sch, 1)
        s.wait_ge(s_sch, ctr["sch"])

    def te_mm(i):
        return 4 * i + 1 if i < 128 else 512 + 2 * (i - 128) + 1

    def te_tp(i):
        return 512 + 2 * (i - 128) + 2

    def vq_qn(i):
        return 3 * i + 2 if i < 128 else 384 + 2 * (i - 128) + 1

    def vq_out(i):
        return 3 * i + 3 if i < 128 else 384 + 2 * (i - 128) + 2

    with nc.Block() as block:
        m_ldc, m_ldx = [0], [0]
        m_ldw = [[0], [0]]
        d_ldc = _dma_rec(nc, s_ldc, m_ldc)
        d_ldx = _dma_rec(nc, s_ldx, m_ldx)
        d_ldw = [_dma_rec(nc, s_ldw[0], m_ldw[0]), _dma_rec(nc, s_ldw[1], m_ldw[1])]

        def g_all(g):
            cload(g, s_ldc)
            m_ldc.append(m_ldc[-1] + NC_ * 16)
            d_ldc(g, gq[:], qg)
            d_ldc(g, gk[:], kg)
            d_ldc(g, coss[:], cos0)
            d_ldc(g, sins[:], sin0)
            d_ldx(g, xb[:, :, :].rearrange("p a n -> p (a n)"), x[:, :])
            for m in range(48):
                if m >= 2:
                    g.wait_ge(s_tq, te_mm((m - 2) * 4 + 3))
                d_ldw[m % 2](g, wt[:, m % 2, :, :].rearrange("p a n -> p (a n)"),
                             wqkv[m, :, :])
        block.gpsimd(g_all)

        def t_ln(t):
            t.wait_ge(s_ldc, m_ldc[-1])
            t.wait_ge(s_ldx, m_ldx[-1])
            for g in range(NG):
                if g >= 1:
                    t.wait_ge(s_sl, 4 * (g - 1) + 2)
                for di in range(16):
                    inst = t.matmul(p_s1[:], C["o128"][:],
                                    xb[:, di, g * GT:(g + 1) * GT],
                                    start=(di == 0), stop=(di == 15))
                inst.then_inc(s_tl, 1)
                t.wait_ge(s_sl, 4 * g + 1)
                for di in range(16):
                    inst = t.matmul(p_s2[:], C["o128"][:], sqb[:, di, :],
                                    start=(di == 0), stop=(di == 15))
                inst.then_inc(s_tl, 1)
                t.wait_ge(s_vl, 3 * g + 2)
                t.matmul(p_gb[:], C["ocol"][:], rstd_r[:], start=True, stop=True)
                t.matmul(p_rot[:], C["ocol"][:], mr_r[:], start=True,
                         stop=True).then_inc(s_tl, 1)
        block.tensor(t_ln)

        def s_ln(s):
            s.wait_ge(s_ldx, m_ldx[-1])
            for g in range(NG):
                if g >= 1:
                    s.wait_ge(s_tl, 3 * (g - 1) + 2)
                for di in range(16):
                    inst = s.activation(sqb[:, di, :], xb[:, di, g * GT:(g + 1) * GT],
                                        AF.Square)
                inst.then_inc(s_sl, 1)
                s.wait_ge(s_tl, 3 * g + 1)
                sedge(s, s.mul(mean_r[:], p_s1[0:1, :], 1.0 / D))
                s.activation(msq_r[:], mean_r[:], AF.Square).then_inc(s_sl, 1)
                s.wait_ge(s_tl, 3 * g + 2)
                s.mul(es_r[:], p_s2[0:1, :], 1.0 / D).then_inc(s_sl, 1)
                s.wait_ge(s_vl, 3 * g + 1)
                s.activation(sd_r[:], var_r[:], AF.Sqrt, bias=1e-5).then_inc(s_sl, 1)
        block.scalar(s_ln)

        def v_ln(v):
            for g in range(NG):
                v.wait_ge(s_sl, 4 * g + 3)
                v.tensor_sub(var_r[:], es_r[:], msq_r[:]).then_inc(s_vl, 1)
                v.wait_ge(s_sl, 4 * g + 4)
                vedge(v, v.reciprocal(rstd_r[:], sd_r[:]))
                v.tensor_mul(mr_r[:], mean_r[:], rstd_r[:]).then_inc(s_vl, 1)
                v.wait_ge(s_tl, 3 * g + 3)
                v.tensor_copy(abc[:], p_gb[:])
                vedge(v, v.tensor_copy(bbc[:], p_rot[:]))
                for di in range(16):
                    inst = v.tensor_mul(xhat[:, di, g * GT:(g + 1) * GT],
                                        xb[:, di, g * GT:(g + 1) * GT], abc[:])
                vedge(v, inst)
                for di in range(16):
                    inst = v.tensor_sub(xhat[:, di, g * GT:(g + 1) * GT],
                                        xhat[:, di, g * GT:(g + 1) * GT], bbc[:])
                inst.then_inc(s_vl, 1)
        block.vector(v_ln)

        def t_qk(t):
            t.wait_ge(s_vl, 3 * NG)
            for m in range(48):
                t.wait_ge(s_ldw[m % 2], m_ldw[m % 2][m // 2 + 1])
                isv = m >= 32
                for tc in range(NG):
                    i = m * 4 + tc
                    par = i % 2
                    if i >= 2:
                        t.wait_ge(s_vq, vq_qn(i - 2))
                    for di in range(16):
                        inst = t.matmul(p_mm[par][:], wt[:, m % 2, di, :],
                                        xhat[:, di, tc * GT:(tc + 1) * GT],
                                        start=(di == 0), stop=(di == 15))
                    inst.then_inc(s_tq, 1)
                    if not isv:
                        t.wait_ge(s_sq, 2 * i + 1)
                        t.matmul(p_var[:], C["o128"][:], sqs[:], start=True,
                                 stop=True).then_inc(s_tq, 1)
                        t.wait_ge(s_vq, 3 * i + 1)
                        gam = gq if m < 16 else gk
                        t.matmul(p_gb[:], gam[:], rsq_r[:], start=True,
                                 stop=True).then_inc(s_tq, 1)
                        t.wait_ge(s_vq, 3 * i + 2)
                        t.matmul(p_rot[:], C["P0"][:], qn[:], start=True,
                                 stop=True).then_inc(s_tq, 1)
                    else:
                        t.wait_ge(s_vq, vq_qn(i))
                        for blk in range(4):
                            inst = t.transpose(p_vt[:, blk, :],
                                               qn[:, blk * 128:(blk + 1) * 128],
                                               C["id128"][:])
                        inst.then_inc(s_tq, 1)
        block.tensor(t_qk)

        def s_qk(s):
            for m in range(32):
                for tc in range(NG):
                    i = m * 4 + tc
                    par = i % 2
                    s.wait_ge(s_tq, 4 * i + 1)
                    if i >= 1:
                        s.wait_ge(s_tq, 4 * (i - 1) + 2)
                    s.activation(sqs[:], p_mm[par][:], AF.Square).then_inc(s_sq, 1)
                    s.wait_ge(s_tq, 4 * i + 2)
                    if i >= 1:
                        s.wait_ge(s_vq, 3 * (i - 1) + 1)
                    s.activation(sdq_r[:], p_var[0:1, :], AF.Sqrt, scale=1.0 / HD,
                                 bias=1e-6).then_inc(s_sq, 1)
        block.scalar(s_qk)

        m_st = [[0], [0]]
        d_st = [_dma_rec(nc, s_st[0], m_st[0]), _dma_rec(nc, s_st[1], m_st[1])]

        def sy_qk(sy):
            for m in range(48):
                for tc in range(NG):
                    i = m * 4 + tc
                    par = i % 2
                    sy.wait_ge(s_vq, vq_out(i))
                    if m < 16:
                        d_st[par](sy, q0[m, :, tc * GT:(tc + 1) * GT], qout[:, par, :])
                    elif m < 32:
                        d_st[par](sy, k0[m - 16, :, tc * GT:(tc + 1) * GT], qout[:, par, :])
                    else:
                        d_st[par](sy, v0[m - 32, tc * GT:(tc + 1) * GT, :]
                                  .rearrange("(b p) d -> p b d", p=128),
                                  vout[:, par, :].rearrange("p (b d) -> p b d", b=4))
            sy.wait_ge(s_st[0], m_st[0][-1])
            sy.wait_ge(s_st[1], m_st[1][-1])
        block.sync(sy_qk)

        def v_qk(v):
            for m in range(48):
                isv = m >= 32
                for tc in range(NG):
                    i = m * 4 + tc
                    par = i % 2
                    if not isv:
                        v.wait_ge(s_sq, 2 * i + 2)
                        v.reciprocal(rsq_r[:], sdq_r[:]).then_inc(s_vq, 1)
                        v.wait_ge(s_tq, 4 * i + 3)
                        vedge(v, v.tensor_copy(grs[:], p_gb[:]))
                        v.tensor_mul(qn[:], p_mm[par][:], grs[:]).then_inc(s_vq, 1)
                        v.wait_ge(s_tq, 4 * i + 4)
                        v.tensor_mul(t1[:], qn[:], coss[:, tc * GT:(tc + 1) * GT])
                        vedge(v, v.tensor_mul(t2[:], p_rot[:], sins[:, tc * GT:(tc + 1) * GT]))
                        if i >= 2:
                            v.wait_ge(s_st[par], m_st[par][i // 2])
                        v.tensor_add(qout[:, par, :], t1[:], t2[:]).then_inc(s_vq, 1)
                    else:
                        v.wait_ge(s_tq, te_mm(i))
                        v.tensor_copy(qn[:], p_mm[par][:]).then_inc(s_vq, 1)
                        v.wait_ge(s_tq, te_tp(i))
                        if i >= 2:
                            v.wait_ge(s_st[par], m_st[par][i // 2])
                        v.tensor_copy(vout[:, par, :].rearrange("p (b n) -> p b n", b=4),
                                      p_vt[:, :, :]).then_inc(s_vq, 1)
        block.vector(v_qk)


    return nc


def hm_layout(x_tok):
    T = x_tok.shape[0]
    return np.ascontiguousarray(
        x_tok.reshape(T, 16, 128).transpose(2, 1, 0).reshape(128, 16 * T))


def retile_w(w, din=D):
    nm, nd = w.shape[1] // 128, din // 128
    return np.ascontiguousarray(
        w.reshape(nd, 128, nm, 128).transpose(2, 1, 0, 3).reshape(nm, 128, nd * 128))


def _prep_l0a_inputs(inputs):
    x_full = np.asarray(inputs["L0_attn_x"], dtype=np.float32)
    wqkv = retile_w(np.asarray(inputs["L0_attn_w_qkv"], dtype=np.float32))
    qg = np.asarray(inputs["L0_attn_q_gamma"], dtype=np.float32).reshape(1, 128)
    kg = np.asarray(inputs["L0_attn_k_gamma"], dtype=np.float32).reshape(1, 128)
    maps = []
    for c in range(8):
        b, r = c // 4, c % 4
        pos = core_positions(r)
        cos0, sin0, _, _ = rope_tables(pos)
        x_slice = x_full[b][pos]
        maps.append(dict(x=hm_layout(x_slice), wqkv=wqkv, qg=qg, kg=kg,
                         cos0=np.ascontiguousarray(cos0),
                         sin0=np.ascontiguousarray(sin0)))
    return maps


def run_l0a(inputs):
    nc = build_l0a()
    maps = _prep_l0a_inputs(inputs)
    res = run_bass_kernel_spmd(nc, maps, core_ids=list(range(8)))
    return res.results


def kernel(**inputs):
    ncA = build_l0a()
    resA = run_bass_kernel_spmd(ncA, _prep_l0a_inputs(inputs),
                                core_ids=list(range(8))).results
    ncB = build_l0b()
    resB = run_bass_kernel_spmd(ncB, _prep_l0b_inputs(inputs, resA),
                                core_ids=list(range(8))).results
    x2_cores = [np.asarray(resB[c]["x2o"]) for c in range(8)]
    ncC = build_l1b()
    resC = run_bass_kernel_spmd(ncC, _prep_l1b_inputs(inputs, x2_cores, resB),
                                core_ids=list(range(8))).results
    out = np.zeros((2, SEQ, D), dtype=np.float32)
    for c in range(8):
        b, r = c // 4, c % 4
        pos = core_positions(r)
        x4 = np.asarray(resC[c]["x4o"], dtype=np.float32)
        out[b][pos] = from_fm(x4.reshape(128, 16, CORE_TOK).transpose(1, 0, 2))
    return out



class _EngProxy:

    def __init__(self, cx, key):
        self._cx = cx
        self._key = key

    def live(self):
        return self._cx.mode == self._key

    def __getattr__(self, name):
        cx, key = self._cx, self._key

        def f(*a, **k):
            if cx.mode == key:
                cx.last[key] = getattr(cx.real, name)(*a, **k)
            return None

        return f

    def wait(self, sem_key, val):
        if val is None or val <= 0:
            return
        if self.live():
            self._cx.real.wait_ge(self._cx.sems[sem_key], val)

    def wait_dma(self, mark):
        if mark is None:
            assert self._cx.mode == "plan" or not self.live(), \
                "wait_dma on a mark from a not-yet-emitted engine"
            return
        if self.live():
            sem, val = mark
            if val > 0:
                self._cx.real.wait_ge(sem, val)

    def inc(self):
        cx, key = self._cx, self._key
        cx.ev[key] += 1
        if cx.mode == key:
            cx.last[key].then_inc(cx.sems[key], 1)
        return cx.ev[key]

    def edge(self):
        cx, key = self._cx, self._key
        ck = key + "ch"
        cx.ev[ck] = cx.ev.get(ck, 0) + 1
        if cx.mode == key:
            cx.last[key].then_inc(cx.sems[ck], 1)
            cx.real.wait_ge(cx.sems[ck], cx.ev[ck])


class Cx:

    ENGS = ("g", "sy", "t", "s", "v")

    def __init__(self, nc):
        self.nc = nc
        self.M = {}
        self.dmarks = {}
        self.mode = "plan"
        self.real = None
        self.last = {}
        self.ev = {k: 0 for k in self.ENGS}
        self.sems = {}
        for k in list(self.ENGS) + ["tch", "sch", "vch"]:
            self.sems[k] = nc.alloc_semaphore("se_" + k)
        self._dsems = {}
        for e in self.ENGS:
            setattr(self, e, _EngProxy(self, e))

    def begin(self, mode, real=None):
        self.mode = mode
        self.real = real
        self.ev = {k: 0 for k in self.ENGS}
        for k in ("tch", "sch", "vch"):
            self.ev[k] = 0
        self._dcnt = {k: 0 for k in self.dmarks}
        self._emitted = getattr(self, "_emitted", set())
        if mode != "plan":
            self._emitted.add(mode)

    def mark(self, name, eng):
        if self.mode == "plan":
            self.M[name] = self.ev[eng]
        return self.M.get(name)

    def dma(self, eng_key, slot, out, in_):
        if slot not in self._dsems:
            self._dsems[slot] = self.nc.alloc_semaphore("sd_" + slot)
            self.dmarks[slot] = [0]
            self._dcnt[slot] = 0
        self._dcnt[slot] += 1
        n = self._dcnt[slot]
        if self.mode == eng_key:
            n0 = self.nc.n_instructions()
            self.real.dma_start(out=out, in_=in_).then_inc(self._dsems[slot], 16)
            assert len(self.dmarks[slot]) == n, (slot, n, len(self.dmarks[slot]))
            self.dmarks[slot].append(
                self.dmarks[slot][-1] + 16 * (self.nc.n_instructions() - n0))
        if self.mode == "plan" or len(self.dmarks[slot]) <= n:
            return None
        return (self._dsems[slot], self.dmarks[slot][n])

    def run(self, block, body):
        self.begin("plan")
        body(self)
        order = [("sy", block.sync), ("g", block.gpsimd), ("t", block.tensor),
                 ("s", block.scalar), ("v", block.vector)]
        for key, starter in order:
            def f(real, key=key):
                self.begin(key, real)
                body(self)
            starter(f)



NEGC_SCALE = float(-np.sqrt(128.0) * 1.002)


def build_l0b(upto="l1a"):
    nc = bass.Bass("TRN2", target_bir_lowering=False, debug=False)
    x = nc.dram_tensor("x", [128, 16 * CORE_TOK], F32, kind="ExternalInput").ap()
    q0o = nc.dram_tensor("q0o", [NH, 128, CORE_TOK], BF16, kind="ExternalInput").ap()
    k0g = nc.dram_tensor("k0g", [NG, NH, 128, KSPAN * 128], BF16, kind="ExternalInput").ap()
    v0g = nc.dram_tensor("v0g", [NG, NH, 128, KSPAN * 128], BF16, kind="ExternalInput").ap()
    masks = nc.dram_tensor("masks", [NG, 128, KSPAN * GT], BF16, kind="ExternalInput").ap()
    qg = nc.dram_tensor("qg", [1, 128], F32, kind="ExternalInput").ap()
    kg = nc.dram_tensor("kg", [1, 128], F32, kind="ExternalInput").ap()
    wout = nc.dram_tensor("wout", [16, 128, 16 * 128], F32, kind="ExternalInput").ap()
    wg_t = nc.dram_tensor("wg_t", [64, 128, 16 * 128], F32, kind="ExternalInput").ap()
    wu_t = nc.dram_tensor("wu_t", [64, 128, 16 * 128], F32, kind="ExternalInput").ap()
    wd_t = nc.dram_tensor("wd_t", [16, 128, 64 * 128], F32, kind="ExternalInput").ap()
    wq_t = nc.dram_tensor("wq_t", [16, 128, 16 * 128], F32, kind="ExternalInput").ap()
    wk_t = nc.dram_tensor("wk_t", [16, 128, 16 * 128], F32, kind="ExternalInput").ap()
    wv_t = nc.dram_tensor("wv_t", [16, 128, 16 * 128], F32, kind="ExternalInput").ap()
    cos1 = nc.dram_tensor("cos1", [128, CORE_TOK], F32, kind="ExternalInput").ap()
    sin1 = nc.dram_tensor("sin1", [128, CORE_TOK], F32, kind="ExternalInput").ap()
    x1o = nc.dram_tensor("x1o", [128, 16 * CORE_TOK], F32, kind="ExternalOutput").ap()
    x2o = nc.dram_tensor("x2o", [128, 16 * CORE_TOK], F32, kind="ExternalOutput").ap()
    q1 = nc.dram_tensor("q1", [NH, 128, CORE_TOK], BF16, kind="ExternalOutput").ap()
    k1 = nc.dram_tensor("k1", [NH, 128, CORE_TOK], BF16, kind="ExternalOutput").ap()
    v1 = nc.dram_tensor("v1", [NH, CORE_TOK, 128], BF16, kind="ExternalOutput").ap()
    qkn = nc.dram_tensor("qkn", [1, 2 * NH], F32, kind="ExternalOutput").ap()

    C, cload, NC_ = _consts(nc)
    sb = nc.alloc_sbuf_tensor
    gq_s = sb("gq_s", [1, 128], F32)
    gk_s = sb("gk_s", [1, 128], F32)
    coss = sb("coss", [128, CORE_TOK], F32)
    sins = sb("sins", [128, CORE_TOK], F32)
    wA = sb("wA", [128, 4, 16 * 128], BF16)
    rows = sb("rows", [1, 12, GT], F32)
    (mean_r, msq_r, es_r, var_r, sd_r, rstd_r, mr_r, den_r, sdq_r, rsq_r,
     nrm_r, spare_r) = [rows[0:1, i, :] for i in range(12)]
    tiny = sb("tiny", [1, 16], F32)
    negc = sb("negc", [128, 1], F32)
    abc = sb("abc", [128, GT], F32)
    bbc = sb("bbc", [128, GT], F32)
    qb = sb("qb", [128, 2, GT], BF16)
    probs = sb("probs", [128, 4, GT], BF16)
    pp = sb("pp", [128, 2, GT], BF16)
    t1 = sb("t1", [128, GT], F32)
    t2 = sb("t2", [128, GT], F32)
    qn = sb("qn", [128, GT], BF16)
    sqs = sb("sqs", [128, GT], BF16)
    qout = sb("qout", [128, 2, GT], BF16)
    vout = sb("vout", [128, 2, GT], BF16)
    rbc = sb("rbc", [128, GT], F32)
    nacc = sb("nacc", [1, 2 * NH], F32)
    xout = sb("xout", [128, 2, GT], F32)

    ARENA = 112 * 1024
    arena = sb("arena", [128, ARENA], mybir.dt.uint8)
    abase = nc.lookup_mloc(arena).addr

    def at(name, shape, dtype, off):
        return nc.alloc_sbuf_tensor_at(name, shape, dtype, offset=abase + off)

    kbuf = at("kbuf", [128, 2, KSPAN * 128], BF16, 0)
    vbuf = at("vbuf", [128, 2, KSPAN * 128], BF16, 18 * 1024)
    mbuf = at("mbuf", [128, KSPAN, GT], BF16, 36 * 1024)
    ao = at("ao", [128, 16, GT], BF16, 72 * 1024)
    yb = at("yb", [128, 16, GT], BF16, 88 * 1024)
    ysq = at("ysq", [128, GT], BF16, 104 * 1024)
    xres = at("xres", [128, 16, GT], F32, 36 * 1024)
    xb1 = at("xb1", [128, 16, GT], BF16, 0)
    xhat1 = at("xhat1", [128, 16, GT], BF16, 16 * 1024)
    gated = at("gated", [128, 64, GT], BF16, 32 * 1024)
    wB = at("wB", [128, 2, 32 * 128], BF16, 96 * 1024)
    yb0 = at("yb0", [128, 16, GT], BF16, 0)
    ysq0 = at("ysq0", [128, GT], BF16, 16 * 1024)
    xres0 = at("xres0", [128, 2, GT], F32, 17 * 1024)
    xres0f = at("xres0f", [128, 16, GT], F32, 32 * 1024)
    xb2 = at("xb2", [128, 16, GT], BF16, 0)
    xhat2 = at("xhat2", [128, 16, CORE_TOK], BF16, 16 * 1024)
    sql1 = at("sql1", [128, 16, GT], BF16, 80 * 1024)
    at0 = dict(xres0f=xres0f, sql1=sql1)

    ps = nc.alloc_psum_tensor
    pS = ps("pS", [128, 4, GT], F32)
    pA = [pS[:, 0, :], pS[:, 1, :], pS[:, 2, :], pS[:, 3, :]]
    pO = ps("pO", [128, GT], F32)
    pB = [ps(f"pB{i}", [1, GT], F32) for i in range(2)]
    pC = ps("pC", [128, 4, 128], BF16)
    pD = pO

    cx = Cx(nc)
    M = cx.M

    def ln_rows(cx, key, bca=pA[2], bcb=pA[3]):
        s, v, t = cx.s, cx.v, cx.t
        s.wait("t", M.get((key, "s12t")))
        s.mul(mean_r, pB[0][0:1, :], 1.0 / D)
        s.edge()
        s.activation(msq_r, mean_r, AF.Square)
        s.edge()
        s.mul(es_r, pB[1][0:1, :], 1.0 / D)
        s.inc()
        cx.mark((key, "es"), "s")
        v.wait("s", M.get((key, "es")))
        v.tensor_sub(var_r, es_r, msq_r)
        v.inc()
        cx.mark((key, "var"), "v")
        s.wait("v", M.get((key, "var")))
        s.activation(sd_r, var_r, AF.Sqrt, bias=1e-5)
        s.inc()
        cx.mark((key, "sd"), "s")
        v.wait("s", M.get((key, "sd")))
        v.reciprocal(rstd_r, sd_r)
        v.edge()
        v.tensor_mul(mr_r, mean_r, rstd_r)
        v.inc()
        cx.mark((key, "mr"), "v")
        t.wait("v", M.get((key, "mr")))
        t.matmul(bca[:], C["ocol"][:], rstd_r, start=True, stop=True)
        t.matmul(bcb[:], C["ocol"][:], mr_r, start=True, stop=True)
        t.inc()
        cx.mark((key, "bc"), "t")
        v.wait("t", M.get((key, "bc")))
        v.tensor_copy(abc[:], bca[:])
        v.tensor_copy(bbc[:], bcb[:])
        v.inc()
        cx.mark((key, "abv"), "v")
        v.wait("v", M.get((key, "abv")))

    def body(cx):
        g, sy, t, s, v = cx.g, cx.sy, cx.t, cx.s, cx.v
        mk = {}
        for dst, srcap in C["pairs"]:
            mk["c"] = cx.dma("g", "c", dst, srcap)
        cx.dma("g", "c", gq_s[:], qg)
        cx.dma("g", "c", gk_s[:], kg)
        cx.dma("g", "c", coss[:], cos1)
        mk["c"] = cx.dma("g", "c", sins[:], sin1)

        v.wait_dma(mk["c"])
        v.memset(nacc[:], 0.0)
        v.reduce_max(tiny[0:1, 0:1], gq_s[:], axis=AX, apply_absolute_value=True)
        v.edge()
        v.reduce_max(tiny[0:1, 1:2], gk_s[:], axis=AX, apply_absolute_value=True)
        v.edge()
        v.tensor_mul(tiny[0:1, 2:3], tiny[0:1, 0:1], tiny[0:1, 1:2])
        v.inc()
        cx.mark("negv", "v")
        s.wait("v", M.get("negv"))
        s.activation(tiny[0:1, 3:4], tiny[0:1, 2:3], AF.Copy,
                     scale=NEGC_SCALE, bias=-0.01)
        s.inc()
        cx.mark("negs", "s")
        t.wait("s", M.get("negs"))
        t.wait_dma(mk["c"])
        t.matmul(pD[:, 0:1], C["ocol"][:], tiny[0:1, 3:4], start=True, stop=True)
        t.inc()
        cx.mark("negt", "t")
        v.wait("t", M.get("negt"))
        v.tensor_copy(negc[:], pD[:, 0:1])
        v.inc()
        cx.mark("negbc", "v")

        for gi in range(NG):
            if gi >= 1:
                g.wait("v", M.get(("resv", gi - 1, 15)))
            mk["m"] = cx.dma("g", "m", mbuf[:, :, :].rearrange("p a n -> p (a n)"),
                             masks[gi, :, :])
            for h in range(NH):
                a = gi * NH + h
                hp = a % 2
                if a >= 2:
                    g.wait("t", M.get(("att_t_end", a - 2)))
                cx.dma("g", "k%d" % hp, kbuf[:, hp, :], k0g[gi, h, :, :])
                cx.dma("g", "k%d" % hp, vbuf[:, hp, :], v0g[gi, h, :, :])
                mk[("kvq", a)] = cx.dma("g", "k%d" % hp, qb[:, hp, :],
                                        q0o[h, :, gi * GT:(gi + 1) * GT])
                t.wait_dma(mk[("kvq", a)])
                NP = KSPAN // 2
                for p in range(NP):
                    for i in (2 * p, 2 * p + 1):
                        sl4 = i % 4
                        if i % 2 == 0:
                            if p >= 2:
                                t.wait("s", M.get(("exp", a, p - 2)))
                            elif a >= 1:
                                t.wait("s", M.get(("exp", a - 1, NP - 2 + p)))
                            if p == 0 and a >= 1:
                                t.wait("v", M.get(("aov", a - 1)))
                            if p == 0 and h == 0 and gi >= 1:
                                t.wait("s", M.get(("ye", gi - 1, 15)))
                                t.wait("v", M.get((("op", gi - 1), "abv")))
                        t.matmul(pS[:, sl4, :], kbuf[:, hp, i * 128:(i + 1) * 128],
                                 qb[:, hp, :], start=True, stop=True)
                        t.inc()
                        cx.mark(("sc", a, i), "t")
                    t.wait("v", M.get(("pm", a, p)))
                    for i in (2 * p, 2 * p + 1):
                        if i == 0:
                            if h == 0 and gi >= 1:
                                t.wait("s", M.get(("opmean", gi - 1)))
                        t.matmul(pO[:], vbuf[:, hp, i * 128:(i + 1) * 128],
                                 probs[:, i % 4, :], start=(i == 0),
                                 stop=(i == KSPAN - 1), skip_group_check=True)
                        t.inc()
                        cx.mark(("pv", a, i), "t")
                    t.wait("v", M.get(("padd", a, p)))
                    t.matmul(pB[0][:], C["o128"][:], pp[:, p % 2, :],
                             start=(p == 0), stop=(p == NP - 1),
                             skip_group_check=True)
                    t.inc()
                    cx.mark(("den", a, p), "t")
                t.wait("v", M.get(("rec", a)))
                t.wait("s", M.get(("exp", a, NP - 1)))
                t.matmul(pS[:, 0, :], C["ocol"][:], den_r, start=True, stop=True)
                t.inc()
                cx.mark(("rbc", a), "t")
                cx.mark(("att_t_end", a), "t")
                if a == 0:
                    s.wait("v", M.get("negbc"))
                for p in range(KSPAN // 2):
                    NP = KSPAN // 2
                    sl4 = (2 * p) % 4
                    s.wait("t", M.get(("sc", a, 2 * p + 1)))
                    if p >= 2:
                        s.wait("t", M.get(("pv", a, 2 * p - 3)))
                        s.wait("v", M.get(("padd", a, p - 2)))
                    elif a >= 1:
                        s.wait("t", M.get(("pv", a - 1, KSPAN - 3 + 2 * p)))
                        s.wait("v", M.get(("padd", a - 1, NP - 2 + p)))
                    s.activation(probs[:, sl4:sl4 + 2, :].rearrange("p a n -> p (a n)"),
                                 pS[:, sl4:sl4 + 2, :].rearrange("p a n -> p (a n)"),
                                 AF.Exp, scale=SCALE, bias=negc[:, 0:1])
                    s.inc()
                    cx.mark(("exp", a, p), "s")
                if h == 0:
                    v.wait_dma(mk["m"])
                for p in range(KSPAN // 2):
                    NP = KSPAN // 2
                    sl4 = (2 * p) % 4
                    v.wait("s", M.get(("exp", a, p)))
                    v.tensor_mul(probs[:, sl4:sl4 + 2, :].rearrange("p a n -> p (a n)"),
                                 probs[:, sl4:sl4 + 2, :].rearrange("p a n -> p (a n)"),
                                 mbuf[:, 2 * p:2 * p + 2, :].rearrange("p a n -> p (a n)"))
                    v.inc()
                    cx.mark(("pm", a, p), "v")
                    v.wait("v", M.get(("pm", a, p)))
                    if p >= 2:
                        v.wait("t", M.get(("den", a, p - 2)))
                    elif a >= 1:
                        v.wait("t", M.get(("den", a - 1, NP - 2 + p)))
                    v.tensor_add(pp[:, p % 2, :], probs[:, sl4, :], probs[:, sl4 + 1, :])
                    v.inc()
                    cx.mark(("padd", a, p), "v")
                v.wait("t", M.get(("den", a, KSPAN // 2 - 1)))
                v.reciprocal(den_r, pB[0][0:1, :])
                v.inc()
                cx.mark(("rec", a), "v")
                v.wait("t", M.get(("rbc", a)))
                if a >= 1:
                    v.wait("v", M.get(("aov", a - 1)))
                v.tensor_copy(rbc[:], pS[:, 0, :])
                v.edge()
                v.tensor_mul(ao[:, h, :], pO[:], rbc[:])
                v.inc()
                cx.mark(("aov", a), "v")
            cx.mark(("attv_end", gi), "v")

            for m in range(16):
                mp = m % 2
                if m >= 2:
                    g.wait("t", M.get(("yt", gi, m - 2)))
                else:
                    g.wait("t", M.get(("att_t_end", gi * NH + NH - 1)))
                mk[("w", gi, m)] = cx.dma("g", "w%d" % mp, wA[:, mp, :], wout[m, :, :])
                mk[("xr", gi)] = cx.dma(
                    "g", "xr", xres[:, m, :],
                    x[:, m * CORE_TOK + gi * GT:m * CORE_TOK + (gi + 1) * GT])
                t.wait_dma(mk[("w", gi, m)])
                if m >= 2:
                    t.wait("s", M.get(("ye", gi, m - 2)))
                if m == 0:
                    t.wait("v", M.get(("aov", gi * NH + NH - 1)))
                for di in range(16):
                    t.matmul(pA[mp][:], wA[:, mp, di * 128:(di + 1) * 128],
                             ao[:, di, :], start=(di == 0), stop=(di == 15))
                t.inc()
                cx.mark(("yt", gi, m), "t")
                s.wait("t", M.get(("yt", gi, m)))
                s.copy(yb[:, m, :], pA[mp][:])
                s.edge()
                s.activation(ysq[:], yb[:, m, :], AF.Square)
                s.inc()
                cx.mark(("ye", gi, m), "s")
                t.wait("s", M.get(("ye", gi, m)))
                if m == 0 and gi >= 1:
                    t.wait("s", M.get(("opes", gi - 1)))
                t.matmul(pB[0][:], C["o128"][:], yb[:, m, :],
                         start=(m == 0), stop=(m == 15), skip_group_check=True)
                t.matmul(pB[1][:], C["o128"][:], ysq[:],
                         start=(m == 0), stop=(m == 15), skip_group_check=True)
                t.inc()
                if m == 15:
                    cx.mark((("op", gi), "s12t"), "t")
            ln_rows(cx, ("op", gi))
            if cx.mode == "plan":
                M[("opmean", gi)] = M[(("op", gi), "es")]
                M[("opes", gi)] = M[(("op", gi), "es")]
            for m in range(16):
                mp = m % 2
                if m == 0:
                    v.wait_dma(mk[("xr", gi)])
                    if gi >= 1:
                        v.wait("v", M.get(("resv", gi - 1, 15)))
                else:
                    v.wait("v", M.get(("resv", gi, m - 1)))
                if m >= 2:
                    v.wait_dma(mk.get(("st", gi, m - 2)))
                elif gi >= 1:
                    v.wait_dma(mk.get(("st", gi - 1, m + 14)))
                v.tensor_sub(t1[:], xres[:, m, :], bbc[:])
                v.tensor_mul(t2[:], yb[:, m, :], abc[:])
                v.edge()
                v.tensor_add(xout[:, mp, :], t1[:], t2[:])
                v.inc()
                cx.mark(("resv", gi, m), "v")
                sy.wait("v", M.get(("resv", gi, m)))
                mk[("st", gi, m)] = cx.dma(
                    "sy", "st%d" % mp,
                    x1o[:, m * CORE_TOK + gi * GT:m * CORE_TOK + (gi + 1) * GT],
                    xout[:, mp, :])
        def drain(slots):
            for sl in slots:
                if cx.mode != "plan" and sl in cx.dmarks:
                    sy.wait_dma((cx._dsems[sl], cx.dmarks[sl][-1]))

        if upto == "att":
            drain(("st0", "st1"))
            return

        xres0f = at0["xres0f"]
        for c in range(NG):
            g.wait("t", M.get(("att_t_end", 63)) if c == 0 else None)
            if c >= 1:
                g.wait("v", M.get(("f0resv", c - 1, 15)))
            g.wait_dma(mk.get(("st", c, 14)))
            g.wait_dma(mk.get(("st", c, 15)))
            mk[("xb1", c)] = cx.dma(
                "g", "xc", xb1[:, :, :],
                x1o[:, :].rearrange("p (a n) -> p a n", a=16)[:, :, c * GT:(c + 1) * GT])
            t.wait_dma(mk[("xb1", c)])
            if c == 0:
                t.wait("s", M.get((("op", 3), "es")))
            else:
                t.wait("s", M.get((("f0o", c - 1), "es")))
            for di in range(16):
                t.matmul(pB[0][:], C["o128"][:], xb1[:, di, :],
                         start=(di == 0), stop=(di == 15), skip_group_check=True)
            t.inc()
            cx.mark(("f0s1", c), "t")
            s.wait_dma(mk[("xb1", c)])
            if c >= 1:
                s.wait("t", M.get(("f0s2", c - 1)))
            for di in range(16):
                s.activation(xhat1[:, di, :], xb1[:, di, :], AF.Square)
            s.inc()
            cx.mark(("f0sq", c), "s")
            t.wait("s", M.get(("f0sq", c)))
            for di in range(16):
                t.matmul(pB[1][:], C["o128"][:], xhat1[:, di, :],
                         start=(di == 0), stop=(di == 15), skip_group_check=True)
            t.inc()
            cx.mark(("f0s2", c), "t")
            if cx.mode == "plan":
                M[(("f0i", c), "s12t")] = M[("f0s2", c)]
            ln_rows(cx, ("f0i", c))
            for di in range(16):
                if di == 0:
                    v.wait("t", M.get(("f0s2", c)))
                v.tensor_mul(xhat1[:, di, :], xb1[:, di, :], abc[:])
            v.edge()
            for di in range(16):
                v.tensor_sub(xhat1[:, di, :], xhat1[:, di, :], bbc[:])
            v.inc()
            cx.mark(("f0xh", c), "v")
            for ht in range(64):
                hp = ht % 2
                if ht >= 2:
                    g.wait("t", M.get(("f0g", c, ht - 2)))
                elif c == 0:
                    g.wait("t", M.get(("yt", 3, 14 + ht)))
                else:
                    g.wait("t", M.get(("f0g", c - 1, 62 + ht)))
                cx.dma("g", "wg%d" % hp, wA[:, hp, :], wg_t[ht, :, :])
                mk[("wgu", c, ht)] = cx.dma("g", "wg%d" % hp, wA[:, 2 + hp, :],
                                            wu_t[ht, :, :])
                t.wait_dma(mk[("wgu", c, ht)])
                if ht == 0:
                    t.wait("v", M.get(("f0xh", c)))
                if ht >= 1:
                    t.wait("s", M.get(("f0sl", c, ht - 1)))
                for di in range(16):
                    t.matmul(pA[0][:], wA[:, hp, di * 128:(di + 1) * 128],
                             xhat1[:, di, :], start=(di == 0), stop=(di == 15))
                t.inc()
                cx.mark(("f0gm", c, ht), "t")
                if ht >= 1:
                    t.wait("v", M.get(("f0gt", c, ht - 1)))
                for di in range(16):
                    t.matmul(pA[1][:], wA[:, 2 + hp, di * 128:(di + 1) * 128],
                             xhat1[:, di, :], start=(di == 0), stop=(di == 15))
                t.inc()
                cx.mark(("f0g", c, ht), "t")
                s.wait("t", M.get(("f0gm", c, ht)))
                if ht >= 1:
                    s.wait("v", M.get(("f0gt", c, ht - 1)))
                s.activation(sqs[:], pA[0][:], AF.Silu)
                s.inc()
                cx.mark(("f0sl", c, ht), "s")
                v.wait("s", M.get(("f0sl", c, ht)))
                v.wait("t", M.get(("f0g", c, ht)))
                if ht <= 1 and c >= 1:
                    v.wait("v", M.get(("f0resv", c - 1, 15)))
                v.tensor_mul(gated[:, ht, :], sqs[:], pA[1][:])
                v.inc()
                cx.mark(("f0gt", c, ht), "v")
            for j in range(16):
                jp = j % 2
                for half in range(2):
                    wp = (2 * j + half) % 2
                    if 2 * j + half >= 2:
                        g.wait("t", M.get(("f0dm", c, (2 * j + half - 2) // 2,
                                           (2 * j + half - 2) % 2)))
                    mk[("wd", c, j, half)] = cx.dma(
                        "g", "wd%d" % wp, wB[:, wp, :],
                        wd_t[j, :, half * 4096:(half + 1) * 4096])
                    t.wait_dma(mk[("wd", c, j, half)])
                    if j == 0 and half == 0:
                        t.wait("v", M.get(("f0gt", c, 63)))
                    if half == 0 and j >= 2:
                        t.wait("s", M.get(("f0ye", c, j - 2)))
                    for q in range(32):
                        t.matmul(pA[jp][:], wB[:, wp, q * 128:(q + 1) * 128],
                                 gated[:, half * 32 + q, :],
                                 start=(half == 0 and q == 0),
                                 stop=(half == 1 and q == 31))
                    t.inc()
                    cx.mark(("f0dm", c, j, half), "t")
                s.wait("t", M.get(("f0dm", c, j, 1)))
                if c >= 1 and j <= 1:
                    s.wait("v", M.get(("f0resv", c - 1, 15)))
                s.copy(yb0[:, j, :], pA[jp][:])
                s.edge()
                s.activation(ysq0[:], yb0[:, j, :], AF.Square)
                s.inc()
                cx.mark(("f0ye", c, j), "s")
                t.wait("s", M.get(("f0ye", c, j)))
                if j == 0:
                    t.wait("s", M.get((("f0i", c), "es")))
                t.matmul(pB[0][:], C["o128"][:], yb0[:, j, :],
                         start=(j == 0), stop=(j == 15), skip_group_check=True)
                t.matmul(pB[1][:], C["o128"][:], ysq0[:],
                         start=(j == 0), stop=(j == 15), skip_group_check=True)
                t.inc()
                if j == 15:
                    cx.mark((("f0o", c), "s12t"), "t")
            ln_rows(cx, ("f0o", c))
            g.wait("t", M.get(("f0dm", c, 15, 1)))
            for m in range(16):
                mk[("xr0", c)] = cx.dma(
                    "g", "xr0", xres0f[:, m, :],
                    x1o[:, m * CORE_TOK + c * GT:m * CORE_TOK + (c + 1) * GT])
            for m in range(16):
                mp = m % 2
                if m == 0:
                    v.wait_dma(mk[("xr0", c)])
                    if c >= 1:
                        v.wait("v", M.get(("f0resv", c - 1, 15)))
                    else:
                        v.wait("v", M.get(("resv", 3, 15)))
                else:
                    v.wait("v", M.get(("f0resv", c, m - 1)))
                if m >= 2:
                    v.wait_dma(mk.get(("stx2", c, m - 2)))
                elif c >= 1:
                    v.wait_dma(mk.get(("stx2", c - 1, m + 14)))
                else:
                    v.wait_dma(mk.get(("st", 3, m + 14)))
                v.tensor_sub(t1[:], xres0f[:, m, :], bbc[:])
                v.tensor_mul(t2[:], yb0[:, m, :], abc[:])
                v.edge()
                v.tensor_add(xout[:, mp, :], t1[:], t2[:])
                v.inc()
                cx.mark(("f0resv", c, m), "v")
                sy.wait("v", M.get(("f0resv", c, m)))
                mk[("stx2", c, m)] = cx.dma(
                    "sy", "st%d" % mp,
                    x2o[:, m * CORE_TOK + c * GT:m * CORE_TOK + (c + 1) * GT],
                    xout[:, mp, :])
        if upto == "f0":
            drain(("st0", "st1"))
            return

        sql1 = at0["sql1"]
        for c in range(NG):
            g.wait("v", M.get(("l1xh", c - 1)) if c >= 1 else
               M.get(("f0resv", 3, 15)))
            g.wait_dma(mk.get(("stx2", c, 14)))
            g.wait_dma(mk.get(("stx2", c, 15)))
            mk[("xb2", c)] = cx.dma(
                "g", "xc", xb2[:, :, :],
                x2o[:, :].rearrange("p (a n) -> p a n", a=16)[:, :, c * GT:(c + 1) * GT])
            t.wait_dma(mk[("xb2", c)])
            if c == 0:
                t.wait("s", M.get((("f0o", 3), "es")))
            else:
                t.wait("s", M.get((("l1i", c - 1), "es")))
            for di in range(16):
                t.matmul(pB[0][:], C["o128"][:], xb2[:, di, :],
                         start=(di == 0), stop=(di == 15), skip_group_check=True)
            t.inc()
            cx.mark(("l1s1", c), "t")
            s.wait_dma(mk[("xb2", c)])
            if c >= 1:
                s.wait("t", M.get(("l1s2", c - 1)))
            if c == 0:
                s.wait("v", M.get(("f0resv", 3, 15)))
            for di in range(16):
                s.activation(sql1[:, di, :], xb2[:, di, :], AF.Square)
            s.inc()
            cx.mark(("l1sq", c), "s")
            t.wait("s", M.get(("l1sq", c)))
            for di in range(16):
                t.matmul(pB[1][:], C["o128"][:], sql1[:, di, :],
                         start=(di == 0), stop=(di == 15), skip_group_check=True)
            t.inc()
            cx.mark(("l1s2", c), "t")
            if cx.mode == "plan":
                M[(("l1i", c), "s12t")] = M[("l1s2", c)]
            ln_rows(cx, ("l1i", c))
            for di in range(16):
                if di == 0:
                    v.wait("t", M.get(("l1s2", c)))
                    if c == 0:
                        v.wait("v", M.get(("f0resv", 3, 15)))
                v.tensor_mul(xhat2[:, di, c * GT:(c + 1) * GT], xb2[:, di, :], abc[:])
            v.edge()
            for di in range(16):
                v.tensor_sub(xhat2[:, di, c * GT:(c + 1) * GT],
                             xhat2[:, di, c * GT:(c + 1) * GT], bbc[:])
            v.inc()
            cx.mark(("l1xh", c), "v")

        wmats = [wq_t, wk_t, wv_t]
        for m in range(48):
            mp = m % 2
            if m >= 2:
                g.wait("t", M.get(("l1mm", (m - 2) * 4 + 3)))
            elif cx.mode == "plan" or True:
                g.wait("t", M.get(("f0g", 3, 62 + m)))
            mk[("l1w", m)] = cx.dma("g", "w%d" % mp, wA[:, mp, :],
                                    wmats[m // 16][m % 16, :, :])
            isv = m >= 32
            for tc in range(NG):
                i = m * 4 + tc
                par = i % 2
                t.wait_dma(mk[("l1w", m)])
                if i == 0:
                    t.wait("v", M.get(("l1xh", 3)))
                    t.wait("s", M.get(("f0ye", 3, 15)))
                if i >= 2:
                    t.wait("v", M.get(("l1qn", i - 2)))
                for di in range(16):
                    t.matmul(pA[par][:], wA[:, mp, di * 128:(di + 1) * 128],
                             xhat2[:, di, tc * GT:(tc + 1) * GT],
                             start=(di == 0), stop=(di == 15))
                t.inc()
                cx.mark(("l1mm", i), "t")
                if not isv:
                    s.wait("t", M.get(("l1mm", i)))
                    if i >= 1:
                        s.wait("t", M.get(("l1nt", i - 1)))
                    s.activation(sqs[:], pA[par][:], AF.Square)
                    s.inc()
                    cx.mark(("l1sqp", i), "s")
                    t.wait("s", M.get(("l1sqp", i)))
                    if i >= 1:
                        t.wait("v", M.get(("l1nv", i - 1)))
                    elif cx.mode == "plan" or True:
                        t.wait("s", M.get((("l1i", 3), "es")))
                    t.matmul(pB[0][:], C["o128"][:], sqs[:], start=True, stop=True)
                    t.inc()
                    cx.mark(("l1nt", i), "t")
                    v.wait("t", M.get(("l1nt", i)))
                    v.reduce_max(tiny[0:1, 4:5], pB[0][0:1, :], axis=AX)
                    v.edge()
                    slot = (m // 16) * 16 + (m % 16)
                    v.tensor_max(nacc[0:1, slot:slot + 1], nacc[0:1, slot:slot + 1],
                                 tiny[0:1, 4:5])
                    v.inc()
                    cx.mark(("l1nv", i), "v")
                    v.wait("t", M.get(("l1mm", i)))
                    v.tensor_copy(qn[:], pA[par][:])
                    v.inc()
                    cx.mark(("l1qn", i), "v")
                    t.wait("v", M.get(("l1qn", i)))
                    if i >= 1:
                        t.wait("v", M.get(("l1out", i - 1)))
                    t.matmul(pD[:], C["P1"][:], qn[:], start=True, stop=True)
                    t.inc()
                    cx.mark(("l1rot", i), "t")
                    v.wait("t", M.get(("l1rot", i)))
                    v.tensor_mul(t1[:], qn[:], coss[:, tc * GT:(tc + 1) * GT])
                    v.tensor_mul(t2[:], pD[:], sins[:, tc * GT:(tc + 1) * GT])
                    v.edge()
                    if i >= 2:
                        v.wait_dma(mk.get(("l1st", i - 2)))
                    v.tensor_add(qout[:, par, :], t1[:], t2[:])
                    v.inc()
                    cx.mark(("l1out", i), "v")
                    sy.wait("v", M.get(("l1out", i)))
                    dst = (q1 if m < 16 else k1)[m % 16, :, tc * GT:(tc + 1) * GT]
                    mk[("l1st", i)] = cx.dma("sy", "st%d" % par, dst, qout[:, par, :])
                else:
                    v.wait("t", M.get(("l1mm", i)))
                    v.tensor_copy(qn[:], pA[par][:])
                    v.inc()
                    cx.mark(("l1qn", i), "v")
                    t.wait("v", M.get(("l1qn", i)))
                    if i >= 1:
                        t.wait("v", M.get(("l1out", i - 1)))
                    for blk in range(4):
                        t.transpose(pC[:, blk, :], qn[:, blk * 128:(blk + 1) * 128],
                                    C["id128"][:])
                    t.inc()
                    cx.mark(("l1rot", i), "t")
                    v.wait("t", M.get(("l1rot", i)))
                    if i >= 2:
                        v.wait_dma(mk.get(("l1st", i - 2)))
                    v.tensor_copy(vout[:, par, :].rearrange("p (b n) -> p b n", b=4),
                                  pC[:, :, :])
                    v.inc()
                    cx.mark(("l1out", i), "v")
                    sy.wait("v", M.get(("l1out", i)))
                    mk[("l1st", i)] = cx.dma(
                        "sy", "st%d" % par,
                        v1[m - 32, tc * GT:(tc + 1) * GT, :]
                        .rearrange("(b p) d -> p b d", p=128),
                        vout[:, par, :].rearrange("p (b d) -> p b d", b=4))
        sy.wait("v", M.get(("l1nv", 127)))
        cx.dma("sy", "st0", qkn[:, :], nacc[:, :])
        drain(("st0", "st1"))

    with nc.Block() as block:
        cx.run(block, body)
    return nc


def build_masks(r):
    out = np.zeros((NG, 128, KSPAN, NG, 128), dtype=BF)
    for g in range(NG):
        kg_run = RUNS[r][g]
        for i in range(KSPAN):
            kb = 4 * kg_run - 32 + i
            if kb < 0:
                continue
            kpos = kb * 128 + np.arange(128)[:, None, None]
            qpos = ((4 * kg_run + np.arange(NG)) * 128)[None, :, None] \
                + np.arange(128)[None, None, :]
            valid = (kpos <= qpos) & (qpos - kpos <= WINDOW)
            out[g, :, i, :, :] = valid.astype(BF)
    return np.ascontiguousarray(out.reshape(NG, 128, KSPAN * NG * 128))


def _prep_l0b_inputs(inputs, resA):
    x_full = np.asarray(inputs["L0_attn_x"], dtype=np.float32)
    qg = np.asarray(inputs["L0_attn_q_gamma"], dtype=np.float32).reshape(1, 128)
    kg = np.asarray(inputs["L0_attn_k_gamma"], dtype=np.float32).reshape(1, 128)
    wout = retile_w(np.asarray(inputs["L0_attn_w_out"], dtype=np.float32))
    wg_t = retile_w(np.ascontiguousarray(
        np.asarray(inputs["L0_ffn_w_gate"], dtype=np.float32).T))
    wu_t = retile_w(np.ascontiguousarray(
        np.asarray(inputs["L0_ffn_w_up"], dtype=np.float32).T))
    wd_t = retile_w(np.ascontiguousarray(
        np.asarray(inputs["L0_ffn_w_down"], dtype=np.float32).T), din=8192)
    wq_t = retile_w(np.ascontiguousarray(
        np.asarray(inputs["L1_attn_wq"], dtype=np.float32).T))
    wk_t = retile_w(np.ascontiguousarray(
        np.asarray(inputs["L1_attn_wk"], dtype=np.float32).T))
    wv_t = retile_w(np.ascontiguousarray(
        np.asarray(inputs["L1_attn_wv"], dtype=np.float32).T))
    owner = {}
    for rr in range(4):
        for li, bb in enumerate(core_blocks(rr)):
            owner[bb] = (rr, li)
    maps = []
    for c in range(8):
        b, r = c // 4, c % 4
        pos = core_positions(r)
        _, _, cos1, sin1 = rope_tables(pos)
        k0g = np.zeros((NG, NH, 128, KSPAN * 128), dtype=BF)
        v0g = np.zeros((NG, NH, 128, KSPAN * 128), dtype=BF)
        for g in range(NG):
            kg_run = RUNS[r][g]
            for i in range(KSPAN):
                bb = 4 * kg_run - 32 + i
                if bb < 0:
                    continue
                rb, li = owner[bb]
                src = resA[b * 4 + rb]
                k0g[g, :, :, i * 128:(i + 1) * 128] = src["k0"][:, :, li * 128:(li + 1) * 128]
                v0g[g, :, :, i * 128:(i + 1) * 128] = src["v0"][:, li * 128:(li + 1) * 128, :]
        maps.append(dict(
            x=hm_layout(x_full[b][pos]), q0o=resA[c]["q0"],
            k0g=k0g, v0g=v0g, masks=build_masks(r), qg=qg, kg=kg,
            wout=wout, wg_t=wg_t, wu_t=wu_t, wd_t=wd_t,
            wq_t=wq_t, wk_t=wk_t, wv_t=wv_t,
            cos1=np.ascontiguousarray(cos1), sin1=np.ascontiguousarray(sin1)))
    return maps


def _f1_consts(nc):
    ssel = np.zeros((8, 128, 64), dtype=np.float32)
    esel = np.zeros((8, 128, 128), dtype=BF)
    for g in range(8):
        for p in range(128):
            a = 8 * g + p // 16
            ssel[g, p, a - 8 * g * 0] = 0.0
    for g in range(8):
        for p in range(128):
            a_local = p // 16
            a_glob = 8 * g + a_local
            ssel[g, p, a_glob] = 1.0
            esel[g, a_glob, p] = 1.0
    G = np.zeros((128, 16), dtype=BF)
    for p in range(128):
        G[p, p % 16] = 1.0
    vsel = np.zeros((16, 128), dtype=BF)
    for p in range(128):
        vsel[p % 16, p] = 1.0 / 16.0
    c_ssel = nc.inline_tensor(ssel.reshape(8 * 128, 64), name="c_ssel")
    c_esel = nc.inline_tensor(esel.reshape(8 * 128, 128).view(np.uint16), name="c_esel")
    c_G = nc.inline_tensor(G.view(np.uint16), name="c_G")
    c_vsel = nc.inline_tensor(vsel.view(np.uint16), name="c_vsel")
    sb = nc.alloc_sbuf_tensor
    t = dict(
        ssel=sb("ssel", [128, 8, 64], F32),
        esel=sb("esel", [128, 8, 128], BF16),
        selG=sb("selG", [128, 16], BF16),
        vsel=sb("vsel", [16, 128], BF16),
    )
    pairs = [
        (t["ssel"][:, :, :],
         c_ssel.ap().rearrange("(a p) n -> p a n", p=128)),
        (t["esel"][:, :, :],
         c_esel.ap().bitcast(BF16).rearrange("(a p) n -> p a n", p=128)),
        (t["selG"][:], c_G.ap().bitcast(BF16)),
        (t["vsel"][:], c_vsel.ap().bitcast(BF16)),
    ]
    return t, pairs


def build_l1b(upto="f1"):
    nc = bass.Bass("TRN2", target_bir_lowering=False, debug=False)
    x = nc.dram_tensor("x", [128, 16 * CORE_TOK], F32, kind="ExternalInput").ap()
    q1o = nc.dram_tensor("q1o", [NH, 128, CORE_TOK], BF16, kind="ExternalInput").ap()
    k1g = nc.dram_tensor("k1g", [NH, 128, SEQ], BF16, kind="ExternalInput").ap()
    v1g = nc.dram_tensor("v1g", [NH, 128, SEQ], BF16, kind="ExternalInput").ap()
    qkng = nc.dram_tensor("qkng", [1, 4 * 2 * NH], F32, kind="ExternalInput").ap()
    wo_t = nc.dram_tensor("wo_t", [16, 128, 16 * 128], F32, kind="ExternalInput").ap()
    W_t = nc.dram_tensor("W_t", [16, 128, 16 * 128], F32, kind="ExternalInput").ap()
    A_t = nc.dram_tensor("A_t", [8, 128, 16 * 128], F32, kind="ExternalInput").ap()
    B_t = nc.dram_tensor("B_t", [16, 128, 8 * 128], F32, kind="ExternalInput").ap()
    bias_t = nc.dram_tensor("bias_t", [128, 16], F32, kind="ExternalInput").ap()
    x3o = nc.dram_tensor("x3o", [128, 16 * CORE_TOK], F32, kind="ExternalOutput").ap()
    x4o = nc.dram_tensor("x4o", [128, 16 * CORE_TOK], F32, kind="ExternalOutput").ap()

    C, cload, NC_ = _consts(nc)
    S1, spairs = _f1_consts(nc)
    sb = nc.alloc_sbuf_tensor
    wA = sb("wA", [128, 4, 16 * 128], BF16)
    rows = sb("rows", [1, 12, GT], F32)
    (mean_r, msq_r, es_r, var_r, sd_r, rstd_r, mr_r, den_r, sdq_r, rsq_r,
     nrm_r, spare_r) = [rows[0:1, i, :] for i in range(12)]
    tiny = sb("tiny", [1, 64], F32)
    negc16 = sb("negc16", [128, 16], F32)
    abc = sb("abc", [128, GT], F32)
    bbc = sb("bbc", [128, GT], F32)
    qb = sb("qb", [128, 2, GT], BF16)
    probs = sb("probs", [128, 4, GT], BF16)
    pp = sb("pp", [128, 2, GT], BF16)
    t1 = sb("t1", [128, GT], F32)
    t2 = sb("t2", [128, GT], F32)
    sqs = sb("sqs", [128, GT], BF16)
    rbc = sb("rbc", [128, GT], F32)
    xout = sb("xout", [128, 2, GT], F32)
    biasb = sb("biasb", [128, 16], F32)
    qkn_s = sb("qkn_s", [1, 128], F32)

    ARENA = 110 * 1024
    arena = sb("arena", [128, ARENA], mybir.dt.uint8)
    abase = nc.lookup_mloc(arena).addr

    def at(name, shape, dtype, off):
        return nc.alloc_sbuf_tensor_at(name, shape, dtype, offset=abase + off)

    kbuf = at("kbuf", [128, 2, 4096], BF16, 0)
    vbuf = at("vbuf", [128, 2, 4096], BF16, 16 * 1024)
    ao = at("ao", [128, 16, GT], BF16, 32 * 1024)
    yb = at("yb", [128, 16, GT], BF16, 48 * 1024)
    ysq = at("ysq", [128, GT], BF16, 64 * 1024)
    xres = at("xres", [128, 16, GT], F32, 66 * 1024)
    xb3 = at("xb3", [128, 16, GT], BF16, 0)
    xhat3 = at("xhat3", [128, 16, GT], BF16, 16 * 1024)
    vecs = at("vecs", [128, 8, GT], BF16, 32 * 1024)
    me = at("me", [128, 8, GT], BF16, 40 * 1024)
    arr_sb = at("arr_sb", [128, GT], F32, 48 * 1024)
    tb = at("tb", [128, 64], F32, 50 * 1024)
    routesT = at("routesT", [128, GT], BF16, 51 * 1024)
    mvt = at("mvt", [128, GT], BF16, 52 * 1024)
    vh_sb = at("vh_sb", [16, GT], BF16, 53 * 1024)
    yb0 = at("yb0", [128, 16, GT], BF16, 54 * 1024)
    ysq0 = at("ysq0", [128, GT], BF16, 70 * 1024)
    xres0f = at("xres0f", [128, 16, GT], F32, 71 * 1024)

    ps = nc.alloc_psum_tensor
    pS = ps("pS", [128, 4, GT], F32)
    pA = [pS[:, 0, :], pS[:, 1, :], pS[:, 2, :], pS[:, 3, :]]
    pO = ps("pO", [128, GT], F32)
    pB = [ps(f"pB{i}", [1, GT], F32) for i in range(2)]
    pC = ps("pC", [128, 4, 128], BF16)
    pD = pO

    cx = Cx(nc)
    M = cx.M

    def ln_rows(cx, key):
        s, v, t = cx.s, cx.v, cx.t
        s.wait("t", M.get((key, "s12t")))
        s.mul(mean_r, pB[0][0:1, :], 1.0 / D)
        s.edge()
        s.activation(msq_r, mean_r, AF.Square)
        s.edge()
        s.mul(es_r, pB[1][0:1, :], 1.0 / D)
        s.inc()
        cx.mark((key, "es"), "s")
        v.wait("s", M.get((key, "es")))
        v.tensor_sub(var_r, es_r, msq_r)
        v.inc()
        cx.mark((key, "var"), "v")
        s.wait("v", M.get((key, "var")))
        s.activation(sd_r, var_r, AF.Sqrt, bias=1e-5)
        s.inc()
        cx.mark((key, "sd"), "s")
        v.wait("s", M.get((key, "sd")))
        v.reciprocal(rstd_r, sd_r)
        v.edge()
        v.tensor_mul(mr_r, mean_r, rstd_r)
        v.inc()
        cx.mark((key, "mr"), "v")
        t.wait("v", M.get((key, "mr")))
        t.matmul(pA[2][:], C["ocol"][:], rstd_r, start=True, stop=True)
        t.matmul(pA[3][:], C["ocol"][:], mr_r, start=True, stop=True)
        t.inc()
        cx.mark((key, "bc"), "t")
        v.wait("t", M.get((key, "bc")))
        v.tensor_copy(abc[:], pA[2][:])
        v.tensor_copy(bbc[:], pA[3][:])
        v.inc()
        cx.mark((key, "abv"), "v")
        v.wait("v", M.get((key, "abv")))

    NKT = 64

    def body(cx):
        g, sy, t, s, v = cx.g, cx.sy, cx.t, cx.s, cx.v
        mk = {}
        for dst, srcap in C["pairs"] + spairs:
            mk["c"] = cx.dma("g", "c", dst, srcap)
        cx.dma("g", "c", qkn_s[:], qkng)
        mk["c"] = cx.dma("g", "c", biasb[:], bias_t)

        v.wait_dma(mk["c"])
        v.tensor_max(tiny[0:1, 0:32], qkn_s[0:1, 0:32], qkn_s[0:1, 32:64])
        v.edge()
        v.tensor_max(tiny[0:1, 32:64], qkn_s[0:1, 64:96], qkn_s[0:1, 96:128])
        v.edge()
        v.tensor_max(tiny[0:1, 0:32], tiny[0:1, 0:32], tiny[0:1, 32:64])
        v.edge()
        v.tensor_mul(tiny[0:1, 32:48], tiny[0:1, 0:16], tiny[0:1, 16:32])
        v.inc()
        cx.mark("negv", "v")
        s.wait("v", M.get("negv"))
        s.activation(tiny[0:1, 48:64], tiny[0:1, 32:48], AF.Sqrt)
        s.edge()
        s.activation(tiny[0:1, 48:64], tiny[0:1, 48:64], AF.Copy,
                     scale=NEGC_SCALE / 128.0, bias=-0.01)
        s.inc()
        cx.mark("negs", "s")
        t.wait("s", M.get("negs"))
        t.matmul(pD[:, 0:16], C["ocol"][:], tiny[0:1, 48:64], start=True, stop=True)
        t.inc()
        cx.mark("negt", "t")
        v.wait("t", M.get("negt"))
        v.tensor_copy(negc16[:], pD[:, 0:16])
        v.inc()
        cx.mark("negbc", "v")

        for gi in range(NG):
            for h in range(NH):
                a = gi * NH + h
                ap_ = a % 2
                if a >= 2:
                    g.wait("t", M.get(("att_t_end", a - 2)))
                mk[("q", a)] = cx.dma("g", "k%d" % ap_, qb[:, ap_, :],
                                      q1o[h, :, gi * GT:(gi + 1) * GT])
                for hh in range(2):
                    sl = (2 * a + hh) % 2
                    if 2 * a + hh >= 2:
                        pa, phh = divmod(2 * a + hh - 2, 2)
                        g.wait("t", M.get(("c1h", pa, phh)))
                    cx.dma("g", "kv%d" % sl, kbuf[:, sl, :],
                           k1g[h, :, hh * 4096:(hh + 1) * 4096])
                    mk[("kv", a, hh)] = cx.dma("g", "kv%d" % sl, vbuf[:, sl, :],
                                               v1g[h, :, hh * 4096:(hh + 1) * 4096])
                t.wait_dma(mk[("q", a)])
                NP = NKT // 2
                for p in range(NP):
                    for i in (2 * p, 2 * p + 1):
                        sl4 = i % 4
                        hh = i // 32
                        sl = (2 * a + hh) % 2
                        if i % 32 == 0:
                            t.wait_dma(mk[("kv", a, hh)])
                        if i % 2 == 0:
                            if p >= 2:
                                t.wait("s", M.get(("exp", a, p - 2)))
                            elif a >= 1:
                                t.wait("s", M.get(("exp", a - 1, NP - 2 + p)))
                            if p == 0 and a >= 1:
                                t.wait("v", M.get(("aov", a - 1)))
                            if p == 0 and h == 0 and gi >= 1:
                                t.wait("s", M.get(("ye", gi - 1, 15)))
                                t.wait("v", M.get((("op", gi - 1), "abv")))
                        t.matmul(pS[:, sl4, :],
                                 kbuf[:, sl, (i % 32) * 128:(i % 32 + 1) * 128],
                                 qb[:, ap_, :], start=True, stop=True)
                        t.inc()
                        cx.mark(("sc", a, i), "t")
                    t.wait("s", M.get(("exp", a, p)))
                    for i in (2 * p, 2 * p + 1):
                        hh = i // 32
                        sl = (2 * a + hh) % 2
                        if i == 0:
                            if a >= 1:
                                t.wait("v", M.get(("aov", a - 1)))
                            if h == 0 and gi >= 1:
                                t.wait("s", M.get(("opmean", gi - 1)))
                        t.matmul(pO[:], vbuf[:, sl, (i % 32) * 128:(i % 32 + 1) * 128],
                                 probs[:, i % 4, :], start=(i == 0),
                                 stop=(i == NKT - 1), skip_group_check=True)
                        t.inc()
                        cx.mark(("pv", a, i), "t")
                    t.wait("v", M.get(("padd", a, p)))
                    t.matmul(pB[0][:], C["o128"][:], pp[:, p % 2, :],
                             start=(p == 0), stop=(p == NP - 1),
                             skip_group_check=True)
                    t.inc()
                    cx.mark(("den", a, p), "t")
                    if (2 * p + 1) % 32 == 31:
                        cx.mark(("c1h", a, (2 * p) // 32), "t")
                t.wait("v", M.get(("rec", a)))
                t.wait("s", M.get(("exp", a, NP - 1)))
                t.matmul(pS[:, 0, :], C["ocol"][:], den_r, start=True, stop=True)
                t.inc()
                cx.mark(("rbc", a), "t")
                cx.mark(("att_t_end", a), "t")
                if a == 0:
                    s.wait("v", M.get("negbc"))
                for p in range(NKT // 2):
                    NP = NKT // 2
                    sl4 = (2 * p) % 4
                    s.wait("t", M.get(("sc", a, 2 * p + 1)))
                    if p >= 2:
                        s.wait("t", M.get(("pv", a, 2 * p - 3)))
                        s.wait("v", M.get(("padd", a, p - 2)))
                    elif a >= 1:
                        s.wait("t", M.get(("pv", a - 1, NKT - 3 + 2 * p)))
                        s.wait("v", M.get(("padd", a - 1, NP - 2 + p)))
                    s.activation(probs[:, sl4:sl4 + 2, :].rearrange("p a n -> p (a n)"),
                                 pS[:, sl4:sl4 + 2, :].rearrange("p a n -> p (a n)"),
                                 AF.Exp, scale=SCALE, bias=negc16[:, h:h + 1])
                    s.inc()
                    cx.mark(("exp", a, p), "s")
                for p in range(NKT // 2):
                    NP = NKT // 2
                    sl4 = (2 * p) % 4
                    v.wait("s", M.get(("exp", a, p)))
                    if p >= 2:
                        v.wait("t", M.get(("den", a, p - 2)))
                    elif a >= 1:
                        v.wait("t", M.get(("den", a - 1, NP - 2 + p)))
                    v.tensor_add(pp[:, p % 2, :], probs[:, sl4, :], probs[:, sl4 + 1, :])
                    v.inc()
                    cx.mark(("padd", a, p), "v")
                v.wait("t", M.get(("den", a, NKT // 2 - 1)))
                v.reciprocal(den_r, pB[0][0:1, :])
                v.inc()
                cx.mark(("rec", a), "v")
                v.wait("t", M.get(("rbc", a)))
                if a >= 1:
                    v.wait("v", M.get(("aov", a - 1)))
                v.tensor_copy(rbc[:], pS[:, 0, :])
                v.edge()
                v.tensor_mul(ao[:, h, :], pO[:], rbc[:])
                v.inc()
                cx.mark(("aov", a), "v")
            cx.mark(("attv_end", gi), "v")

            for m in range(16):
                mp = m % 2
                if m >= 2:
                    g.wait("t", M.get(("yt", gi, m - 2)))
                else:
                    g.wait("t", M.get(("att_t_end", gi * NH + NH - 1)))
                mk[("w", gi, m)] = cx.dma("g", "w%d" % mp, wA[:, mp, :], wo_t[m, :, :])
                mk[("xr", gi)] = cx.dma(
                    "g", "xr", xres[:, m, :],
                    x[:, m * CORE_TOK + gi * GT:m * CORE_TOK + (gi + 1) * GT])
                t.wait_dma(mk[("w", gi, m)])
                if m >= 2:
                    t.wait("s", M.get(("ye", gi, m - 2)))
                if m == 0:
                    t.wait("v", M.get(("aov", gi * NH + NH - 1)))
                for di in range(16):
                    t.matmul(pA[mp][:], wA[:, mp, di * 128:(di + 1) * 128],
                             ao[:, di, :], start=(di == 0), stop=(di == 15))
                t.inc()
                cx.mark(("yt", gi, m), "t")
                s.wait("t", M.get(("yt", gi, m)))
                s.copy(yb[:, m, :], pA[mp][:])
                s.edge()
                s.activation(ysq[:], yb[:, m, :], AF.Square)
                s.inc()
                cx.mark(("ye", gi, m), "s")
                t.wait("s", M.get(("ye", gi, m)))
                if m == 0 and gi >= 1:
                    t.wait("s", M.get(("opes", gi - 1)))
                t.matmul(pB[0][:], C["o128"][:], yb[:, m, :],
                         start=(m == 0), stop=(m == 15), skip_group_check=True)
                t.matmul(pB[1][:], C["o128"][:], ysq[:],
                         start=(m == 0), stop=(m == 15), skip_group_check=True)
                t.inc()
                if m == 15:
                    cx.mark((("op", gi), "s12t"), "t")
            ln_rows(cx, ("op", gi))
            if cx.mode == "plan":
                M[("opmean", gi)] = M[(("op", gi), "es")]
                M[("opes", gi)] = M[(("op", gi), "es")]
            for m in range(16):
                mp = m % 2
                if m == 0:
                    v.wait_dma(mk[("xr", gi)])
                    if gi >= 1:
                        v.wait("v", M.get(("resv", gi - 1, 15)))
                else:
                    v.wait("v", M.get(("resv", gi, m - 1)))
                if m >= 2:
                    v.wait_dma(mk.get(("st", gi, m - 2)))
                elif gi >= 1:
                    v.wait_dma(mk.get(("st", gi - 1, m + 14)))
                v.tensor_sub(t1[:], xres[:, m, :], bbc[:])
                v.tensor_mul(t2[:], yb[:, m, :], abc[:])
                v.edge()
                v.tensor_add(xout[:, mp, :], t1[:], t2[:])
                v.inc()
                cx.mark(("resv", gi, m), "v")
                sy.wait("v", M.get(("resv", gi, m)))
                mk[("st", gi, m)] = cx.dma(
                    "sy", "st%d" % mp,
                    x3o[:, m * CORE_TOK + gi * GT:m * CORE_TOK + (gi + 1) * GT],
                    xout[:, mp, :])

        def drain(slots):
            for sl in slots:
                if cx.mode != "plan" and sl in cx.dmarks:
                    sy.wait_dma((cx._dsems[sl], cx.dmarks[sl][-1]))

        if upto == "att":
            drain(("st0", "st1"))
            return

        for c in range(NG):
            g.wait("t", M.get(("att_t_end", 63)) if c == 0 else None)
            if c >= 1:
                g.wait("v", M.get(("f1resv", c - 1, 15)))
            g.wait_dma(mk.get(("st", c, 14)))
            g.wait_dma(mk.get(("st", c, 15)))
            mk[("xb3", c)] = cx.dma(
                "g", "xc", xb3[:, :, :],
                x3o[:, :].rearrange("p (a n) -> p a n", a=16)[:, :, c * GT:(c + 1) * GT])
            t.wait_dma(mk[("xb3", c)])
            if c == 0:
                t.wait("s", M.get((("op", 3), "es")))
            else:
                t.wait("s", M.get((("f1o", c - 1), "es")))
            for di in range(16):
                t.matmul(pB[0][:], C["o128"][:], xb3[:, di, :],
                         start=(di == 0), stop=(di == 15), skip_group_check=True)
            t.inc()
            cx.mark(("f1s1", c), "t")
            s.wait_dma(mk[("xb3", c)])
            if c >= 1:
                s.wait("t", M.get(("f1s2", c - 1)))
            for di in range(16):
                s.activation(xhat3[:, di, :], xb3[:, di, :], AF.Square)
            s.inc()
            cx.mark(("f1sq", c), "s")
            t.wait("s", M.get(("f1sq", c)))
            for di in range(16):
                t.matmul(pB[1][:], C["o128"][:], xhat3[:, di, :],
                         start=(di == 0), stop=(di == 15), skip_group_check=True)
            t.inc()
            cx.mark(("f1s2", c), "t")
            if cx.mode == "plan":
                M[(("f1i", c), "s12t")] = M[("f1s2", c)]
            ln_rows(cx, ("f1i", c))
            for di in range(16):
                if di == 0:
                    v.wait("t", M.get(("f1s2", c)))
                    if c == 0:
                        v.wait("v", M.get(("resv", 3, 15)))
                v.tensor_mul(xhat3[:, di, :], xb3[:, di, :], abc[:])
            v.edge()
            for di in range(16):
                v.tensor_sub(xhat3[:, di, :], xhat3[:, di, :], bbc[:])
            v.inc()
            cx.mark(("f1xh", c), "v")
            for ag in range(8):
                gp = ag % 2
                if ag >= 2:
                    g.wait("t", M.get(("f1vm", c, ag - 2)))
                elif c == 0:
                    g.wait("t", M.get(("yt", 3, 14 + ag)))
                else:
                    g.wait("t", M.get(("f1bm", c - 1, 14 + ag)))
                mk[("wa", c, ag)] = cx.dma("g", "w%d" % gp, wA[:, gp, :], A_t[ag, :, :])
                t.wait_dma(mk[("wa", c, ag)])
                if ag == 0:
                    t.wait("v", M.get(("f1xh", c)))
                if ag >= 2:
                    t.wait("v", M.get(("f1ve", c, ag - 2)))
                for di in range(16):
                    t.matmul(pA[gp][:], wA[:, gp, di * 128:(di + 1) * 128],
                             xhat3[:, di, :], start=(di == 0), stop=(di == 15))
                t.inc()
                cx.mark(("f1vm", c, ag), "t")
                v.wait("t", M.get(("f1vm", c, ag)))
                v.tensor_copy(vecs[:, ag, :], pA[gp][:])
                v.inc()
                cx.mark(("f1ve", c, ag), "v")
            for ag in range(8):
                s.wait("v", M.get(("f1ve", c, ag)))
                if ag >= 1:
                    s.wait("t", M.get(("f1am", c, ag - 1)))
                s.activation(t1[:], vecs[:, ag, :], AF.Square)
                s.inc()
                cx.mark(("f1as", c, ag), "s")
                t.wait("s", M.get(("f1as", c, ag)))
                if ag == 0:
                    t.wait("s", M.get((("f1i", c), "es")))
                    t.wait("v", M.get((("f1i", c), "abv")))
                t.matmul(pA[2][0:64, :], S1["ssel"][:, ag, :], t1[:],
                         start=(ag == 0), stop=(ag == 7), skip_group_check=True)
                t.inc()
                cx.mark(("f1am", c, ag), "t")
            v.wait("t", M.get(("f1am", c, 7)))
            v.tensor_copy(arr_sb[0:64, :], pA[2][0:64, :])
            v.inc()
            cx.mark(("f1ac", c), "v")
            t.wait("v", M.get(("f1ac", c)))
            if c >= 1:
                t.wait("v", M.get(("f1t4", c - 1, 3)))
            for tt in range(4):
                t.transpose(pD[:, tt * 128:(tt + 1) * 128],
                            arr_sb[:, tt * 128:(tt + 1) * 128], C["idf32"][:])
            t.inc()
            cx.mark(("f1tr", c), "t")
            for tt in range(4):
                v.wait("t", M.get(("f1tr", c)))
                v.tensor_copy(tb[:], pD[:, tt * 128:tt * 128 + 64])
                v.edge()
                for rnd in range(3):
                    v.reduce_max(t2[:, 0:1], tb[:], axis=AX)
                    v.edge()
                    v.tensor_single_scalar(mvt[:, 0:64], tb[:], t2[:, 0:1], ALU.is_ge)
                    v.edge()
                    v.scalar_tensor_tensor(tb[:], mvt[:, 0:64], -1e30, tb[:],
                                           ALU.mult, ALU.add)
                    v.edge()
                v.reduce_max(t2[:, 0:1], tb[:], axis=AX)
                v.edge()
                v.tensor_single_scalar(mvt[:, 0:64], pD[:, tt * 128:tt * 128 + 64],
                                       t2[:, 0:1], ALU.is_ge)
                v.inc()
                cx.mark(("f1t4", c, tt), "v")
                t.wait("v", M.get(("f1t4", c, tt)))
                if tt == 0 and c >= 1:
                    t.wait("v", M.get(("f1rtc", c - 1, 3)))
                t.transpose(pC[:, tt, :][0:64, :], mvt[:, 0:64], C["id128"][:])
                t.inc()
                cx.mark(("f1rt", c, tt), "t")
                v.wait("t", M.get(("f1rt", c, tt)))
                if tt == 0:
                    if c >= 1:
                        v.wait("t", M.get(("f1mem", c - 1, 7)))
                    v.memset(routesT[:], 0.0)
                    v.edge()
                v.tensor_copy(routesT[0:64, tt * 128:(tt + 1) * 128], pC[:, tt, :][0:64, :])
                v.inc()
                cx.mark(("f1rtc", c, tt), "v")
            for ag in range(8):
                t.wait("v", M.get(("f1rtc", c, 3)))
                if ag == 0:
                    if c >= 1:
                        t.wait("v", M.get(("f1u", c - 1, 7)))
                else:
                    t.wait("v", M.get(("f1me", c, ag - 1)))
                t.matmul(pA[2][:], S1["esel"][:, ag, :], routesT[:, :],
                         start=True, stop=True)
                t.inc()
                cx.mark(("f1mem", c, ag), "t")
                v.wait("t", M.get(("f1mem", c, ag)))
                v.tensor_copy(me[:, ag, :], pA[2][:])
                v.edge()
                v.tensor_mul(mvt[:], vecs[:, ag, :], me[:, ag, :])
                v.inc()
                cx.mark(("f1me", c, ag), "v")
                t.wait("v", M.get(("f1me", c, ag)))
                if ag == 0 and c >= 1:
                    t.wait("v", M.get(("f1vh", c - 1)))
                t.matmul(pA[3][0:16, :], S1["selG"][:], mvt[:],
                         start=(ag == 0), stop=(ag == 7), skip_group_check=True)
                t.inc()
                cx.mark(("f1gm", c, ag), "t")
            v.wait("t", M.get(("f1gm", c, 7)))
            v.tensor_copy(vh_sb[:], pA[3][0:16, :])
            v.inc()
            cx.mark(("f1vh", c), "v")
            t.wait("v", M.get(("f1vh", c)))
            t.matmul(pA[2][:], S1["vsel"][:], vh_sb[:], start=True, stop=True)
            t.inc()
            cx.mark(("f1vr", c), "t")
            for ag in range(8):
                v.wait("t", M.get(("f1vr", c)))
                if ag >= 1:
                    v.wait("v", M.get(("f1u", c, ag - 1)))
                v.tensor_mul(me[:, ag, :], me[:, ag, :], pA[2][:])
                v.inc()
                cx.mark(("f1u", c, ag), "v")
            for j in range(16):
                jp = j % 2
                if j >= 2:
                    g.wait("t", M.get(("f1bm", c, j - 2)))
                else:
                    g.wait("t", M.get(("f1vm", c, 6 + j)))
                cx.dma("g", "w%d" % jp, wA[:, jp, :], W_t[j, :, :])
                mk[("wb1", c, j)] = cx.dma("g", "w%d" % jp,
                                           wA[:, 2 + jp, 0:1024], B_t[j, :, :])
                t.wait_dma(mk[("wb1", c, j)])
                if j == 0:
                    t.wait("v", M.get(("f1u", c, 7)))
                if j >= 2:
                    t.wait("s", M.get(("f1ye", c, j - 2)))
                for di in range(16):
                    t.matmul(pA[jp][:], wA[:, jp, di * 128:(di + 1) * 128],
                             xhat3[:, di, :], start=(di == 0), stop=False)
                for ag in range(8):
                    t.matmul(pA[jp][:], wA[:, 2 + jp, ag * 128:(ag + 1) * 128],
                             me[:, ag, :], start=False, stop=(ag == 7))
                t.inc()
                cx.mark(("f1bm", c, j), "t")
                s.wait("t", M.get(("f1bm", c, j)))
                s.add(yb0[:, j, :], pA[jp][:], biasb[:, j:j + 1])
                s.edge()
                s.activation(ysq0[:], yb0[:, j, :], AF.Square)
                s.inc()
                cx.mark(("f1ye", c, j), "s")
                t.wait("s", M.get(("f1ye", c, j)))
                if j == 0:
                    t.wait("s", M.get((("f1i", c), "es")))
                t.matmul(pB[0][:], C["o128"][:], yb0[:, j, :],
                         start=(j == 0), stop=(j == 15), skip_group_check=True)
                t.matmul(pB[1][:], C["o128"][:], ysq0[:],
                         start=(j == 0), stop=(j == 15), skip_group_check=True)
                t.inc()
                if j == 15:
                    cx.mark((("f1o", c), "s12t"), "t")
            ln_rows(cx, ("f1o", c))
            g.wait("t", M.get(("f1bm", c, 15)))
            if c == 0:
                g.wait("v", M.get(("resv", 3, 15)))
            else:
                g.wait("v", M.get(("f1resv", c - 1, 15)))
            for m in range(16):
                mk[("xr0", c)] = cx.dma(
                    "g", "xr0", xres0f[:, m, :],
                    x3o[:, m * CORE_TOK + c * GT:m * CORE_TOK + (c + 1) * GT])
            for m in range(16):
                mp = m % 2
                if m == 0:
                    v.wait_dma(mk[("xr0", c)])
                    if c >= 1:
                        v.wait("v", M.get(("f1resv", c - 1, 15)))
                    else:
                        v.wait("v", M.get(("resv", 3, 15)))
                else:
                    v.wait("v", M.get(("f1resv", c, m - 1)))
                if m >= 2:
                    v.wait_dma(mk.get(("stx4", c, m - 2)))
                elif c >= 1:
                    v.wait_dma(mk.get(("stx4", c - 1, m + 14)))
                else:
                    v.wait_dma(mk.get(("st", 3, m + 14)))
                v.tensor_sub(t1[:], xres0f[:, m, :], bbc[:])
                v.tensor_mul(t2[:], yb0[:, m, :], abc[:])
                v.edge()
                v.tensor_add(xout[:, mp, :], t1[:], t2[:])
                v.inc()
                cx.mark(("f1resv", c, m), "v")
                sy.wait("v", M.get(("f1resv", c, m)))
                mk[("stx4", c, m)] = cx.dma(
                    "sy", "st%d" % mp,
                    x4o[:, m * CORE_TOK + c * GT:m * CORE_TOK + (c + 1) * GT],
                    xout[:, mp, :])
        drain(("st0", "st1"))
    with nc.Block() as block:
        cx.run(block, body)
    return nc


def _prep_l1b_inputs(inputs, x2_cores, resB):
    wo_t = retile_w(np.ascontiguousarray(
        np.asarray(inputs["L1_attn_wo"], dtype=np.float32).T))
    W_t = retile_w(np.ascontiguousarray(
        np.asarray(inputs["L1_ffn_W"], dtype=np.float32).T))
    A = np.asarray(inputs["L1_ffn_A"], dtype=np.float32)
    A_t = retile_w(np.ascontiguousarray(A.reshape(1024, 2048).T))
    B = np.asarray(inputs["L1_ffn_B"], dtype=np.float32)
    B_flat = np.ascontiguousarray(B.transpose(0, 2, 1).reshape(1024, 2048))
    B_t = retile_w(B_flat, din=1024)
    bias_t = np.ascontiguousarray(
        np.asarray(inputs["L1_ffn_bias"], dtype=np.float32).reshape(16, 128).T)
    owner = {}
    for rr in range(4):
        for li, bb in enumerate(core_blocks(rr)):
            owner[bb] = (rr, li)
    maps = []
    for c in range(8):
        b, r = c // 4, c % 4
        k1gf = np.zeros((NH, 128, SEQ), dtype=BF)
        v1gf = np.zeros((NH, 128, SEQ), dtype=BF)
        for bb in range(64):
            rb, li = owner[bb]
            src = resB[b * 4 + rb]
            k1gf[:, :, bb * 128:(bb + 1) * 128] = src["k1"][:, :, li * 128:(li + 1) * 128]
            v1gf[:, :, bb * 128:(bb + 1) * 128] = src["v1"][:, li * 128:(li + 1) * 128, :]
        qkng = np.concatenate(
            [np.asarray(resB[b * 4 + rr]["qkn"], dtype=np.float32).reshape(1, 32)
             for rr in range(4)], axis=1)
        maps.append(dict(
            x=x2_cores[c], q1o=resB[c]["q1"], k1g=k1gf, v1g=v1gf,
            qkng=np.ascontiguousarray(qkng),
            wo_t=wo_t, W_t=W_t, A_t=A_t, B_t=B_t, bias_t=bias_t))
    return maps


# version 04 (final)
# speedup vs baseline: 2.2409x; 1.6556x over previous
import numpy as np
import ml_dtypes

import concourse.bass as bass
import concourse.mybir as mybir
from concourse.bass_utils import run_bass_kernel_spmd

F32 = mybir.dt.float32
BF16 = mybir.dt.bfloat16
AF = mybir.ActivationFunctionType
ALU = mybir.AluOpType
AX = mybir.AxisListType.X
BF = ml_dtypes.bfloat16

D = 2048
NH = 16
HD = 128
SEQ = 8192
WINDOW = 4096
CORE_TOK = 2048
NG = 4
GT = 512
KSPAN = 36

RUNS = [[0, 7, 8, 9], [1, 6, 10, 11], [2, 5, 12, 13], [3, 4, 14, 15]]
SCALE = float(1.0 / np.sqrt(HD))


def core_blocks(r):
    return [4 * k + j for k in RUNS[r] for j in range(4)]


def core_positions(r):
    return np.concatenate([np.arange(b * 128, b * 128 + 128) for b in core_blocks(r)])


_INVF = 10000.0 ** (-(np.arange(64, dtype=np.float64) * 2.0) / 128.0)


def rope_tables(positions):
    t = np.asarray(positions, dtype=np.float64)
    ang = _INVF[:, None] * t[None, :]
    c, s = np.cos(ang), np.sin(ang)
    cos0 = np.concatenate([c, c], 0).astype(np.float32)
    sin0 = np.concatenate([s, s], 0).astype(np.float32)
    cos1 = np.repeat(c, 2, axis=0).astype(np.float32)
    sin1 = np.repeat(s, 2, axis=0).astype(np.float32)
    return cos0, sin0, cos1, sin1


def rot_mat_l0():
    P = np.zeros((128, 128), dtype=BF)
    for i in range(64):
        P[i + 64, i] = -1.0
        P[i, i + 64] = 1.0
    return P


def rot_mat_l1():
    P = np.zeros((128, 128), dtype=BF)
    for j in range(64):
        P[2 * j + 1, 2 * j] = -1.0
        P[2 * j, 2 * j + 1] = 1.0
    return P


def to_fm(x_tok):
    T = x_tok.shape[0]
    return np.ascontiguousarray(x_tok.reshape(T, D // 128, 128).transpose(1, 2, 0))


def from_fm(x_fm):
    return np.ascontiguousarray(x_fm.transpose(2, 0, 1).reshape(x_fm.shape[2], D))


FVALS = [1e-5, 1e-6, 1.0 / 2048.0, 1.0 / 128.0, -1.0, 1.0, 0.0625,
         -1e30, 1e30, 0.5, 2.0, -0.5, -0.01, 1.0 / 8192.0]


def _consts(nc):
    c_P0 = nc.inline_tensor(rot_mat_l0().view(np.uint16), name="c_P0")
    c_P1 = nc.inline_tensor(rot_mat_l1().view(np.uint16), name="c_P1")
    c_id = nc.inline_tensor(np.eye(128, dtype=BF).view(np.uint16), name="c_id")
    c_o128 = nc.inline_tensor(np.ones((128, 1), dtype=BF).view(np.uint16), name="c_o128")
    c_ocol = nc.inline_tensor(np.ones((1, 128), dtype=np.float32), name="c_ocol")
    c_idf = nc.inline_tensor(np.eye(128, dtype=np.float32), name="c_idf")
    c_fp = nc.inline_tensor(np.array([FVALS], dtype=np.float32), name="c_fp")
    sb = nc.alloc_sbuf_tensor
    t = dict(
        P0=sb("P0", [128, 128], BF16),
        P1=sb("P1", [128, 128], BF16),
        id128=sb("id128", [128, 128], BF16),
        o128=sb("o128", [128, 1], BF16),
        ocol=sb("ocol", [1, 128], F32),
        idf32=sb("idf32", [128, 128], F32),
        fpool=sb("fpool", [1, len(FVALS)], F32),
    )
    for i, v in enumerate(FVALS):
        nc.const_aps.register(F32, float(v), t["fpool"][0:1, i:i + 1])

    t["pairs"] = [
        (t["P0"][:], c_P0.ap().bitcast(BF16)),
        (t["P1"][:], c_P1.ap().bitcast(BF16)),
        (t["id128"][:], c_id.ap().bitcast(BF16)),
        (t["o128"][:], c_o128.ap().bitcast(BF16)),
        (t["ocol"][:], c_ocol.ap()),
        (t["idf32"][:], c_idf.ap()),
        (t["fpool"][:], c_fp.ap()),
    ]

    def load(g, sem):
        for dst, srcap in t["pairs"]:
            g.dma_start(out=dst, in_=srcap).then_inc(sem, 16)

    return t, load, 6


def _dma_rec(nc, sem, marks):
    def dma(eng, out, in_):
        n0 = nc.n_instructions()
        eng.dma_start(out=out, in_=in_).then_inc(sem, 16)
        marks.append(marks[-1] + 16 * (nc.n_instructions() - n0))
        return marks[-1]
    return dma


def build_l0a():
    nc = bass.Bass("TRN2", target_bir_lowering=False, debug=False)
    x = nc.dram_tensor("x", [128, 16 * CORE_TOK], F32, kind="ExternalInput").ap()
    wqkv = nc.dram_tensor("wqkv", [48, 128, 16 * 128], F32, kind="ExternalInput").ap()
    qg = nc.dram_tensor("qg", [1, 128], F32, kind="ExternalInput").ap()
    kg = nc.dram_tensor("kg", [1, 128], F32, kind="ExternalInput").ap()
    cos0 = nc.dram_tensor("cos0", [128, CORE_TOK], F32, kind="ExternalInput").ap()
    sin0 = nc.dram_tensor("sin0", [128, CORE_TOK], F32, kind="ExternalInput").ap()
    q0 = nc.dram_tensor("q0", [NH, 128, CORE_TOK], BF16, kind="ExternalOutput").ap()
    k0 = nc.dram_tensor("k0", [NH, 128, CORE_TOK], BF16, kind="ExternalOutput").ap()
    v0 = nc.dram_tensor("v0", [NH, CORE_TOK, 128], BF16, kind="ExternalOutput").ap()

    C, cload, NC_ = _consts(nc)
    sb = nc.alloc_sbuf_tensor
    xb = sb("xb", [128, 16, CORE_TOK], BF16)
    sqb = sb("sqb", [128, 16, GT], BF16)
    xhat = sb("xhat", [128, 16, CORE_TOK], BF16)
    gq = sb("gq", [1, 128], F32)
    gk = sb("gk", [1, 128], F32)
    coss = sb("coss", [128, CORE_TOK], F32)
    sins = sb("sins", [128, CORE_TOK], F32)
    wt = sb("wt", [128, 2, 16, 128], BF16)
    mean_r = sb("mean_r", [1, GT], F32)
    msq_r = sb("msq_r", [1, GT], F32)
    es_r = sb("es_r", [1, GT], F32)
    var_r = sb("var_r", [1, GT], F32)
    sd_r = sb("sd_r", [1, GT], F32)
    rstd_r = sb("rstd_r", [1, GT], F32)
    mr_r = sb("mr_r", [1, GT], F32)
    abc = sb("abc", [128, GT], F32)
    bbc = sb("bbc", [128, GT], F32)
    sqs = sb("sqs", [128, GT], BF16)
    sdq_r = sb("sdq_r", [1, GT], F32)
    rsq_r = sb("rsq_r", [1, GT], F32)
    grs = sb("grs", [128, GT], F32)
    qn = sb("qn", [128, GT], BF16)
    t1 = sb("t1", [128, GT], F32)
    t2 = sb("t2", [128, GT], F32)
    qout = sb("qout", [128, 2, GT], BF16)
    vout = sb("vout", [128, 2, GT], BF16)

    ps = nc.alloc_psum_tensor
    p_mm = [ps("p_mm0", [128, GT], F32), ps("p_mm1", [128, GT], F32)]
    p_s1 = ps("p_s1", [1, GT], F32)
    p_s2 = ps("p_s2", [1, GT], F32)
    p_var = ps("p_var", [1, GT], F32)
    p_gb = ps("p_gb", [128, GT], F32)
    p_rot = ps("p_rot", [128, GT], F32)
    p_vt = ps("p_vt", [128, 4, 128], BF16)

    sem = nc.alloc_semaphore
    s_ldc = sem("s_ldc")
    s_ldx = sem("s_ldx")
    s_ldw = [sem("s_ldw0"), sem("s_ldw1")]
    s_st = [sem("s_st0"), sem("s_st1")]
    s_tl = sem("s_tl")
    s_sl = sem("s_sl")
    s_vl = sem("s_vl")
    s_tq = sem("s_tq")
    s_vch = sem("s_vch")
    s_sch = sem("s_sch")
    s_sq = sem("s_sq")
    s_vq = sem("s_vq")

    NLDC = NC_ + 4
    ctr = {"vch": 0, "sch": 0}

    def vedge(v, inst):
        ctr["vch"] += 1
        inst.then_inc(s_vch, 1)
        v.wait_ge(s_vch, ctr["vch"])

    def sedge(s, inst):
        ctr["sch"] += 1
        inst.then_inc(s_sch, 1)
        s.wait_ge(s_sch, ctr["sch"])

    def te_mm(i):
        return 4 * i + 1 if i < 128 else 512 + 2 * (i - 128) + 1

    def te_tp(i):
        return 512 + 2 * (i - 128) + 2

    def vq_qn(i):
        return 3 * i + 2 if i < 128 else 384 + 2 * (i - 128) + 1

    def vq_out(i):
        return 3 * i + 3 if i < 128 else 384 + 2 * (i - 128) + 2

    with nc.Block() as block:
        m_ldc, m_ldx = [0], [0]
        m_ldw = [[0], [0]]
        d_ldc = _dma_rec(nc, s_ldc, m_ldc)
        d_ldx = _dma_rec(nc, s_ldx, m_ldx)
        d_ldw = [_dma_rec(nc, s_ldw[0], m_ldw[0]), _dma_rec(nc, s_ldw[1], m_ldw[1])]

        def g_all(g):
            cload(g, s_ldc)
            m_ldc.append(m_ldc[-1] + NC_ * 16)
            d_ldc(g, gq[:], qg)
            d_ldc(g, gk[:], kg)
            d_ldc(g, coss[:], cos0)
            d_ldc(g, sins[:], sin0)
            d_ldx(g, xb[:, :, :].rearrange("p a n -> p (a n)"), x[:, :])
            for m in range(48):
                if m >= 2:
                    g.wait_ge(s_tq, te_mm((m - 2) * 4 + 3))
                d_ldw[m % 2](g, wt[:, m % 2, :, :].rearrange("p a n -> p (a n)"),
                             wqkv[m, :, :])
        block.gpsimd(g_all)

        def t_ln(t):
            t.wait_ge(s_ldc, m_ldc[-1])
            t.wait_ge(s_ldx, m_ldx[-1])
            for g in range(NG):
                if g >= 1:
                    t.wait_ge(s_sl, 4 * (g - 1) + 2)
                for di in range(16):
                    inst = t.matmul(p_s1[:], C["o128"][:],
                                    xb[:, di, g * GT:(g + 1) * GT],
                                    start=(di == 0), stop=(di == 15))
                inst.then_inc(s_tl, 1)
                t.wait_ge(s_sl, 4 * g + 1)
                for di in range(16):
                    inst = t.matmul(p_s2[:], C["o128"][:], sqb[:, di, :],
                                    start=(di == 0), stop=(di == 15))
                inst.then_inc(s_tl, 1)
                t.wait_ge(s_vl, 3 * g + 2)
                t.matmul(p_gb[:], C["ocol"][:], rstd_r[:], start=True, stop=True)
                t.matmul(p_rot[:], C["ocol"][:], mr_r[:], start=True,
                         stop=True).then_inc(s_tl, 1)
        block.tensor(t_ln)

        def s_ln(s):
            s.wait_ge(s_ldx, m_ldx[-1])
            for g in range(NG):
                if g >= 1:
                    s.wait_ge(s_tl, 3 * (g - 1) + 2)
                for di in range(16):
                    inst = s.activation(sqb[:, di, :], xb[:, di, g * GT:(g + 1) * GT],
                                        AF.Square)
                inst.then_inc(s_sl, 1)
                s.wait_ge(s_tl, 3 * g + 1)
                sedge(s, s.mul(mean_r[:], p_s1[0:1, :], 1.0 / D))
                s.activation(msq_r[:], mean_r[:], AF.Square).then_inc(s_sl, 1)
                s.wait_ge(s_tl, 3 * g + 2)
                s.mul(es_r[:], p_s2[0:1, :], 1.0 / D).then_inc(s_sl, 1)
                s.wait_ge(s_vl, 3 * g + 1)
                s.activation(sd_r[:], var_r[:], AF.Sqrt, bias=1e-5).then_inc(s_sl, 1)
        block.scalar(s_ln)

        def v_ln(v):
            for g in range(NG):
                v.wait_ge(s_sl, 4 * g + 3)
                v.tensor_sub(var_r[:], es_r[:], msq_r[:]).then_inc(s_vl, 1)
                v.wait_ge(s_sl, 4 * g + 4)
                vedge(v, v.reciprocal(rstd_r[:], sd_r[:]))
                v.tensor_mul(mr_r[:], mean_r[:], rstd_r[:]).then_inc(s_vl, 1)
                v.wait_ge(s_tl, 3 * g + 3)
                v.tensor_copy(abc[:], p_gb[:])
                vedge(v, v.tensor_copy(bbc[:], p_rot[:]))
                for di in range(16):
                    inst = v.tensor_mul(xhat[:, di, g * GT:(g + 1) * GT],
                                        xb[:, di, g * GT:(g + 1) * GT], abc[:])
                vedge(v, inst)
                for di in range(16):
                    inst = v.tensor_sub(xhat[:, di, g * GT:(g + 1) * GT],
                                        xhat[:, di, g * GT:(g + 1) * GT], bbc[:])
                inst.then_inc(s_vl, 1)
        block.vector(v_ln)

        def t_qk(t):
            t.wait_ge(s_vl, 3 * NG)
            for m in range(48):
                t.wait_ge(s_ldw[m % 2], m_ldw[m % 2][m // 2 + 1])
                isv = m >= 32
                for tc in range(NG):
                    i = m * 4 + tc
                    par = i % 2
                    if i >= 2:
                        t.wait_ge(s_vq, vq_qn(i - 2))
                    for di in range(16):
                        inst = t.matmul(p_mm[par][:], wt[:, m % 2, di, :],
                                        xhat[:, di, tc * GT:(tc + 1) * GT],
                                        start=(di == 0), stop=(di == 15))
                    inst.then_inc(s_tq, 1)
                    if not isv:
                        t.wait_ge(s_sq, 2 * i + 1)
                        t.matmul(p_var[:], C["o128"][:], sqs[:], start=True,
                                 stop=True).then_inc(s_tq, 1)
                        t.wait_ge(s_vq, 3 * i + 1)
                        gam = gq if m < 16 else gk
                        t.matmul(p_gb[:], gam[:], rsq_r[:], start=True,
                                 stop=True).then_inc(s_tq, 1)
                        t.wait_ge(s_vq, 3 * i + 2)
                        t.matmul(p_rot[:], C["P0"][:], qn[:], start=True,
                                 stop=True).then_inc(s_tq, 1)
                    else:
                        t.wait_ge(s_vq, vq_qn(i))
                        for blk in range(4):
                            inst = t.transpose(p_vt[:, blk, :],
                                               qn[:, blk * 128:(blk + 1) * 128],
                                               C["id128"][:])
                        inst.then_inc(s_tq, 1)
        block.tensor(t_qk)

        def s_qk(s):
            for m in range(32):
                for tc in range(NG):
                    i = m * 4 + tc
                    par = i % 2
                    s.wait_ge(s_tq, 4 * i + 1)
                    if i >= 1:
                        s.wait_ge(s_tq, 4 * (i - 1) + 2)
                    s.activation(sqs[:], p_mm[par][:], AF.Square).then_inc(s_sq, 1)
                    s.wait_ge(s_tq, 4 * i + 2)
                    if i >= 1:
                        s.wait_ge(s_vq, 3 * (i - 1) + 1)
                    s.activation(sdq_r[:], p_var[0:1, :], AF.Sqrt, scale=1.0 / HD,
                                 bias=1e-6).then_inc(s_sq, 1)
        block.scalar(s_qk)

        m_st = [[0], [0]]
        d_st = [_dma_rec(nc, s_st[0], m_st[0]), _dma_rec(nc, s_st[1], m_st[1])]

        def sy_qk(sy):
            for m in range(48):
                for tc in range(NG):
                    i = m * 4 + tc
                    par = i % 2
                    sy.wait_ge(s_vq, vq_out(i))
                    if m < 16:
                        d_st[par](sy, q0[m, :, tc * GT:(tc + 1) * GT], qout[:, par, :])
                    elif m < 32:
                        d_st[par](sy, k0[m - 16, :, tc * GT:(tc + 1) * GT], qout[:, par, :])
                    else:
                        d_st[par](sy, v0[m - 32, tc * GT:(tc + 1) * GT, :]
                                  .rearrange("(b p) d -> p b d", p=128),
                                  vout[:, par, :].rearrange("p (b d) -> p b d", b=4))
            sy.wait_ge(s_st[0], m_st[0][-1])
            sy.wait_ge(s_st[1], m_st[1][-1])
        block.sync(sy_qk)

        def v_qk(v):
            for m in range(48):
                isv = m >= 32
                for tc in range(NG):
                    i = m * 4 + tc
                    par = i % 2
                    if not isv:
                        v.wait_ge(s_sq, 2 * i + 2)
                        v.reciprocal(rsq_r[:], sdq_r[:]).then_inc(s_vq, 1)
                        v.wait_ge(s_tq, 4 * i + 3)
                        vedge(v, v.tensor_copy(grs[:], p_gb[:]))
                        v.tensor_mul(qn[:], p_mm[par][:], grs[:]).then_inc(s_vq, 1)
                        v.wait_ge(s_tq, 4 * i + 4)
                        v.tensor_mul(t1[:], qn[:], coss[:, tc * GT:(tc + 1) * GT])
                        vedge(v, v.tensor_mul(t2[:], p_rot[:], sins[:, tc * GT:(tc + 1) * GT]))
                        if i >= 2:
                            v.wait_ge(s_st[par], m_st[par][i // 2])
                        v.tensor_add(qout[:, par, :], t1[:], t2[:]).then_inc(s_vq, 1)
                    else:
                        v.wait_ge(s_tq, te_mm(i))
                        v.tensor_copy(qn[:], p_mm[par][:]).then_inc(s_vq, 1)
                        v.wait_ge(s_tq, te_tp(i))
                        if i >= 2:
                            v.wait_ge(s_st[par], m_st[par][i // 2])
                        v.tensor_copy(vout[:, par, :].rearrange("p (b n) -> p b n", b=4),
                                      p_vt[:, :, :]).then_inc(s_vq, 1)
        block.vector(v_qk)


    return nc


def hm_layout(x_tok):
    T = x_tok.shape[0]
    return np.ascontiguousarray(
        x_tok.reshape(T, 16, 128).transpose(2, 1, 0).reshape(128, 16 * T))


def retile_w(w, din=D):
    nm, nd = w.shape[1] // 128, din // 128
    return np.ascontiguousarray(
        w.reshape(nd, 128, nm, 128).transpose(2, 1, 0, 3).reshape(nm, 128, nd * 128))


def _prep_l0a_inputs(inputs):
    x_full = np.asarray(inputs["L0_attn_x"], dtype=np.float32)
    wqkv = retile_w(np.asarray(inputs["L0_attn_w_qkv"], dtype=np.float32))
    qg = np.asarray(inputs["L0_attn_q_gamma"], dtype=np.float32).reshape(1, 128)
    kg = np.asarray(inputs["L0_attn_k_gamma"], dtype=np.float32).reshape(1, 128)
    maps = []
    for c in range(8):
        b, r = c // 4, c % 4
        pos = core_positions(r)
        cos0, sin0, _, _ = rope_tables(pos)
        x_slice = x_full[b][pos]
        sin0s = sin0.copy()
        sin0s[0:64] *= -1.0
        maps.append(dict(x=hm_layout(x_slice), wqkv=wqkv, qg=qg, kg=kg,
                         cos0=np.ascontiguousarray(cos0),
                         sin0=np.ascontiguousarray(sin0s)))
    return maps


def run_l0a(inputs):
    nc = build_l0a2()
    maps = _prep_l0a_inputs(inputs)
    res = run_bass_kernel_spmd(nc, maps, core_ids=list(range(8)))
    return res.results


def kernel(**inputs):
    ncA = build_l0a2()
    resA = run_bass_kernel_spmd(ncA, _prep_l0a_inputs(inputs),
                                core_ids=list(range(8))).results
    ncB = build_l0b()
    resB = run_bass_kernel_spmd(ncB, _prep_l0b_inputs(inputs, resA),
                                core_ids=list(range(8))).results
    x2_cores = [np.asarray(resB[c]["x2o"]) for c in range(8)]
    ncC = build_l1b()
    resC = run_bass_kernel_spmd(ncC, _prep_l1b_inputs(inputs, x2_cores, resB),
                                core_ids=list(range(8))).results
    out = np.zeros((2, SEQ, D), dtype=np.float32)
    for c in range(8):
        b, r = c // 4, c % 4
        pos = core_positions(r)
        x4 = np.asarray(resC[c]["x4o"], dtype=np.float32)
        out[b][pos] = from_fm(x4.reshape(128, 16, CORE_TOK).transpose(1, 0, 2))
    return out



class _EngProxy:

    def __init__(self, cx, key):
        self._cx = cx
        self._key = key

    def live(self):
        return self._cx.mode == self._key

    def __getattr__(self, name):
        cx, key = self._cx, self._key

        def f(*a, **k):
            if cx.mode == key:
                cx.last[key] = getattr(cx.real, name)(*a, **k)
            return None

        return f

    def wait(self, sem_key, val):
        if val is None or val <= 0:
            return
        if self.live():
            self._cx.real.wait_ge(self._cx.sems[sem_key], val)

    def wait_dma(self, mark):
        if mark is None:
            assert self._cx.mode == "plan" or not self.live(), \
                "wait_dma on a mark from a not-yet-emitted engine"
            return
        if self.live():
            sem, val = mark
            if val > 0:
                self._cx.real.wait_ge(sem, val)

    def inc(self):
        cx, key = self._cx, self._key
        cx.ev[key] += 1
        if cx.mode == key:
            cx.last[key].then_inc(cx.sems[key], 1)
        return cx.ev[key]

    def edge(self):
        cx, key = self._cx, self._key
        ck = key + "ch"
        cx.ev[ck] = cx.ev.get(ck, 0) + 1
        if cx.mode == key:
            cx.last[key].then_inc(cx.sems[ck], 1)
            cx.real.wait_ge(cx.sems[ck], cx.ev[ck])


class Cx:

    ENGS = ("g", "sy", "t", "s", "v")

    def __init__(self, nc):
        self.nc = nc
        self.M = {}
        self.dmarks = {}
        self.mode = "plan"
        self.real = None
        self.last = {}
        self.ev = {k: 0 for k in self.ENGS}
        self.sems = {}
        for k in list(self.ENGS) + ["tch", "sch", "vch"]:
            self.sems[k] = nc.alloc_semaphore("se_" + k)
        self._dsems = {}
        for e in self.ENGS:
            setattr(self, e, _EngProxy(self, e))

    def begin(self, mode, real=None):
        self.mode = mode
        self.real = real
        self.ev = {k: 0 for k in self.ENGS}
        for k in ("tch", "sch", "vch"):
            self.ev[k] = 0
        self._dcnt = {k: 0 for k in self.dmarks}
        self._emitted = getattr(self, "_emitted", set())
        if mode != "plan":
            self._emitted.add(mode)

    def mark(self, name, eng):
        if self.mode == "plan":
            self.M[name] = self.ev[eng]
        return self.M.get(name)

    def dma(self, eng_key, slot, out, in_):
        if slot not in self._dsems:
            self._dsems[slot] = self.nc.alloc_semaphore("sd_" + slot)
            self.dmarks[slot] = [0]
            self._dcnt[slot] = 0
        self._dcnt[slot] += 1
        n = self._dcnt[slot]
        if self.mode == eng_key:
            n0 = self.nc.n_instructions()
            self.real.dma_start(out=out, in_=in_).then_inc(self._dsems[slot], 16)
            assert len(self.dmarks[slot]) == n, (slot, n, len(self.dmarks[slot]))
            self.dmarks[slot].append(
                self.dmarks[slot][-1] + 16 * (self.nc.n_instructions() - n0))
        if self.mode == "plan" or len(self.dmarks[slot]) <= n:
            return None
        return (self._dsems[slot], self.dmarks[slot][n])

    def run(self, block, body):
        self.begin("plan")
        body(self)
        order = [("sy", block.sync), ("g", block.gpsimd), ("t", block.tensor),
                 ("s", block.scalar), ("v", block.vector)]
        for key, starter in order:
            def f(real, key=key):
                self.begin(key, real)
                body(self)
            starter(f)



NEGC_SCALE = float(-np.sqrt(128.0) * 1.002)


def build_l0b(upto="l1a"):
    nc = bass.Bass("TRN2", target_bir_lowering=False, debug=False)
    x = nc.dram_tensor("x", [128, 16 * CORE_TOK], F32, kind="ExternalInput").ap()
    q0o = nc.dram_tensor("q0o", [NH, 128, CORE_TOK], BF16, kind="ExternalInput").ap()
    k0g = nc.dram_tensor("k0g", [NG, NH, 128, KSPAN * 128], BF16, kind="ExternalInput").ap()
    v0g = nc.dram_tensor("v0g", [NG, NH, 128, KSPAN * 128], BF16, kind="ExternalInput").ap()
    masks = nc.dram_tensor("masks", [NG, 128, KSPAN * GT], BF16, kind="ExternalInput").ap()
    qg = nc.dram_tensor("qg", [1, 128], F32, kind="ExternalInput").ap()
    kg = nc.dram_tensor("kg", [1, 128], F32, kind="ExternalInput").ap()
    wout = nc.dram_tensor("wout", [16, 128, 16 * 128], F32, kind="ExternalInput").ap()
    wg_t = nc.dram_tensor("wg_t", [64, 128, 16 * 128], F32, kind="ExternalInput").ap()
    wu_t = nc.dram_tensor("wu_t", [64, 128, 16 * 128], F32, kind="ExternalInput").ap()
    wd_t = nc.dram_tensor("wd_t", [16, 128, 64 * 128], F32, kind="ExternalInput").ap()
    wq_t = nc.dram_tensor("wq_t", [16, 128, 16 * 128], F32, kind="ExternalInput").ap()
    wk_t = nc.dram_tensor("wk_t", [16, 128, 16 * 128], F32, kind="ExternalInput").ap()
    wv_t = nc.dram_tensor("wv_t", [16, 128, 16 * 128], F32, kind="ExternalInput").ap()
    cos1 = nc.dram_tensor("cos1", [128, CORE_TOK], F32, kind="ExternalInput").ap()
    sin1 = nc.dram_tensor("sin1", [128, CORE_TOK], F32, kind="ExternalInput").ap()
    x1o = nc.dram_tensor("x1o", [128, 16 * CORE_TOK], F32, kind="ExternalOutput").ap()
    x2o = nc.dram_tensor("x2o", [128, 16 * CORE_TOK], F32, kind="ExternalOutput").ap()
    q1 = nc.dram_tensor("q1", [NH, 128, CORE_TOK], BF16, kind="ExternalOutput").ap()
    k1 = nc.dram_tensor("k1", [NH, 128, CORE_TOK], BF16, kind="ExternalOutput").ap()
    v1 = nc.dram_tensor("v1", [NH, CORE_TOK, 128], BF16, kind="ExternalOutput").ap()
    qkn = nc.dram_tensor("qkn", [1, 2 * NH], F32, kind="ExternalOutput").ap()

    C, cload, NC_ = _consts(nc)
    sb = nc.alloc_sbuf_tensor
    gq_s = sb("gq_s", [1, 128], F32)
    gk_s = sb("gk_s", [1, 128], F32)
    coss = sb("coss", [128, CORE_TOK], F32)
    sins = sb("sins", [128, CORE_TOK], F32)
    wA = sb("wA", [128, 4, 16 * 128], BF16)
    rows = sb("rows", [1, 12, GT], F32)
    (mean_r, msq_r, es_r, var_r, sd_r, rstd_r, mr_r, den_r, sdq_r, rsq_r,
     nrm_r, spare_r) = [rows[0:1, i, :] for i in range(12)]
    tiny = sb("tiny", [1, 16], F32)
    negc = sb("negc", [128, 1], F32)
    abc = sb("abc", [128, GT], F32)
    bbc = sb("bbc", [128, GT], F32)
    qb = sb("qb", [128, 2, GT], BF16)
    probs = sb("probs", [128, 4, GT], BF16)
    pp = sb("pp", [128, 2, GT], BF16)
    t1 = sb("t1", [128, GT], F32)
    t2 = sb("t2", [128, GT], F32)
    qn = sb("qn", [128, GT], BF16)
    sqs = sb("sqs", [128, GT], BF16)
    qout = sb("qout", [128, 2, GT], BF16)
    vout = sb("vout", [128, 2, GT], BF16)
    rbc = sb("rbc", [128, GT], F32)
    nacc = sb("nacc", [1, 2 * NH], F32)
    xout = sb("xout", [128, 2, GT], F32)

    ARENA = 112 * 1024
    arena = sb("arena", [128, ARENA], mybir.dt.uint8)
    abase = nc.lookup_mloc(arena).addr

    def at(name, shape, dtype, off):
        return nc.alloc_sbuf_tensor_at(name, shape, dtype, offset=abase + off)

    kbuf = at("kbuf", [128, 2, KSPAN * 128], BF16, 0)
    vbuf = at("vbuf", [128, 2, KSPAN * 128], BF16, 18 * 1024)
    mbuf = at("mbuf", [128, KSPAN, GT], BF16, 36 * 1024)
    ao = at("ao", [128, 16, GT], BF16, 72 * 1024)
    yb = at("yb", [128, 16, GT], BF16, 88 * 1024)
    ysq = at("ysq", [128, GT], BF16, 104 * 1024)
    xres = at("xres", [128, 16, GT], F32, 36 * 1024)
    xb1 = at("xb1", [128, 16, GT], BF16, 0)
    xhat1 = at("xhat1", [128, 16, GT], BF16, 16 * 1024)
    gated = at("gated", [128, 64, GT], BF16, 32 * 1024)
    wB = at("wB", [128, 2, 32 * 128], BF16, 96 * 1024)
    yb0 = at("yb0", [128, 16, GT], BF16, 0)
    ysq0 = at("ysq0", [128, GT], BF16, 16 * 1024)
    xres0 = at("xres0", [128, 2, GT], F32, 17 * 1024)
    xres0f = at("xres0f", [128, 16, GT], F32, 32 * 1024)
    xb2 = at("xb2", [128, 16, GT], BF16, 0)
    xhat2 = at("xhat2", [128, 16, CORE_TOK], BF16, 16 * 1024)
    sql1 = at("sql1", [128, 16, GT], BF16, 80 * 1024)
    at0 = dict(xres0f=xres0f, sql1=sql1)

    ps = nc.alloc_psum_tensor
    pS = ps("pS", [128, 4, GT], F32)
    pA = [pS[:, 0, :], pS[:, 1, :], pS[:, 2, :], pS[:, 3, :]]
    pO = ps("pO", [128, GT], F32)
    pB = [ps(f"pB{i}", [1, GT], F32) for i in range(2)]
    pC = ps("pC", [128, 4, 128], BF16)
    pD = pO

    cx = Cx(nc)
    M = cx.M

    def ln_rows(cx, key, bca=pA[2], bcb=pA[3]):
        s, v, t = cx.s, cx.v, cx.t
        s.wait("t", M.get((key, "s12t")))
        s.mul(mean_r, pB[0][0:1, :], 1.0 / D)
        s.edge()
        s.activation(msq_r, mean_r, AF.Square)
        s.edge()
        s.mul(es_r, pB[1][0:1, :], 1.0 / D)
        s.inc()
        cx.mark((key, "es"), "s")
        v.wait("s", M.get((key, "es")))
        v.tensor_sub(var_r, es_r, msq_r)
        v.inc()
        cx.mark((key, "var"), "v")
        s.wait("v", M.get((key, "var")))
        s.activation(sd_r, var_r, AF.Sqrt, bias=1e-5)
        s.inc()
        cx.mark((key, "sd"), "s")
        v.wait("s", M.get((key, "sd")))
        v.reciprocal(rstd_r, sd_r)
        v.edge()
        v.tensor_mul(mr_r, mean_r, rstd_r)
        v.inc()
        cx.mark((key, "mr"), "v")
        t.wait("v", M.get((key, "mr")))
        t.matmul(bca[:], C["ocol"][:], rstd_r, start=True, stop=True)
        t.matmul(bcb[:], C["ocol"][:], mr_r, start=True, stop=True)
        t.inc()
        cx.mark((key, "bc"), "t")
        v.wait("t", M.get((key, "bc")))
        v.tensor_copy(abc[:], bca[:])
        v.tensor_copy(bbc[:], bcb[:])
        v.inc()
        cx.mark((key, "abv"), "v")
        v.wait("v", M.get((key, "abv")))

    def body(cx):
        g, sy, t, s, v = cx.g, cx.sy, cx.t, cx.s, cx.v
        mk = {}
        for dst, srcap in C["pairs"]:
            mk["c"] = cx.dma("g", "c", dst, srcap)
        cx.dma("g", "c", gq_s[:], qg)
        cx.dma("g", "c", gk_s[:], kg)
        cx.dma("g", "c", coss[:], cos1)
        mk["c"] = cx.dma("g", "c", sins[:], sin1)

        v.wait_dma(mk["c"])
        v.memset(nacc[:], 0.0)
        v.reduce_max(tiny[0:1, 0:1], gq_s[:], axis=AX, apply_absolute_value=True)
        v.edge()
        v.reduce_max(tiny[0:1, 1:2], gk_s[:], axis=AX, apply_absolute_value=True)
        v.edge()
        v.tensor_mul(tiny[0:1, 2:3], tiny[0:1, 0:1], tiny[0:1, 1:2])
        v.inc()
        cx.mark("negv", "v")
        s.wait("v", M.get("negv"))
        s.activation(tiny[0:1, 3:4], tiny[0:1, 2:3], AF.Copy,
                     scale=NEGC_SCALE, bias=-0.01)
        s.inc()
        cx.mark("negs", "s")
        t.wait("s", M.get("negs"))
        t.wait_dma(mk["c"])
        t.matmul(pD[:, 0:1], C["ocol"][:], tiny[0:1, 3:4], start=True, stop=True)
        t.inc()
        cx.mark("negt", "t")
        v.wait("t", M.get("negt"))
        v.tensor_copy(negc[:], pD[:, 0:1])
        v.inc()
        cx.mark("negbc", "v")

        for gi in range(NG):
            if gi >= 1:
                g.wait("v", M.get(("resv", gi - 1, 15)))
            mk["m"] = cx.dma("g", "m", mbuf[:, :, :].rearrange("p a n -> p (a n)"),
                             masks[gi, :, :])
            for h in range(NH):
                a = gi * NH + h
                hp = a % 2
                if a >= 2:
                    g.wait("t", M.get(("att_t_end", a - 2)))
                cx.dma("g", "k%d" % hp, kbuf[:, hp, :], k0g[gi, h, :, :])
                cx.dma("g", "k%d" % hp, vbuf[:, hp, :], v0g[gi, h, :, :])
                mk[("kvq", a)] = cx.dma("g", "k%d" % hp, qb[:, hp, :],
                                        q0o[h, :, gi * GT:(gi + 1) * GT])
                t.wait_dma(mk[("kvq", a)])
                NP = KSPAN // 2
                LAG = 2

                def pvden_b(q, a=a, hp=hp, gi=gi, h=h):
                    t.wait("v", M.get(("pm", a, q)))
                    for i in (2 * q, 2 * q + 1):
                        if i == 0:
                            if h == 0 and gi >= 1:
                                t.wait("s", M.get(("opmean", gi - 1)))
                        t.matmul(pO[:], vbuf[:, hp, i * 128:(i + 1) * 128],
                                 probs[:, i % 4, :], start=(i == 0),
                                 stop=(i == KSPAN - 1), skip_group_check=True)
                        t.inc()
                        cx.mark(("pv", a, i), "t")
                    t.wait("v", M.get(("padd", a, q)))
                    t.matmul(pB[0][:], C["o128"][:], pp[:, q % 2, :],
                             start=(q == 0), stop=(q == NP - 1),
                             skip_group_check=True)
                    t.inc()
                    cx.mark(("den", a, q), "t")

                for p in range(NP + LAG):
                    if p < NP:
                        for i in (2 * p, 2 * p + 1):
                            sl4 = i % 4
                            if i % 2 == 0:
                                if p >= 2:
                                    t.wait("s", M.get(("exp", a, p - 2)))
                                elif a >= 1:
                                    t.wait("s", M.get(("exp", a - 1, NP - 2 + p)))
                                if p == 0 and a >= 1:
                                    t.wait("v", M.get(("aov", a - 1)))
                                if p == 0 and h == 0 and gi >= 1:
                                    t.wait("s", M.get(("ye", gi - 1, 15)))
                                    t.wait("v", M.get((("op", gi - 1), "abv")))
                            t.matmul(pS[:, sl4, :], kbuf[:, hp, i * 128:(i + 1) * 128],
                                     qb[:, hp, :], start=True, stop=True)
                            t.inc()
                            cx.mark(("sc", a, i), "t")
                    if p >= LAG:
                        pvden_b(p - LAG)
                t.wait("v", M.get(("rec", a)))
                t.wait("s", M.get(("exp", a, NP - 1)))
                t.matmul(pS[:, 0, :], C["ocol"][:], den_r, start=True, stop=True)
                t.inc()
                cx.mark(("rbc", a), "t")
                cx.mark(("att_t_end", a), "t")
                if a == 0:
                    s.wait("v", M.get("negbc"))
                for p in range(KSPAN // 2):
                    NP = KSPAN // 2
                    sl4 = (2 * p) % 4
                    s.wait("t", M.get(("sc", a, 2 * p + 1)))
                    if p >= 2:
                        s.wait("t", M.get(("pv", a, 2 * p - 3)))
                        s.wait("v", M.get(("padd", a, p - 2)))
                    elif a >= 1:
                        s.wait("t", M.get(("pv", a - 1, KSPAN - 3 + 2 * p)))
                        s.wait("v", M.get(("padd", a - 1, NP - 2 + p)))
                    s.activation(probs[:, sl4:sl4 + 2, :].rearrange("p a n -> p (a n)"),
                                 pS[:, sl4:sl4 + 2, :].rearrange("p a n -> p (a n)"),
                                 AF.Exp, scale=SCALE, bias=negc[:, 0:1])
                    s.inc()
                    cx.mark(("exp", a, p), "s")
                if h == 0:
                    v.wait_dma(mk["m"])
                for p in range(KSPAN // 2):
                    NP = KSPAN // 2
                    sl4 = (2 * p) % 4
                    v.wait("s", M.get(("exp", a, p)))
                    v.tensor_mul(probs[:, sl4:sl4 + 2, :].rearrange("p a n -> p (a n)"),
                                 probs[:, sl4:sl4 + 2, :].rearrange("p a n -> p (a n)"),
                                 mbuf[:, 2 * p:2 * p + 2, :].rearrange("p a n -> p (a n)"))
                    v.inc()
                    cx.mark(("pm", a, p), "v")
                    v.wait("v", M.get(("pm", a, p)))
                    if p >= 2:
                        v.wait("t", M.get(("den", a, p - 2)))
                    elif a >= 1:
                        v.wait("t", M.get(("den", a - 1, NP - 2 + p)))
                    v.tensor_add(pp[:, p % 2, :], probs[:, sl4, :], probs[:, sl4 + 1, :])
                    v.inc()
                    cx.mark(("padd", a, p), "v")
                v.wait("t", M.get(("den", a, KSPAN // 2 - 1)))
                v.reciprocal(den_r, pB[0][0:1, :])
                v.inc()
                cx.mark(("rec", a), "v")
                v.wait("t", M.get(("rbc", a)))
                if a >= 1:
                    v.wait("v", M.get(("aov", a - 1)))
                v.tensor_copy(rbc[:], pS[:, 0, :])
                v.edge()
                v.tensor_mul(ao[:, h, :], pO[:], rbc[:])
                v.inc()
                cx.mark(("aov", a), "v")
            cx.mark(("attv_end", gi), "v")

            for m in range(16):
                mp = m % 2
                if m >= 2:
                    g.wait("t", M.get(("yt", gi, m - 2)))
                else:
                    g.wait("t", M.get(("att_t_end", gi * NH + NH - 1)))
                mk[("w", gi, m)] = cx.dma("g", "w%d" % mp, wA[:, mp, :], wout[m, :, :])
                mk[("xr", gi)] = cx.dma(
                    "g", "xr", xres[:, m, :],
                    x[:, m * CORE_TOK + gi * GT:m * CORE_TOK + (gi + 1) * GT])
                t.wait_dma(mk[("w", gi, m)])
                if m >= 2:
                    t.wait("s", M.get(("ye", gi, m - 2)))
                if m == 0:
                    t.wait("v", M.get(("aov", gi * NH + NH - 1)))
                for di in range(16):
                    t.matmul(pA[mp][:], wA[:, mp, di * 128:(di + 1) * 128],
                             ao[:, di, :], start=(di == 0), stop=(di == 15))
                t.inc()
                cx.mark(("yt", gi, m), "t")
                s.wait("t", M.get(("yt", gi, m)))
                s.copy(yb[:, m, :], pA[mp][:])
                s.edge()
                s.activation(ysq[:], yb[:, m, :], AF.Square)
                s.inc()
                cx.mark(("ye", gi, m), "s")
                t.wait("s", M.get(("ye", gi, m)))
                if m == 0 and gi >= 1:
                    t.wait("s", M.get(("opes", gi - 1)))
                t.matmul(pB[0][:], C["o128"][:], yb[:, m, :],
                         start=(m == 0), stop=(m == 15), skip_group_check=True)
                t.matmul(pB[1][:], C["o128"][:], ysq[:],
                         start=(m == 0), stop=(m == 15), skip_group_check=True)
                t.inc()
                if m == 15:
                    cx.mark((("op", gi), "s12t"), "t")
            ln_rows(cx, ("op", gi))
            if cx.mode == "plan":
                M[("opmean", gi)] = M[(("op", gi), "es")]
                M[("opes", gi)] = M[(("op", gi), "es")]
            for m in range(16):
                mp = m % 2
                if m == 0:
                    v.wait_dma(mk[("xr", gi)])
                    if gi >= 1:
                        v.wait("v", M.get(("resv", gi - 1, 15)))
                else:
                    v.wait("v", M.get(("resv", gi, m - 1)))
                if m >= 2:
                    v.wait_dma(mk.get(("st", gi, m - 2)))
                elif gi >= 1:
                    v.wait_dma(mk.get(("st", gi - 1, m + 14)))
                v.tensor_sub(t1[:], xres[:, m, :], bbc[:])
                v.tensor_mul(t2[:], yb[:, m, :], abc[:])
                v.edge()
                v.tensor_add(xout[:, mp, :], t1[:], t2[:])
                v.inc()
                cx.mark(("resv", gi, m), "v")
                sy.wait("v", M.get(("resv", gi, m)))
                mk[("st", gi, m)] = cx.dma(
                    "sy", "st%d" % mp,
                    x1o[:, m * CORE_TOK + gi * GT:m * CORE_TOK + (gi + 1) * GT],
                    xout[:, mp, :])
        def drain(slots):
            for sl in slots:
                if cx.mode != "plan" and sl in cx.dmarks:
                    sy.wait_dma((cx._dsems[sl], cx.dmarks[sl][-1]))

        if upto == "att":
            drain(("st0", "st1"))
            return

        xres0f = at0["xres0f"]
        for c in range(NG):
            g.wait("t", M.get(("att_t_end", 63)) if c == 0 else None)
            if c >= 1:
                g.wait("v", M.get(("f0resv", c - 1, 15)))
            g.wait_dma(mk.get(("st", c, 14)))
            g.wait_dma(mk.get(("st", c, 15)))
            mk[("xb1", c)] = cx.dma(
                "g", "xc", xb1[:, :, :],
                x1o[:, :].rearrange("p (a n) -> p a n", a=16)[:, :, c * GT:(c + 1) * GT])
            t.wait_dma(mk[("xb1", c)])
            if c == 0:
                t.wait("s", M.get((("op", 3), "es")))
            else:
                t.wait("s", M.get((("f0o", c - 1), "es")))
            for di in range(16):
                t.matmul(pB[0][:], C["o128"][:], xb1[:, di, :],
                         start=(di == 0), stop=(di == 15), skip_group_check=True)
            t.inc()
            cx.mark(("f0s1", c), "t")
            s.wait_dma(mk[("xb1", c)])
            if c >= 1:
                s.wait("t", M.get(("f0s2", c - 1)))
            for di in range(16):
                s.activation(xhat1[:, di, :], xb1[:, di, :], AF.Square)
            s.inc()
            cx.mark(("f0sq", c), "s")
            t.wait("s", M.get(("f0sq", c)))
            for di in range(16):
                t.matmul(pB[1][:], C["o128"][:], xhat1[:, di, :],
                         start=(di == 0), stop=(di == 15), skip_group_check=True)
            t.inc()
            cx.mark(("f0s2", c), "t")
            if cx.mode == "plan":
                M[(("f0i", c), "s12t")] = M[("f0s2", c)]
            ln_rows(cx, ("f0i", c))
            for di in range(16):
                if di == 0:
                    v.wait("t", M.get(("f0s2", c)))
                v.tensor_mul(xhat1[:, di, :], xb1[:, di, :], abc[:])
            v.edge()
            for di in range(16):
                v.tensor_sub(xhat1[:, di, :], xhat1[:, di, :], bbc[:])
            v.inc()
            cx.mark(("f0xh", c), "v")
            for ht in range(64):
                hp = ht % 2
                if ht >= 2:
                    g.wait("t", M.get(("f0g", c, ht - 2)))
                elif c == 0:
                    g.wait("t", M.get(("yt", 3, 14 + ht)))
                else:
                    g.wait("t", M.get(("f0g", c - 1, 62 + ht)))
                cx.dma("g", "wg%d" % hp, wA[:, hp, :], wg_t[ht, :, :])
                mk[("wgu", c, ht)] = cx.dma("g", "wg%d" % hp, wA[:, 2 + hp, :],
                                            wu_t[ht, :, :])
                t.wait_dma(mk[("wgu", c, ht)])
                if ht == 0:
                    t.wait("v", M.get(("f0xh", c)))
                if ht >= 1:
                    t.wait("s", M.get(("f0sl", c, ht - 1)))
                for di in range(16):
                    t.matmul(pA[0][:], wA[:, hp, di * 128:(di + 1) * 128],
                             xhat1[:, di, :], start=(di == 0), stop=(di == 15))
                t.inc()
                cx.mark(("f0gm", c, ht), "t")
                if ht >= 1:
                    t.wait("v", M.get(("f0gt", c, ht - 1)))
                for di in range(16):
                    t.matmul(pA[1][:], wA[:, 2 + hp, di * 128:(di + 1) * 128],
                             xhat1[:, di, :], start=(di == 0), stop=(di == 15))
                t.inc()
                cx.mark(("f0g", c, ht), "t")
                s.wait("t", M.get(("f0gm", c, ht)))
                if ht >= 1:
                    s.wait("v", M.get(("f0gt", c, ht - 1)))
                s.activation(sqs[:], pA[0][:], AF.Silu)
                s.inc()
                cx.mark(("f0sl", c, ht), "s")
                v.wait("s", M.get(("f0sl", c, ht)))
                v.wait("t", M.get(("f0g", c, ht)))
                if ht <= 1 and c >= 1:
                    v.wait("v", M.get(("f0resv", c - 1, 15)))
                v.tensor_mul(gated[:, ht, :], sqs[:], pA[1][:])
                v.inc()
                cx.mark(("f0gt", c, ht), "v")
            for j in range(16):
                jp = j % 2
                for half in range(2):
                    wp = (2 * j + half) % 2
                    if 2 * j + half >= 2:
                        g.wait("t", M.get(("f0dm", c, (2 * j + half - 2) // 2,
                                           (2 * j + half - 2) % 2)))
                    mk[("wd", c, j, half)] = cx.dma(
                        "g", "wd%d" % wp, wB[:, wp, :],
                        wd_t[j, :, half * 4096:(half + 1) * 4096])
                    t.wait_dma(mk[("wd", c, j, half)])
                    if j == 0 and half == 0:
                        t.wait("v", M.get(("f0gt", c, 63)))
                    if half == 0 and j >= 2:
                        t.wait("s", M.get(("f0ye", c, j - 2)))
                    for q in range(32):
                        t.matmul(pA[jp][:], wB[:, wp, q * 128:(q + 1) * 128],
                                 gated[:, half * 32 + q, :],
                                 start=(half == 0 and q == 0),
                                 stop=(half == 1 and q == 31))
                    t.inc()
                    cx.mark(("f0dm", c, j, half), "t")
                s.wait("t", M.get(("f0dm", c, j, 1)))
                if c >= 1 and j <= 1:
                    s.wait("v", M.get(("f0resv", c - 1, 15)))
                s.copy(yb0[:, j, :], pA[jp][:])
                s.edge()
                s.activation(ysq0[:], yb0[:, j, :], AF.Square)
                s.inc()
                cx.mark(("f0ye", c, j), "s")
                t.wait("s", M.get(("f0ye", c, j)))
                if j == 0:
                    t.wait("s", M.get((("f0i", c), "es")))
                t.matmul(pB[0][:], C["o128"][:], yb0[:, j, :],
                         start=(j == 0), stop=(j == 15), skip_group_check=True)
                t.matmul(pB[1][:], C["o128"][:], ysq0[:],
                         start=(j == 0), stop=(j == 15), skip_group_check=True)
                t.inc()
                if j == 15:
                    cx.mark((("f0o", c), "s12t"), "t")
            ln_rows(cx, ("f0o", c))
            g.wait("t", M.get(("f0dm", c, 15, 1)))
            for m in range(16):
                mk[("xr0", c)] = cx.dma(
                    "g", "xr0", xres0f[:, m, :],
                    x1o[:, m * CORE_TOK + c * GT:m * CORE_TOK + (c + 1) * GT])
            for m in range(16):
                mp = m % 2
                if m == 0:
                    v.wait_dma(mk[("xr0", c)])
                    if c >= 1:
                        v.wait("v", M.get(("f0resv", c - 1, 15)))
                    else:
                        v.wait("v", M.get(("resv", 3, 15)))
                else:
                    v.wait("v", M.get(("f0resv", c, m - 1)))
                if m >= 2:
                    v.wait_dma(mk.get(("stx2", c, m - 2)))
                elif c >= 1:
                    v.wait_dma(mk.get(("stx2", c - 1, m + 14)))
                else:
                    v.wait_dma(mk.get(("st", 3, m + 14)))
                v.tensor_sub(t1[:], xres0f[:, m, :], bbc[:])
                v.tensor_mul(t2[:], yb0[:, m, :], abc[:])
                v.edge()
                v.tensor_add(xout[:, mp, :], t1[:], t2[:])
                v.inc()
                cx.mark(("f0resv", c, m), "v")
                sy.wait("v", M.get(("f0resv", c, m)))
                mk[("stx2", c, m)] = cx.dma(
                    "sy", "st%d" % mp,
                    x2o[:, m * CORE_TOK + c * GT:m * CORE_TOK + (c + 1) * GT],
                    xout[:, mp, :])
        if upto == "f0":
            drain(("st0", "st1"))
            return

        sql1 = at0["sql1"]
        for c in range(NG):
            g.wait("v", M.get(("l1xh", c - 1)) if c >= 1 else
               M.get(("f0resv", 3, 15)))
            g.wait_dma(mk.get(("stx2", c, 14)))
            g.wait_dma(mk.get(("stx2", c, 15)))
            mk[("xb2", c)] = cx.dma(
                "g", "xc", xb2[:, :, :],
                x2o[:, :].rearrange("p (a n) -> p a n", a=16)[:, :, c * GT:(c + 1) * GT])
            t.wait_dma(mk[("xb2", c)])
            if c == 0:
                t.wait("s", M.get((("f0o", 3), "es")))
            else:
                t.wait("s", M.get((("l1i", c - 1), "es")))
            for di in range(16):
                t.matmul(pB[0][:], C["o128"][:], xb2[:, di, :],
                         start=(di == 0), stop=(di == 15), skip_group_check=True)
            t.inc()
            cx.mark(("l1s1", c), "t")
            s.wait_dma(mk[("xb2", c)])
            if c >= 1:
                s.wait("t", M.get(("l1s2", c - 1)))
            if c == 0:
                s.wait("v", M.get(("f0resv", 3, 15)))
            for di in range(16):
                s.activation(sql1[:, di, :], xb2[:, di, :], AF.Square)
            s.inc()
            cx.mark(("l1sq", c), "s")
            t.wait("s", M.get(("l1sq", c)))
            for di in range(16):
                t.matmul(pB[1][:], C["o128"][:], sql1[:, di, :],
                         start=(di == 0), stop=(di == 15), skip_group_check=True)
            t.inc()
            cx.mark(("l1s2", c), "t")
            if cx.mode == "plan":
                M[(("l1i", c), "s12t")] = M[("l1s2", c)]
            ln_rows(cx, ("l1i", c))
            for di in range(16):
                if di == 0:
                    v.wait("t", M.get(("l1s2", c)))
                    if c == 0:
                        v.wait("v", M.get(("f0resv", 3, 15)))
                v.tensor_mul(xhat2[:, di, c * GT:(c + 1) * GT], xb2[:, di, :], abc[:])
            v.edge()
            for di in range(16):
                v.tensor_sub(xhat2[:, di, c * GT:(c + 1) * GT],
                             xhat2[:, di, c * GT:(c + 1) * GT], bbc[:])
            v.inc()
            cx.mark(("l1xh", c), "v")

        wmats = [wq_t, wk_t, wv_t]
        for m in range(48):
            mp = m % 2
            if m >= 2:
                g.wait("t", M.get(("l1mm", (m - 2) * 4 + 3)))
            elif cx.mode == "plan" or True:
                g.wait("t", M.get(("f0g", 3, 62 + m)))
            mk[("l1w", m)] = cx.dma("g", "w%d" % mp, wA[:, mp, :],
                                    wmats[m // 16][m % 16, :, :])
            isv = m >= 32
            for tc in range(NG):
                i = m * 4 + tc
                par = i % 2
                t.wait_dma(mk[("l1w", m)])
                if i == 0:
                    t.wait("v", M.get(("l1xh", 3)))
                    t.wait("s", M.get(("f0ye", 3, 15)))
                if i >= 2:
                    t.wait("v", M.get(("l1qn", i - 2)))
                for di in range(16):
                    t.matmul(pA[par][:], wA[:, mp, di * 128:(di + 1) * 128],
                             xhat2[:, di, tc * GT:(tc + 1) * GT],
                             start=(di == 0), stop=(di == 15))
                t.inc()
                cx.mark(("l1mm", i), "t")
                if not isv:
                    s.wait("t", M.get(("l1mm", i)))
                    if i >= 1:
                        s.wait("t", M.get(("l1nt", i - 1)))
                    s.activation(sqs[:], pA[par][:], AF.Square)
                    s.inc()
                    cx.mark(("l1sqp", i), "s")
                    t.wait("s", M.get(("l1sqp", i)))
                    if i >= 1:
                        t.wait("v", M.get(("l1nv", i - 1)))
                    elif cx.mode == "plan" or True:
                        t.wait("s", M.get((("l1i", 3), "es")))
                    t.matmul(pB[0][:], C["o128"][:], sqs[:], start=True, stop=True)
                    t.inc()
                    cx.mark(("l1nt", i), "t")
                    v.wait("t", M.get(("l1nt", i)))
                    v.reduce_max(tiny[0:1, 4:5], pB[0][0:1, :], axis=AX)
                    v.edge()
                    slot = (m // 16) * 16 + (m % 16)
                    v.tensor_max(nacc[0:1, slot:slot + 1], nacc[0:1, slot:slot + 1],
                                 tiny[0:1, 4:5])
                    v.inc()
                    cx.mark(("l1nv", i), "v")
                    v.wait("t", M.get(("l1mm", i)))
                    v.tensor_copy(qn[:], pA[par][:])
                    v.inc()
                    cx.mark(("l1qn", i), "v")
                    t.wait("v", M.get(("l1qn", i)))
                    if i >= 1:
                        t.wait("v", M.get(("l1out", i - 1)))
                    t.matmul(pD[:], C["P1"][:], qn[:], start=True, stop=True)
                    t.inc()
                    cx.mark(("l1rot", i), "t")
                    v.wait("t", M.get(("l1rot", i)))
                    v.tensor_mul(t1[:], qn[:], coss[:, tc * GT:(tc + 1) * GT])
                    v.tensor_mul(t2[:], pD[:], sins[:, tc * GT:(tc + 1) * GT])
                    v.edge()
                    if i >= 2:
                        v.wait_dma(mk.get(("l1st", i - 2)))
                    v.tensor_add(qout[:, par, :], t1[:], t2[:])
                    v.inc()
                    cx.mark(("l1out", i), "v")
                    sy.wait("v", M.get(("l1out", i)))
                    dst = (q1 if m < 16 else k1)[m % 16, :, tc * GT:(tc + 1) * GT]
                    mk[("l1st", i)] = cx.dma("sy", "st%d" % par, dst, qout[:, par, :])
                else:
                    v.wait("t", M.get(("l1mm", i)))
                    v.tensor_copy(qn[:], pA[par][:])
                    v.inc()
                    cx.mark(("l1qn", i), "v")
                    t.wait("v", M.get(("l1qn", i)))
                    if i >= 1:
                        t.wait("v", M.get(("l1out", i - 1)))
                    for blk in range(4):
                        t.transpose(pC[:, blk, :], qn[:, blk * 128:(blk + 1) * 128],
                                    C["id128"][:])
                    t.inc()
                    cx.mark(("l1rot", i), "t")
                    v.wait("t", M.get(("l1rot", i)))
                    if i >= 2:
                        v.wait_dma(mk.get(("l1st", i - 2)))
                    v.tensor_copy(vout[:, par, :].rearrange("p (b n) -> p b n", b=4),
                                  pC[:, :, :])
                    v.inc()
                    cx.mark(("l1out", i), "v")
                    sy.wait("v", M.get(("l1out", i)))
                    mk[("l1st", i)] = cx.dma(
                        "sy", "st%d" % par,
                        v1[m - 32, tc * GT:(tc + 1) * GT, :]
                        .rearrange("(b p) d -> p b d", p=128),
                        vout[:, par, :].rearrange("p (b d) -> p b d", b=4))
        sy.wait("v", M.get(("l1nv", 127)))
        cx.dma("sy", "st0", qkn[:, :], nacc[:, :])
        drain(("st0", "st1"))

    with nc.Block() as block:
        cx.run(block, body)
    return nc


def build_masks(r):
    out = np.zeros((NG, 128, KSPAN, NG, 128), dtype=BF)
    for g in range(NG):
        kg_run = RUNS[r][g]
        for i in range(KSPAN):
            kb = 4 * kg_run - 32 + i
            if kb < 0:
                continue
            kpos = kb * 128 + np.arange(128)[:, None, None]
            qpos = ((4 * kg_run + np.arange(NG)) * 128)[None, :, None] \
                + np.arange(128)[None, None, :]
            valid = (kpos <= qpos) & (qpos - kpos <= WINDOW)
            out[g, :, i, :, :] = valid.astype(BF)
    return np.ascontiguousarray(out.reshape(NG, 128, KSPAN * NG * 128))


def _prep_l0b_inputs(inputs, resA):
    x_full = np.asarray(inputs["L0_attn_x"], dtype=np.float32)
    qg = np.asarray(inputs["L0_attn_q_gamma"], dtype=np.float32).reshape(1, 128)
    kg = np.asarray(inputs["L0_attn_k_gamma"], dtype=np.float32).reshape(1, 128)
    wout = retile_w(np.asarray(inputs["L0_attn_w_out"], dtype=np.float32))
    wg_t = retile_w(np.ascontiguousarray(
        np.asarray(inputs["L0_ffn_w_gate"], dtype=np.float32).T))
    wu_t = retile_w(np.ascontiguousarray(
        np.asarray(inputs["L0_ffn_w_up"], dtype=np.float32).T))
    wd_t = retile_w(np.ascontiguousarray(
        np.asarray(inputs["L0_ffn_w_down"], dtype=np.float32).T), din=8192)
    wq_t = retile_w(np.ascontiguousarray(
        np.asarray(inputs["L1_attn_wq"], dtype=np.float32).T))
    wk_t = retile_w(np.ascontiguousarray(
        np.asarray(inputs["L1_attn_wk"], dtype=np.float32).T))
    wv_t = retile_w(np.ascontiguousarray(
        np.asarray(inputs["L1_attn_wv"], dtype=np.float32).T))
    owner = {}
    for rr in range(4):
        for li, bb in enumerate(core_blocks(rr)):
            owner[bb] = (rr, li)
    maps = []
    for c in range(8):
        b, r = c // 4, c % 4
        pos = core_positions(r)
        _, _, cos1, sin1 = rope_tables(pos)
        k0g = np.zeros((NG, NH, 128, KSPAN * 128), dtype=BF)
        v0g = np.zeros((NG, NH, 128, KSPAN * 128), dtype=BF)
        for g in range(NG):
            kg_run = RUNS[r][g]
            for i in range(KSPAN):
                bb = 4 * kg_run - 32 + i
                if bb < 0:
                    continue
                rb, li = owner[bb]
                src = resA[b * 4 + rb]
                k0g[g, :, :, i * 128:(i + 1) * 128] = src["k0"][:, :, li * 128:(li + 1) * 128]
                v0g[g, :, :, i * 128:(i + 1) * 128] = src["v0"][:, li * 128:(li + 1) * 128, :]
        maps.append(dict(
            x=hm_layout(x_full[b][pos]), q0o=resA[c]["q0"],
            k0g=k0g, v0g=v0g, masks=build_masks(r), qg=qg, kg=kg,
            wout=wout, wg_t=wg_t, wu_t=wu_t, wd_t=wd_t,
            wq_t=wq_t, wk_t=wk_t, wv_t=wv_t,
            cos1=np.ascontiguousarray(cos1), sin1=np.ascontiguousarray(sin1)))
    return maps


def _f1_consts(nc):
    ssel = np.zeros((8, 128, 64), dtype=np.float32)
    esel = np.zeros((8, 128, 128), dtype=BF)
    for g in range(8):
        for p in range(128):
            a = 8 * g + p // 16
            ssel[g, p, a - 8 * g * 0] = 0.0
    for g in range(8):
        for p in range(128):
            a_local = p // 16
            a_glob = 8 * g + a_local
            ssel[g, p, a_glob] = 1.0
            esel[g, a_glob, p] = 1.0
    G = np.zeros((128, 16), dtype=BF)
    for p in range(128):
        G[p, p % 16] = 1.0
    vsel = np.zeros((16, 128), dtype=BF)
    for p in range(128):
        vsel[p % 16, p] = 1.0 / 16.0
    c_ssel = nc.inline_tensor(ssel.reshape(8 * 128, 64), name="c_ssel")
    c_esel = nc.inline_tensor(esel.reshape(8 * 128, 128).view(np.uint16), name="c_esel")
    c_G = nc.inline_tensor(G.view(np.uint16), name="c_G")
    c_vsel = nc.inline_tensor(vsel.view(np.uint16), name="c_vsel")
    sb = nc.alloc_sbuf_tensor
    t = dict(
        ssel=sb("ssel", [128, 8, 64], F32),
        esel=sb("esel", [128, 8, 128], BF16),
        selG=sb("selG", [128, 16], BF16),
        vsel=sb("vsel", [16, 128], BF16),
    )
    pairs = [
        (t["ssel"][:, :, :],
         c_ssel.ap().rearrange("(a p) n -> p a n", p=128)),
        (t["esel"][:, :, :],
         c_esel.ap().bitcast(BF16).rearrange("(a p) n -> p a n", p=128)),
        (t["selG"][:], c_G.ap().bitcast(BF16)),
        (t["vsel"][:], c_vsel.ap().bitcast(BF16)),
    ]
    return t, pairs


def build_l1b(upto="f1"):
    nc = bass.Bass("TRN2", target_bir_lowering=False, debug=False)
    x = nc.dram_tensor("x", [128, 16 * CORE_TOK], F32, kind="ExternalInput").ap()
    q1o = nc.dram_tensor("q1o", [NH, 128, CORE_TOK], BF16, kind="ExternalInput").ap()
    k1g = nc.dram_tensor("k1g", [NH, 128, SEQ], BF16, kind="ExternalInput").ap()
    v1g = nc.dram_tensor("v1g", [NH, 128, SEQ], BF16, kind="ExternalInput").ap()
    qkng = nc.dram_tensor("qkng", [1, 4 * 2 * NH], F32, kind="ExternalInput").ap()
    wo_t = nc.dram_tensor("wo_t", [16, 128, 16 * 128], F32, kind="ExternalInput").ap()
    W_t = nc.dram_tensor("W_t", [16, 128, 16 * 128], F32, kind="ExternalInput").ap()
    A_t = nc.dram_tensor("A_t", [8, 128, 16 * 128], F32, kind="ExternalInput").ap()
    B_t = nc.dram_tensor("B_t", [16, 128, 8 * 128], F32, kind="ExternalInput").ap()
    bias_t = nc.dram_tensor("bias_t", [128, 16], F32, kind="ExternalInput").ap()
    x3o = nc.dram_tensor("x3o", [128, 16 * CORE_TOK], F32, kind="ExternalOutput").ap()
    x4o = nc.dram_tensor("x4o", [128, 16 * CORE_TOK], F32, kind="ExternalOutput").ap()

    C, cload, NC_ = _consts(nc)
    S1, spairs = _f1_consts(nc)
    sb = nc.alloc_sbuf_tensor
    wA = sb("wA", [128, 4, 16 * 128], BF16)
    rows = sb("rows", [1, 12, GT], F32)
    (mean_r, msq_r, es_r, var_r, sd_r, rstd_r, mr_r, den_r, sdq_r, rsq_r,
     nrm_r, spare_r) = [rows[0:1, i, :] for i in range(12)]
    tiny = sb("tiny", [1, 64], F32)
    negc16 = sb("negc16", [128, 16], F32)
    abc = sb("abc", [128, GT], F32)
    bbc = sb("bbc", [128, GT], F32)
    qb = sb("qb", [128, 2, GT], BF16)
    probs = sb("probs", [128, 4, GT], BF16)
    pp = sb("pp", [128, 2, GT], BF16)
    t1 = sb("t1", [128, GT], F32)
    t2 = sb("t2", [128, GT], F32)
    sqs = sb("sqs", [128, GT], BF16)
    rbc = sb("rbc", [128, GT], F32)
    xout = sb("xout", [128, 2, GT], F32)
    biasb = sb("biasb", [128, 16], F32)
    qkn_s = sb("qkn_s", [1, 128], F32)

    ARENA = 110 * 1024
    arena = sb("arena", [128, ARENA], mybir.dt.uint8)
    abase = nc.lookup_mloc(arena).addr

    def at(name, shape, dtype, off):
        return nc.alloc_sbuf_tensor_at(name, shape, dtype, offset=abase + off)

    kbuf = at("kbuf", [128, 2, 4096], BF16, 0)
    vbuf = at("vbuf", [128, 2, 4096], BF16, 16 * 1024)
    ao = at("ao", [128, 16, GT], BF16, 32 * 1024)
    yb = at("yb", [128, 16, GT], BF16, 48 * 1024)
    ysq = at("ysq", [128, GT], BF16, 64 * 1024)
    xres = at("xres", [128, 16, GT], F32, 66 * 1024)
    xb3 = at("xb3", [128, 16, GT], BF16, 0)
    xhat3 = at("xhat3", [128, 16, GT], BF16, 16 * 1024)
    vecs = at("vecs", [128, 8, GT], BF16, 32 * 1024)
    me = at("me", [128, 8, GT], BF16, 40 * 1024)
    arr_sb = at("arr_sb", [128, GT], F32, 48 * 1024)
    tb = at("tb", [128, 64], F32, 50 * 1024)
    routesT = at("routesT", [128, GT], BF16, 51 * 1024)
    mvt = at("mvt", [128, GT], BF16, 52 * 1024)
    vh_sb = at("vh_sb", [16, GT], BF16, 53 * 1024)
    yb0 = at("yb0", [128, 16, GT], BF16, 54 * 1024)
    ysq0 = at("ysq0", [128, GT], BF16, 70 * 1024)
    xres0f = at("xres0f", [128, 16, GT], F32, 71 * 1024)

    ps = nc.alloc_psum_tensor
    pS = ps("pS", [128, 4, GT], F32)
    pA = [pS[:, 0, :], pS[:, 1, :], pS[:, 2, :], pS[:, 3, :]]
    pO = ps("pO", [128, GT], F32)
    pB = [ps(f"pB{i}", [1, GT], F32) for i in range(2)]
    pC = ps("pC", [128, 4, 128], BF16)
    pD = pO

    cx = Cx(nc)
    M = cx.M

    def ln_rows(cx, key):
        s, v, t = cx.s, cx.v, cx.t
        s.wait("t", M.get((key, "s12t")))
        s.mul(mean_r, pB[0][0:1, :], 1.0 / D)
        s.edge()
        s.activation(msq_r, mean_r, AF.Square)
        s.edge()
        s.mul(es_r, pB[1][0:1, :], 1.0 / D)
        s.inc()
        cx.mark((key, "es"), "s")
        v.wait("s", M.get((key, "es")))
        v.tensor_sub(var_r, es_r, msq_r)
        v.inc()
        cx.mark((key, "var"), "v")
        s.wait("v", M.get((key, "var")))
        s.activation(sd_r, var_r, AF.Sqrt, bias=1e-5)
        s.inc()
        cx.mark((key, "sd"), "s")
        v.wait("s", M.get((key, "sd")))
        v.reciprocal(rstd_r, sd_r)
        v.edge()
        v.tensor_mul(mr_r, mean_r, rstd_r)
        v.inc()
        cx.mark((key, "mr"), "v")
        t.wait("v", M.get((key, "mr")))
        t.matmul(pA[2][:], C["ocol"][:], rstd_r, start=True, stop=True)
        t.matmul(pA[3][:], C["ocol"][:], mr_r, start=True, stop=True)
        t.inc()
        cx.mark((key, "bc"), "t")
        v.wait("t", M.get((key, "bc")))
        v.tensor_copy(abc[:], pA[2][:])
        v.tensor_copy(bbc[:], pA[3][:])
        v.inc()
        cx.mark((key, "abv"), "v")
        v.wait("v", M.get((key, "abv")))

    NKT = 64

    def body(cx):
        g, sy, t, s, v = cx.g, cx.sy, cx.t, cx.s, cx.v
        mk = {}
        for dst, srcap in C["pairs"] + spairs:
            mk["c"] = cx.dma("g", "c", dst, srcap)
        cx.dma("g", "c", qkn_s[:], qkng)
        mk["c"] = cx.dma("g", "c", biasb[:], bias_t)

        v.wait_dma(mk["c"])
        v.tensor_max(tiny[0:1, 0:32], qkn_s[0:1, 0:32], qkn_s[0:1, 32:64])
        v.edge()
        v.tensor_max(tiny[0:1, 32:64], qkn_s[0:1, 64:96], qkn_s[0:1, 96:128])
        v.edge()
        v.tensor_max(tiny[0:1, 0:32], tiny[0:1, 0:32], tiny[0:1, 32:64])
        v.edge()
        v.tensor_mul(tiny[0:1, 32:48], tiny[0:1, 0:16], tiny[0:1, 16:32])
        v.inc()
        cx.mark("negv", "v")
        s.wait("v", M.get("negv"))
        s.activation(tiny[0:1, 48:64], tiny[0:1, 32:48], AF.Sqrt)
        s.edge()
        s.activation(tiny[0:1, 48:64], tiny[0:1, 48:64], AF.Copy,
                     scale=NEGC_SCALE / 128.0, bias=-0.01)
        s.inc()
        cx.mark("negs", "s")
        t.wait("s", M.get("negs"))
        t.matmul(pD[:, 0:16], C["ocol"][:], tiny[0:1, 48:64], start=True, stop=True)
        t.inc()
        cx.mark("negt", "t")
        v.wait("t", M.get("negt"))
        v.tensor_copy(negc16[:], pD[:, 0:16])
        v.inc()
        cx.mark("negbc", "v")

        for gi in range(NG):
            for h in range(NH):
                a = gi * NH + h
                ap_ = a % 2
                if a >= 2:
                    g.wait("t", M.get(("att_t_end", a - 2)))
                mk[("q", a)] = cx.dma("g", "k%d" % ap_, qb[:, ap_, :],
                                      q1o[h, :, gi * GT:(gi + 1) * GT])
                for hh in range(2):
                    sl = (2 * a + hh) % 2
                    if 2 * a + hh >= 2:
                        pa, phh = divmod(2 * a + hh - 2, 2)
                        g.wait("t", M.get(("c1h", pa, phh)))
                    cx.dma("g", "kv%d" % sl, kbuf[:, sl, :],
                           k1g[h, :, hh * 4096:(hh + 1) * 4096])
                    mk[("kv", a, hh)] = cx.dma("g", "kv%d" % sl, vbuf[:, sl, :],
                                               v1g[h, :, hh * 4096:(hh + 1) * 4096])
                t.wait_dma(mk[("q", a)])
                NP = NKT // 2
                LAG = 2

                def pvden_c(q, a=a, gi=gi, h=h):
                    t.wait("s", M.get(("exp", a, q)))
                    for i in (2 * q, 2 * q + 1):
                        hh = i // 32
                        sl = (2 * a + hh) % 2
                        if i == 0:
                            if a >= 1:
                                t.wait("v", M.get(("aov", a - 1)))
                            if h == 0 and gi >= 1:
                                t.wait("s", M.get(("opmean", gi - 1)))
                        t.matmul(pO[:], vbuf[:, sl, (i % 32) * 128:(i % 32 + 1) * 128],
                                 probs[:, i % 4, :], start=(i == 0),
                                 stop=(i == NKT - 1), skip_group_check=True)
                        t.inc()
                        cx.mark(("pv", a, i), "t")
                    t.wait("v", M.get(("padd", a, q)))
                    t.matmul(pB[0][:], C["o128"][:], pp[:, q % 2, :],
                             start=(q == 0), stop=(q == NP - 1),
                             skip_group_check=True)
                    t.inc()
                    cx.mark(("den", a, q), "t")
                    if (2 * q + 1) % 32 == 31:
                        cx.mark(("c1h", a, (2 * q) // 32), "t")

                for p in range(NP + LAG):
                    if p < NP:
                        for i in (2 * p, 2 * p + 1):
                            sl4 = i % 4
                            hh = i // 32
                            sl = (2 * a + hh) % 2
                            if i % 32 == 0:
                                t.wait_dma(mk[("kv", a, hh)])
                            if i % 2 == 0:
                                if p >= 2:
                                    t.wait("s", M.get(("exp", a, p - 2)))
                                elif a >= 1:
                                    t.wait("s", M.get(("exp", a - 1, NP - 2 + p)))
                                if p == 0 and a >= 1:
                                    t.wait("v", M.get(("aov", a - 1)))
                                if p == 0 and h == 0 and gi >= 1:
                                    t.wait("s", M.get(("ye", gi - 1, 15)))
                                    t.wait("v", M.get((("op", gi - 1), "abv")))
                            t.matmul(pS[:, sl4, :],
                                     kbuf[:, sl, (i % 32) * 128:(i % 32 + 1) * 128],
                                     qb[:, ap_, :], start=True, stop=True)
                            t.inc()
                            cx.mark(("sc", a, i), "t")
                    if p >= LAG:
                        pvden_c(p - LAG)
                t.wait("v", M.get(("rec", a)))
                t.wait("s", M.get(("exp", a, NP - 1)))
                t.matmul(pS[:, 0, :], C["ocol"][:], den_r, start=True, stop=True)
                t.inc()
                cx.mark(("rbc", a), "t")
                cx.mark(("att_t_end", a), "t")
                if a == 0:
                    s.wait("v", M.get("negbc"))
                for p in range(NKT // 2):
                    NP = NKT // 2
                    sl4 = (2 * p) % 4
                    s.wait("t", M.get(("sc", a, 2 * p + 1)))
                    if p >= 2:
                        s.wait("t", M.get(("pv", a, 2 * p - 3)))
                        s.wait("v", M.get(("padd", a, p - 2)))
                    elif a >= 1:
                        s.wait("t", M.get(("pv", a - 1, NKT - 3 + 2 * p)))
                        s.wait("v", M.get(("padd", a - 1, NP - 2 + p)))
                    s.activation(probs[:, sl4:sl4 + 2, :].rearrange("p a n -> p (a n)"),
                                 pS[:, sl4:sl4 + 2, :].rearrange("p a n -> p (a n)"),
                                 AF.Exp, scale=SCALE, bias=negc16[:, h:h + 1])
                    s.inc()
                    cx.mark(("exp", a, p), "s")
                for p in range(NKT // 2):
                    NP = NKT // 2
                    sl4 = (2 * p) % 4
                    v.wait("s", M.get(("exp", a, p)))
                    if p >= 2:
                        v.wait("t", M.get(("den", a, p - 2)))
                    elif a >= 1:
                        v.wait("t", M.get(("den", a - 1, NP - 2 + p)))
                    v.tensor_add(pp[:, p % 2, :], probs[:, sl4, :], probs[:, sl4 + 1, :])
                    v.inc()
                    cx.mark(("padd", a, p), "v")
                v.wait("t", M.get(("den", a, NKT // 2 - 1)))
                v.reciprocal(den_r, pB[0][0:1, :])
                v.inc()
                cx.mark(("rec", a), "v")
                v.wait("t", M.get(("rbc", a)))
                if a >= 1:
                    v.wait("v", M.get(("aov", a - 1)))
                v.tensor_copy(rbc[:], pS[:, 0, :])
                v.edge()
                v.tensor_mul(ao[:, h, :], pO[:], rbc[:])
                v.inc()
                cx.mark(("aov", a), "v")
            cx.mark(("attv_end", gi), "v")

            for m in range(16):
                mp = m % 2
                if m >= 2:
                    g.wait("t", M.get(("yt", gi, m - 2)))
                else:
                    g.wait("t", M.get(("att_t_end", gi * NH + NH - 1)))
                mk[("w", gi, m)] = cx.dma("g", "w%d" % mp, wA[:, mp, :], wo_t[m, :, :])
                mk[("xr", gi)] = cx.dma(
                    "g", "xr", xres[:, m, :],
                    x[:, m * CORE_TOK + gi * GT:m * CORE_TOK + (gi + 1) * GT])
                t.wait_dma(mk[("w", gi, m)])
                if m >= 2:
                    t.wait("s", M.get(("ye", gi, m - 2)))
                if m == 0:
                    t.wait("v", M.get(("aov", gi * NH + NH - 1)))
                for di in range(16):
                    t.matmul(pA[mp][:], wA[:, mp, di * 128:(di + 1) * 128],
                             ao[:, di, :], start=(di == 0), stop=(di == 15))
                t.inc()
                cx.mark(("yt", gi, m), "t")
                s.wait("t", M.get(("yt", gi, m)))
                s.copy(yb[:, m, :], pA[mp][:])
                s.edge()
                s.activation(ysq[:], yb[:, m, :], AF.Square)
                s.inc()
                cx.mark(("ye", gi, m), "s")
                t.wait("s", M.get(("ye", gi, m)))
                if m == 0 and gi >= 1:
                    t.wait("s", M.get(("opes", gi - 1)))
                t.matmul(pB[0][:], C["o128"][:], yb[:, m, :],
                         start=(m == 0), stop=(m == 15), skip_group_check=True)
                t.matmul(pB[1][:], C["o128"][:], ysq[:],
                         start=(m == 0), stop=(m == 15), skip_group_check=True)
                t.inc()
                if m == 15:
                    cx.mark((("op", gi), "s12t"), "t")
            ln_rows(cx, ("op", gi))
            if cx.mode == "plan":
                M[("opmean", gi)] = M[(("op", gi), "es")]
                M[("opes", gi)] = M[(("op", gi), "es")]
            for m in range(16):
                mp = m % 2
                if m == 0:
                    v.wait_dma(mk[("xr", gi)])
                    if gi >= 1:
                        v.wait("v", M.get(("resv", gi - 1, 15)))
                else:
                    v.wait("v", M.get(("resv", gi, m - 1)))
                if m >= 2:
                    v.wait_dma(mk.get(("st", gi, m - 2)))
                elif gi >= 1:
                    v.wait_dma(mk.get(("st", gi - 1, m + 14)))
                v.tensor_sub(t1[:], xres[:, m, :], bbc[:])
                v.tensor_mul(t2[:], yb[:, m, :], abc[:])
                v.edge()
                v.tensor_add(xout[:, mp, :], t1[:], t2[:])
                v.inc()
                cx.mark(("resv", gi, m), "v")
                sy.wait("v", M.get(("resv", gi, m)))
                mk[("st", gi, m)] = cx.dma(
                    "sy", "st%d" % mp,
                    x3o[:, m * CORE_TOK + gi * GT:m * CORE_TOK + (gi + 1) * GT],
                    xout[:, mp, :])

        def drain(slots):
            for sl in slots:
                if cx.mode != "plan" and sl in cx.dmarks:
                    sy.wait_dma((cx._dsems[sl], cx.dmarks[sl][-1]))

        if upto == "att":
            drain(("st0", "st1"))
            return

        for c in range(NG):
            g.wait("t", M.get(("att_t_end", 63)) if c == 0 else None)
            if c >= 1:
                g.wait("v", M.get(("f1resv", c - 1, 15)))
            g.wait_dma(mk.get(("st", c, 14)))
            g.wait_dma(mk.get(("st", c, 15)))
            mk[("xb3", c)] = cx.dma(
                "g", "xc", xb3[:, :, :],
                x3o[:, :].rearrange("p (a n) -> p a n", a=16)[:, :, c * GT:(c + 1) * GT])
            t.wait_dma(mk[("xb3", c)])
            if c == 0:
                t.wait("s", M.get((("op", 3), "es")))
            else:
                t.wait("s", M.get((("f1o", c - 1), "es")))
            for di in range(16):
                t.matmul(pB[0][:], C["o128"][:], xb3[:, di, :],
                         start=(di == 0), stop=(di == 15), skip_group_check=True)
            t.inc()
            cx.mark(("f1s1", c), "t")
            s.wait_dma(mk[("xb3", c)])
            if c >= 1:
                s.wait("t", M.get(("f1s2", c - 1)))
            for di in range(16):
                s.activation(xhat3[:, di, :], xb3[:, di, :], AF.Square)
            s.inc()
            cx.mark(("f1sq", c), "s")
            t.wait("s", M.get(("f1sq", c)))
            for di in range(16):
                t.matmul(pB[1][:], C["o128"][:], xhat3[:, di, :],
                         start=(di == 0), stop=(di == 15), skip_group_check=True)
            t.inc()
            cx.mark(("f1s2", c), "t")
            if cx.mode == "plan":
                M[(("f1i", c), "s12t")] = M[("f1s2", c)]
            ln_rows(cx, ("f1i", c))
            for di in range(16):
                if di == 0:
                    v.wait("t", M.get(("f1s2", c)))
                    if c == 0:
                        v.wait("v", M.get(("resv", 3, 15)))
                v.tensor_mul(xhat3[:, di, :], xb3[:, di, :], abc[:])
            v.edge()
            for di in range(16):
                v.tensor_sub(xhat3[:, di, :], xhat3[:, di, :], bbc[:])
            v.inc()
            cx.mark(("f1xh", c), "v")
            for ag in range(8):
                gp = ag % 2
                if ag >= 2:
                    g.wait("t", M.get(("f1vm", c, ag - 2)))
                elif c == 0:
                    g.wait("t", M.get(("yt", 3, 14 + ag)))
                else:
                    g.wait("t", M.get(("f1bm", c - 1, 14 + ag)))
                mk[("wa", c, ag)] = cx.dma("g", "w%d" % gp, wA[:, gp, :], A_t[ag, :, :])
                t.wait_dma(mk[("wa", c, ag)])
                if ag == 0:
                    t.wait("v", M.get(("f1xh", c)))
                if ag >= 2:
                    t.wait("v", M.get(("f1ve", c, ag - 2)))
                for di in range(16):
                    t.matmul(pA[gp][:], wA[:, gp, di * 128:(di + 1) * 128],
                             xhat3[:, di, :], start=(di == 0), stop=(di == 15))
                t.inc()
                cx.mark(("f1vm", c, ag), "t")
                v.wait("t", M.get(("f1vm", c, ag)))
                v.tensor_copy(vecs[:, ag, :], pA[gp][:])
                v.inc()
                cx.mark(("f1ve", c, ag), "v")
            for ag in range(8):
                s.wait("v", M.get(("f1ve", c, ag)))
                if ag >= 1:
                    s.wait("t", M.get(("f1am", c, ag - 1)))
                s.activation(t1[:], vecs[:, ag, :], AF.Square)
                s.inc()
                cx.mark(("f1as", c, ag), "s")
                t.wait("s", M.get(("f1as", c, ag)))
                if ag == 0:
                    t.wait("s", M.get((("f1i", c), "es")))
                    t.wait("v", M.get((("f1i", c), "abv")))
                t.matmul(pA[2][0:64, :], S1["ssel"][:, ag, :], t1[:],
                         start=(ag == 0), stop=(ag == 7), skip_group_check=True)
                t.inc()
                cx.mark(("f1am", c, ag), "t")
            v.wait("t", M.get(("f1am", c, 7)))
            v.tensor_copy(arr_sb[0:64, :], pA[2][0:64, :])
            v.inc()
            cx.mark(("f1ac", c), "v")
            t.wait("v", M.get(("f1ac", c)))
            if c >= 1:
                t.wait("v", M.get(("f1t4", c - 1, 3)))
            for tt in range(4):
                t.transpose(pD[:, tt * 128:(tt + 1) * 128],
                            arr_sb[:, tt * 128:(tt + 1) * 128], C["idf32"][:])
            t.inc()
            cx.mark(("f1tr", c), "t")
            for tt in range(4):
                v.wait("t", M.get(("f1tr", c)))
                v.tensor_copy(tb[:], pD[:, tt * 128:tt * 128 + 64])
                v.edge()
                for rnd in range(3):
                    v.reduce_max(t2[:, 0:1], tb[:], axis=AX)
                    v.edge()
                    v.tensor_single_scalar(mvt[:, 0:64], tb[:], t2[:, 0:1], ALU.is_ge)
                    v.edge()
                    v.scalar_tensor_tensor(tb[:], mvt[:, 0:64], -1e30, tb[:],
                                           ALU.mult, ALU.add)
                    v.edge()
                v.reduce_max(t2[:, 0:1], tb[:], axis=AX)
                v.edge()
                v.tensor_single_scalar(mvt[:, 0:64], pD[:, tt * 128:tt * 128 + 64],
                                       t2[:, 0:1], ALU.is_ge)
                v.inc()
                cx.mark(("f1t4", c, tt), "v")
                t.wait("v", M.get(("f1t4", c, tt)))
                if tt == 0 and c >= 1:
                    t.wait("v", M.get(("f1rtc", c - 1, 3)))
                t.transpose(pC[:, tt, :][0:64, :], mvt[:, 0:64], C["id128"][:])
                t.inc()
                cx.mark(("f1rt", c, tt), "t")
                v.wait("t", M.get(("f1rt", c, tt)))
                if tt == 0:
                    if c >= 1:
                        v.wait("t", M.get(("f1mem", c - 1, 7)))
                    v.memset(routesT[:], 0.0)
                    v.edge()
                v.tensor_copy(routesT[0:64, tt * 128:(tt + 1) * 128], pC[:, tt, :][0:64, :])
                v.inc()
                cx.mark(("f1rtc", c, tt), "v")
            for ag in range(8):
                t.wait("v", M.get(("f1rtc", c, 3)))
                if ag == 0:
                    if c >= 1:
                        t.wait("v", M.get(("f1u", c - 1, 7)))
                else:
                    t.wait("v", M.get(("f1me", c, ag - 1)))
                t.matmul(pA[2][:], S1["esel"][:, ag, :], routesT[:, :],
                         start=True, stop=True)
                t.inc()
                cx.mark(("f1mem", c, ag), "t")
                v.wait("t", M.get(("f1mem", c, ag)))
                v.tensor_copy(me[:, ag, :], pA[2][:])
                v.edge()
                v.tensor_mul(mvt[:], vecs[:, ag, :], me[:, ag, :])
                v.inc()
                cx.mark(("f1me", c, ag), "v")
                t.wait("v", M.get(("f1me", c, ag)))
                if ag == 0 and c >= 1:
                    t.wait("v", M.get(("f1vh", c - 1)))
                t.matmul(pA[3][0:16, :], S1["selG"][:], mvt[:],
                         start=(ag == 0), stop=(ag == 7), skip_group_check=True)
                t.inc()
                cx.mark(("f1gm", c, ag), "t")
            v.wait("t", M.get(("f1gm", c, 7)))
            v.tensor_copy(vh_sb[:], pA[3][0:16, :])
            v.inc()
            cx.mark(("f1vh", c), "v")
            t.wait("v", M.get(("f1vh", c)))
            t.matmul(pA[2][:], S1["vsel"][:], vh_sb[:], start=True, stop=True)
            t.inc()
            cx.mark(("f1vr", c), "t")
            for ag in range(8):
                v.wait("t", M.get(("f1vr", c)))
                if ag >= 1:
                    v.wait("v", M.get(("f1u", c, ag - 1)))
                v.tensor_mul(me[:, ag, :], me[:, ag, :], pA[2][:])
                v.inc()
                cx.mark(("f1u", c, ag), "v")
            for j in range(16):
                jp = j % 2
                if j >= 2:
                    g.wait("t", M.get(("f1bm", c, j - 2)))
                else:
                    g.wait("t", M.get(("f1vm", c, 6 + j)))
                cx.dma("g", "w%d" % jp, wA[:, jp, :], W_t[j, :, :])
                mk[("wb1", c, j)] = cx.dma("g", "w%d" % jp,
                                           wA[:, 2 + jp, 0:1024], B_t[j, :, :])
                t.wait_dma(mk[("wb1", c, j)])
                if j == 0:
                    t.wait("v", M.get(("f1u", c, 7)))
                if j >= 2:
                    t.wait("s", M.get(("f1ye", c, j - 2)))
                for di in range(16):
                    t.matmul(pA[jp][:], wA[:, jp, di * 128:(di + 1) * 128],
                             xhat3[:, di, :], start=(di == 0), stop=False)
                for ag in range(8):
                    t.matmul(pA[jp][:], wA[:, 2 + jp, ag * 128:(ag + 1) * 128],
                             me[:, ag, :], start=False, stop=(ag == 7))
                t.inc()
                cx.mark(("f1bm", c, j), "t")
                s.wait("t", M.get(("f1bm", c, j)))
                s.add(yb0[:, j, :], pA[jp][:], biasb[:, j:j + 1])
                s.edge()
                s.activation(ysq0[:], yb0[:, j, :], AF.Square)
                s.inc()
                cx.mark(("f1ye", c, j), "s")
                t.wait("s", M.get(("f1ye", c, j)))
                if j == 0:
                    t.wait("s", M.get((("f1i", c), "es")))
                t.matmul(pB[0][:], C["o128"][:], yb0[:, j, :],
                         start=(j == 0), stop=(j == 15), skip_group_check=True)
                t.matmul(pB[1][:], C["o128"][:], ysq0[:],
                         start=(j == 0), stop=(j == 15), skip_group_check=True)
                t.inc()
                if j == 15:
                    cx.mark((("f1o", c), "s12t"), "t")
            ln_rows(cx, ("f1o", c))
            g.wait("t", M.get(("f1bm", c, 15)))
            if c == 0:
                g.wait("v", M.get(("resv", 3, 15)))
            else:
                g.wait("v", M.get(("f1resv", c - 1, 15)))
            for m in range(16):
                mk[("xr0", c)] = cx.dma(
                    "g", "xr0", xres0f[:, m, :],
                    x3o[:, m * CORE_TOK + c * GT:m * CORE_TOK + (c + 1) * GT])
            for m in range(16):
                mp = m % 2
                if m == 0:
                    v.wait_dma(mk[("xr0", c)])
                    if c >= 1:
                        v.wait("v", M.get(("f1resv", c - 1, 15)))
                    else:
                        v.wait("v", M.get(("resv", 3, 15)))
                else:
                    v.wait("v", M.get(("f1resv", c, m - 1)))
                if m >= 2:
                    v.wait_dma(mk.get(("stx4", c, m - 2)))
                elif c >= 1:
                    v.wait_dma(mk.get(("stx4", c - 1, m + 14)))
                else:
                    v.wait_dma(mk.get(("st", 3, m + 14)))
                v.tensor_sub(t1[:], xres0f[:, m, :], bbc[:])
                v.tensor_mul(t2[:], yb0[:, m, :], abc[:])
                v.edge()
                v.tensor_add(xout[:, mp, :], t1[:], t2[:])
                v.inc()
                cx.mark(("f1resv", c, m), "v")
                sy.wait("v", M.get(("f1resv", c, m)))
                mk[("stx4", c, m)] = cx.dma(
                    "sy", "st%d" % mp,
                    x4o[:, m * CORE_TOK + c * GT:m * CORE_TOK + (c + 1) * GT],
                    xout[:, mp, :])
        drain(("st0", "st1"))
    with nc.Block() as block:
        cx.run(block, body)
    return nc


def _prep_l1b_inputs(inputs, x2_cores, resB):
    wo_t = retile_w(np.ascontiguousarray(
        np.asarray(inputs["L1_attn_wo"], dtype=np.float32).T))
    W_t = retile_w(np.ascontiguousarray(
        np.asarray(inputs["L1_ffn_W"], dtype=np.float32).T))
    A = np.asarray(inputs["L1_ffn_A"], dtype=np.float32)
    A_t = retile_w(np.ascontiguousarray(A.reshape(1024, 2048).T))
    B = np.asarray(inputs["L1_ffn_B"], dtype=np.float32)
    B_flat = np.ascontiguousarray(B.transpose(0, 2, 1).reshape(1024, 2048))
    B_t = retile_w(B_flat, din=1024)
    bias_t = np.ascontiguousarray(
        np.asarray(inputs["L1_ffn_bias"], dtype=np.float32).reshape(16, 128).T)
    owner = {}
    for rr in range(4):
        for li, bb in enumerate(core_blocks(rr)):
            owner[bb] = (rr, li)
    maps = []
    for c in range(8):
        b, r = c // 4, c % 4
        k1gf = np.zeros((NH, 128, SEQ), dtype=BF)
        v1gf = np.zeros((NH, 128, SEQ), dtype=BF)
        for bb in range(64):
            rb, li = owner[bb]
            src = resB[b * 4 + rb]
            k1gf[:, :, bb * 128:(bb + 1) * 128] = src["k1"][:, :, li * 128:(li + 1) * 128]
            v1gf[:, :, bb * 128:(bb + 1) * 128] = src["v1"][:, li * 128:(li + 1) * 128, :]
        qkng = np.concatenate(
            [np.asarray(resB[b * 4 + rr]["qkn"], dtype=np.float32).reshape(1, 32)
             for rr in range(4)], axis=1)
        maps.append(dict(
            x=x2_cores[c], q1o=resB[c]["q1"], k1g=k1gf, v1g=v1gf,
            qkng=np.ascontiguousarray(qkng),
            wo_t=wo_t, W_t=W_t, A_t=A_t, B_t=B_t, bias_t=bias_t))
    return maps


def build_l0a2():
    nc = bass.Bass("TRN2", target_bir_lowering=False, debug=False)
    x = nc.dram_tensor("x", [128, 16 * CORE_TOK], F32, kind="ExternalInput").ap()
    wqkv = nc.dram_tensor("wqkv", [48, 128, 16 * 128], F32, kind="ExternalInput").ap()
    qg = nc.dram_tensor("qg", [1, 128], F32, kind="ExternalInput").ap()
    kg = nc.dram_tensor("kg", [1, 128], F32, kind="ExternalInput").ap()
    cos0 = nc.dram_tensor("cos0", [128, CORE_TOK], F32, kind="ExternalInput").ap()
    sin0 = nc.dram_tensor("sin0", [128, CORE_TOK], F32, kind="ExternalInput").ap()
    q0 = nc.dram_tensor("q0", [NH, 128, CORE_TOK], BF16, kind="ExternalOutput").ap()
    k0 = nc.dram_tensor("k0", [NH, 128, CORE_TOK], BF16, kind="ExternalOutput").ap()
    v0 = nc.dram_tensor("v0", [NH, CORE_TOK, 128], BF16, kind="ExternalOutput").ap()

    C, cload, NC_ = _consts(nc)
    PW = 2 * GT
    sb = nc.alloc_sbuf_tensor
    gqc = sb("gqc", [128, 1], F32)
    gkc = sb("gkc", [128, 1], F32)
    coss = sb("coss", [128, CORE_TOK], F32)
    sins = sb("sins", [128, CORE_TOK], F32)
    wA = sb("wA", [128, 2, 16 * 128], BF16)
    rows = sb("rows", [1, 8, GT], F32)
    mean_r, msq_r, es_r, var_r, sd_r, rstd_r, mr_r = [rows[0:1, i, :]
                                                      for i in range(7)]
    sdq_r = sb("sdq_r", [1, PW], F32)
    rsq_r = sb("rsq_r", [1, PW], F32)
    pg = sb("pg", [128, 3, PW], BF16)
    rsh = sb("rsh", [128, 2, PW], BF16)
    sqs = sb("sqs", [128, 2, PW], BF16)
    t1 = sb("t1", [128, PW], F32)
    t2 = sb("t2", [128, PW], F32)
    qout = sb("qout", [128, 2, PW], BF16)
    vout = sb("vout", [128, 2, PW], BF16)
    xb = sb("xbA", [128, 16, GT], BF16)
    xhat = sb("xhatA", [128, 16, CORE_TOK], BF16)
    sqb = sb("sqbA", [128, 16, GT], BF16)

    ps = nc.alloc_psum_tensor
    pS = ps("pS", [128, 4, GT], F32)
    pR = ps("pR", [128, 2, GT], F32)
    pC = ps("pC", [128, 2, GT], BF16)
    pVb = ps("pVb", [1, GT], F32)
    vrow = [pR[0:1, j, :] for j in range(2)]
    pRf = pR[:, :, :].rearrange("p a n -> p (a n)")
    pC8 = pC[:, :, :].rearrange("p a (b c) -> p (a b) c", c=128)

    cx = Cx(nc)
    M = cx.M

    def ln_rows(cx, key):
        s, v, t = cx.s, cx.v, cx.t
        s.wait("t", M.get((key, "s12t")))
        s.mul(mean_r, vrow[0], 1.0 / D)
        s.edge()
        s.activation(msq_r, mean_r, AF.Square)
        s.edge()
        s.mul(es_r, vrow[1], 1.0 / D)
        s.inc()
        cx.mark((key, "es"), "s")
        v.wait("s", M.get((key, "es")))
        v.tensor_sub(var_r, es_r, msq_r)
        v.inc()
        cx.mark((key, "var"), "v")
        s.wait("v", M.get((key, "var")))
        s.activation(sd_r, var_r, AF.Sqrt, bias=1e-5)
        s.inc()
        cx.mark((key, "sd"), "s")
        v.wait("s", M.get((key, "sd")))
        v.reciprocal(rstd_r, sd_r)
        v.edge()
        v.tensor_mul(mr_r, mean_r, rstd_r)
        v.inc()
        cx.mark((key, "mr"), "v")
        t.wait("v", M.get((key, "mr")))
        t.matmul(pS[:, 2, :], C["ocol"][:], rstd_r, start=True, stop=True)
        t.matmul(pS[:, 3, :], C["ocol"][:], mr_r, start=True, stop=True)
        t.inc()
        cx.mark((key, "bc"), "t")

    def body(cx):
        g, sy, t, s, v = cx.g, cx.sy, cx.t, cx.s, cx.v
        mk = {}
        for dst, srcap in C["pairs"]:
            mk["c"] = cx.dma("g", "c", dst, srcap)
        cx.dma("g", "c", gqc[:], qg.rearrange("a (p b) -> (a p) b", p=128))
        cx.dma("g", "c", gkc[:], kg.rearrange("a (p b) -> (a p) b", p=128))
        cx.dma("g", "c", coss[:], cos0)
        mk["c"] = cx.dma("g", "c", sins[:], sin0)

        for c in range(NG):
            if c >= 1:
                g.wait("v", M.get(("axh", c - 1)))
            mk[("xb", c)] = cx.dma(
                "g", "xc", xb[:, :, :],
                x[:, :].rearrange("p (a n) -> p a n", a=16)[:, :, c * GT:(c + 1) * GT])
            t.wait_dma(mk[("xb", c)])
            if c == 0:
                t.wait_dma(mk["c"])
            else:
                t.wait("s", M.get((("ai", c - 1), "es")))
            for di in range(16):
                t.matmul(vrow[0], C["o128"][:], xb[:, di, :],
                         start=(di == 0), stop=(di == 15), skip_group_check=True)
            t.inc()
            cx.mark(("as1", c), "t")
            s.wait_dma(mk[("xb", c)])
            if c >= 1:
                s.wait("t", M.get(("as2", c - 1)))
            for di in range(16):
                s.activation(sqb[:, di, :], xb[:, di, :], AF.Square)
            s.inc()
            cx.mark(("asq", c), "s")
            t.wait("s", M.get(("asq", c)))
            for di in range(16):
                t.matmul(vrow[1], C["o128"][:], sqb[:, di, :],
                         start=(di == 0), stop=(di == 15), skip_group_check=True)
            t.inc()
            cx.mark(("as2", c), "t")
            if cx.mode == "plan":
                M[(("ai", c), "s12t")] = M[("as2", c)]
            if c >= 1:
                cx.t.wait("v", M.get(("axh", c - 1)))
            ln_rows(cx, ("ai", c))
            for di in range(16):
                if di == 0:
                    v.wait("t", M.get((("ai", c), "bc")))
                v.tensor_mul(xhat[:, di, c * GT:(c + 1) * GT], xb[:, di, :],
                             pS[:, 2, :])
            v.edge()
            for di in range(16):
                v.tensor_sub(xhat[:, di, c * GT:(c + 1) * GT],
                             xhat[:, di, c * GT:(c + 1) * GT], pS[:, 3, :])
            v.inc()
            cx.mark(("axh", c), "v")

        NP = 96

        def emit_mm(i):
            m, tp_ = i // 2, i % 2
            par = i % 2
            p3 = i % 3
            isv = m >= 32
            if tp_ == 0:
                if m >= 2:
                    g.wait("t", M.get(("amm", (m - 2) * 2 + 1)))
                mk[("w", m)] = cx.dma("g", "w%d" % (m % 2), wA[:, m % 2, :],
                                      wqkv[m, :, :])
            t.wait_dma(mk[("w", m)])
            if i < 2:
                t.wait("v", M.get(("axh", 3)))
            if i >= 2:
                t.wait("s", M.get(("apc", i - 2)))
            for half in range(2):
                tc = tp_ * 2 + half
                for di in range(16):
                    t.matmul(pS[:, 2 * par + half, :],
                             wA[:, m % 2, di * 128:(di + 1) * 128],
                             xhat[:, di, tc * GT:(tc + 1) * GT],
                             start=(di == 0), stop=(di == 15))
            t.inc()
            cx.mark(("amm", i), "t")
            pmm = pS[:, 2 * par:2 * par + 2, :].rearrange("p a n -> p (a n)")
            s.wait("t", M.get(("amm", i)))
            if not isv:
                if i >= 2:
                    s.wait("t", M.get(("avar", i - 2)))
                s.activation(sqs[:, par, :], pmm, AF.Square)
                s.edge()
            if i >= 3:
                s.wait("v", M.get(("aout", i - 3)))
            if isv:
                s.activation(pg[:, p3, :], pmm, AF.Copy)
            else:
                gam = gqc if m < 16 else gkc
                s.activation(pg[:, p3, :], pmm, AF.Copy, scale=gam[:, 0:1])
            s.inc()
            cx.mark(("apc", i), "s")
            if not isv:
                sy.wait("s", M.get(("apc", i)))
                if i >= 2:
                    sy.wait("v", M.get(("aout", i - 2)))
                cx.dma("sy", "rs%d" % par, rsh[0:64, par, :], pg[64:128, p3, :])
                mk[("rs", i)] = cx.dma("sy", "rs%d" % par, rsh[64:128, par, :],
                                       pg[0:64, p3, :])

        def stage1(i):
            m, tp_ = i // 2, i % 2
            par = i % 2
            p3 = i % 3
            if m < 32:
                if i >= 1:
                    t.wait("s", M.get(("asd", i - 1)))
                else:
                    t.wait("s", M.get(("apc", i)))
                t.matmul(pVb[0:1, :], C["o128"][:], sqs[:, par, 0:GT],
                         start=True, stop=True)
                t.inc()
                cx.mark(("avar0", i), "t")
                s.wait("t", M.get(("avar0", i)))
                if i >= 1:
                    s.wait("v", M.get(("arc", i - 1)))
                s.activation(sdq_r[0:1, 0:GT], pVb[0:1, :], AF.Sqrt,
                             scale=1.0 / HD, bias=1e-6)
                s.inc()
                cx.mark(("asd0", i), "s")
                t.wait("s", M.get(("asd0", i)))
                t.matmul(pVb[0:1, :], C["o128"][:], sqs[:, par, GT:PW],
                         start=True, stop=True)
                t.inc()
                cx.mark(("avar", i), "t")
                s.wait("t", M.get(("avar", i)))
                s.activation(sdq_r[0:1, GT:PW], pVb[0:1, :], AF.Sqrt,
                             scale=1.0 / HD, bias=1e-6)
                s.inc()
                cx.mark(("asd", i), "s")
                v.wait("s", M.get(("asd0", i)))
                v.reciprocal(rsq_r[0:1, 0:GT], sdq_r[0:1, 0:GT])
                v.edge()
                v.wait("s", M.get(("asd", i)))
                v.reciprocal(rsq_r[0:1, GT:PW], sdq_r[0:1, GT:PW])
                v.inc()
                cx.mark(("arc", i), "v")
            else:
                t.wait("s", M.get(("apc", i)))
                if i >= 1:
                    t.wait("v", M.get(("aout", i - 1)))
                for blk in range(8):
                    t.transpose(pC8[:, blk, :], pg[:, p3, blk * 128:(blk + 1) * 128],
                                C["id128"][:])
                t.inc()
                cx.mark(("arot", i), "t")

        def stage2(i):
            m, tp_ = i // 2, i % 2
            par = i % 2
            p3 = i % 3
            if m < 32:
                t.wait("v", M.get(("arc", i)))
                if i >= 1:
                    t.wait("v", M.get(("aout", i - 1)))
                for j in range(2):
                    t.matmul(pR[:, j, :], C["ocol"][:],
                             rsq_r[0:1, j * GT:(j + 1) * GT], start=True, stop=True)
                t.inc()
                cx.mark(("aob", i), "t")
                v.tensor_mul(t1[:], pg[:, p3, :], coss[:, tp_ * PW:tp_ * PW + PW])
                v.edge()
                v.wait_dma(mk.get(("rs", i)))
                v.tensor_mul(t2[:], rsh[:, par, :], sins[:, tp_ * PW:tp_ * PW + PW])
                v.edge()
                v.tensor_add(t1[:], t1[:], t2[:])
                v.edge()
                v.wait("t", M.get(("aob", i)))
                if i >= 2:
                    v.wait_dma(mk.get(("st", i - 2)))
                v.tensor_mul(qout[:, par, :], t1[:], pRf)
                v.inc()
                cx.mark(("aout", i), "v")
                sy.wait("v", M.get(("aout", i)))
                dst = (q0 if m < 16 else k0)[m % 16, :, tp_ * PW:(tp_ + 1) * PW]
                mk[("st", i)] = cx.dma("sy", "st%d" % par, dst, qout[:, par, :])
            else:
                v.wait("t", M.get(("arot", i)))
                if i >= 2:
                    v.wait_dma(mk.get(("st", i - 2)))
                v.tensor_copy(vout[:, par, :].rearrange("p (b n) -> p b n", b=8),
                              pC8[:, :, :])
                v.inc()
                cx.mark(("aout", i), "v")
                sy.wait("v", M.get(("aout", i)))
                mk[("st", i)] = cx.dma(
                    "sy", "st%d" % par,
                    v0[m - 32, tp_ * PW:(tp_ + 1) * PW, :]
                    .rearrange("(b p) d -> p b d", p=128),
                    vout[:, par, :].rearrange("p (b d) -> p b d", b=8))

        for idx in range(NP + 2):
            if idx < NP:
                emit_mm(idx)
            if idx >= 2:
                stage2(idx - 2)
            if idx >= 1 and idx - 1 < NP:
                stage1(idx - 1)
        for sl in ("st0", "st1"):
            if cx.mode != "plan" and sl in cx.dmarks:
                sy.wait_dma((cx._dsems[sl], cx.dmarks[sl][-1]))

    with nc.Block() as block:
        cx.run(block, body)
    return nc
